# version 13
# baseline (speedup 1.0000x reference)
import numpy as np
from contextlib import ExitStack

import concourse.bass as bass
import concourse.mybir as mybir
from concourse.bass_utils import run_bass_kernel_spmd
from concourse.aot_env import allow_ici_all_to_all

F32 = mybir.dt.float32
I32 = mybir.dt.int32
AF = mybir.ActivationFunctionType
ALU = mybir.AluOpType
AX = mybir.AxisListType

NCORE = 8
B, S, D = 2, 8192, 1024
TPC = 2048
NG, GT = 4, 512
NCH = 8
DFF, RANK, IFF = 4096, 256, 3072
NIC = IFF // 128
LN_EPS = 1e-5
ATT_EPS2 = 2e-6
HEADS, HD = 16, 64

ENGINES = ("sync", "tensor", "scalar", "vector", "gpsimd")


def K_SKIP():
    return globals().get("_ATT_SKIP", set())
SEM_NAMES = ["tS", "sS", "vS", "gS", "qL", "qA", "qG", "qV", "cc", "bar"]


class Plan:
    def __init__(self):
        self.ops = {e: [] for e in ENGINES}
        self.ctr = {}

    def tok(self, sem):
        return (sem, self.ctr.get(sem, 0))

    def op(self, eng, meth, *args, after=(), inc=None, incn=1, **kw):
        waits = {}
        for t in after:
            if t is None:
                continue
            s, v = t
            if v > 0:
                waits[s] = max(waits.get(s, 0), v)
        if inc is not None:
            self.ctr[inc] = self.ctr.get(inc, 0) + incn
        self.ops[eng].append((meth, args, kw, sorted(waits.items()), inc, incn))
        return (inc, self.ctr[inc]) if inc is not None else None

    def dma(self, eng, out, in_, sem, after=()):
        return self.op(eng, "dma_start", after=after, inc=sem, incn=16, out=out, in_=in_)

    def barrier(self):
        self.ctr["bar"] = self.ctr.get("bar", 0) + len(ENGINES)
        tgt = self.ctr["bar"]
        for e in ENGINES:
            self.ops[e].append(("sem_inc", ("bar", 1), {}, [], None, None))
            self.ops[e].append(("wait_ge", ("bar", tgt), {}, [], None, None))
        return ("bar", tgt)

    def replay(self, eng, h, sems):
        for meth, args, kw, waits, inc, incn in self.ops[eng]:
            for s, v in waits:
                h.wait_ge(sems[s], v)
            if meth == "sem_inc":
                h.sem_inc(sems[args[0]], args[1])
                continue
            if meth == "wait_ge":
                h.wait_ge(sems[args[0]], args[1])
                continue
            if meth == "collective":
                kind, op_, ins_, outs_ = args
                with allow_ici_all_to_all():
                    inst = h.collective_compute(
                        kind, op_, replica_groups=[list(range(NCORE))],
                        ins=ins_, outs=outs_)
            else:
                inst = getattr(h, meth)(*args, **kw)
            if inc is not None:
                inst.then_inc(sems[inc], incn)


class Arena:

    def __init__(self, ap, total):
        self.ap = ap
        self.total = total
        self.off = 0

    def alloc(self, *shape):
        n = int(np.prod(shape))
        assert self.off + n <= self.total, (self.off, n, self.total)
        v = self.ap[:, self.off:self.off + n]
        self.off += n
        if len(shape) > 1:
            names = " ".join(f"a{i}" for i in range(len(shape)))
            v = v.rearrange(f"p ({names}) -> p {names}",
                            **{f"a{i}": s for i, s in enumerate(shape)})
        return v

    def reset(self):
        self.off = 0


def build(upto="full"):
    nc = bass.Bass("TRN2", target_bir_lowering=False, debug=False,
                   detect_race_conditions=False)

    def din(name, shape):
        return nc.dram_tensor(name, list(shape), F32, kind="ExternalInput").ap()

    x0_d = din("x0", (TPC, D))
    wg_d = [din(n, (128, NCH, NCH, 128)) for n in ("Wi_b", "Wf_b", "Wog_b", "Ws_b")]
    bcols_d = din("bcols", (128, 4 * NCH))
    gb_d = din("gb_bc", (128, 2 * D))
    u1_d = din("U1", (D, RANK))
    v1_d = din("V1_b", (128, 32, 2, 128))
    b1c_d = din("b1cols", (128, 32))
    u2_d = din("U2_b", (128, 32, 2, 128))
    v2_d = din("V2", (RANK, D))
    b2bc_d = din("b2_bc", (128, D))
    wqT_d = din("WqT_b", (128, NCH, NCH, 128))
    wkT_d = din("WkT_b", (128, NCH, NCH, 128))
    wvT_d = din("WvT", (D, D))
    woT_d = din("WoT_b", (128, NCH, D))
    masks_d = din("masks", (128, 4 * 512))
    onehot_d = din("onehot", (128, NCORE))
    ident_d = din("ident_in", (128, 128))
    wup_d = din("wup", (IFF, D))
    wgt_d = din("wgt", (IFF, D))
    wdn_d = din("wdn", (D, IFF))

    out_d = nc.dram_tensor("out", [TPC, D], F32, kind="ExternalOutput").ap()

    def dint(name, shape):
        return nc.dram_tensor(name, list(shape), F32).ap()

    zl_sp = dint("zl_sp", (D, TPC))
    c_sp = dint("c_sp", (D, TPC))
    carry_bin = dint("carry_bin", (128, 16))
    carry_out = dint("carry_out", (NCORE * 128, 16))
    qk_bin = dint("qk_bin", (NCORE, 2, 128, TPC))
    qk_out = dint("qk_out", (NCORE, 2, 128, TPC))
    v_bin = dint("v_bin", (NCORE, TPC, 128))
    v_out = dint("v_out", (NCORE, TPC, 128))
    o_bin = dint("o_bin", (NCORE, TPC, 128))
    o_out = dint("o_out", (NCORE, TPC, 128))
    wupT_d = dint("wupT", (NIC, 128, NCH, 128))
    wgtT_d = dint("wgtT", (NIC, 128, NCH, 128))
    wdnT_d = dint("wdnT", (NIC, 128, D))
    up_sp = dint("up_sp", (NIC, 128, TPC))
    gt_sp = dint("gt_sp", (NIC, 128, TPC))
    upq_sp = dint("upq_sp", (NIC, 128, TPC))
    g2_sp = dint("g2_sp", (NIC, 128, TPC))
    o_sp = dint("o_sp", (NIC, 128, TPC))
    ar_bin = [dint(f"ar{i}_bin", (1, 4)) for i in range(4)]
    ar_out = [dint(f"ar{i}_out", (1, 4)) for i in range(4)]

    zl_v = zl_sp.rearrange("(c p) t -> p c t", p=128)
    c_v = c_sp.rearrange("(c p) t -> p c t", p=128)
    carry_v = carry_out.rearrange("(c p) k -> p c k", p=128)

    p = Plan()

    with ExitStack() as ctx:
        sems = {s: ctx.enter_context(nc.semaphore(s)) for s in SEM_NAMES}
        pb = [ctx.enter_context(nc.psum_tensor(f"pb{i}", [128, 512], F32))
              for i in range(8)]
        slab_t = ctx.enter_context(nc.sbuf_tensor("slab", [128, 16 * D], F32))
        slab = slab_t[:, :].rearrange("p (a d) -> p a d", a=16)
        AR_COLS = 32000
        arena_t = ctx.enter_context(nc.sbuf_tensor("arena", [128, AR_COLS], F32))
        A = Arena(arena_t[:, :], AR_COLS)
        iar_t = ctx.enter_context(nc.sbuf_tensor("iar", [128, 3072], I32))
        ident = ctx.enter_context(nc.sbuf_tensor("ident_s", [128, 128], F32))
        bcols = ctx.enter_context(nc.sbuf_tensor("bcols_s", [128, 4, NCH], F32))
        b1cols = ctx.enter_context(nc.sbuf_tensor("b1cols_s", [128, 32], F32))
        onehot = ctx.enter_context(nc.sbuf_tensor("onehot_s", [128, NCORE], F32))
        hincol = ctx.enter_context(nc.sbuf_tensor("hincol", [128, NCH], F32))
        ccell = ctx.enter_context(nc.sbuf_tensor("ccell", [128, 8], F32))
        onescol = ctx.enter_context(nc.sbuf_tensor("onescol", [128, 1], F32))
        onesrow = ctx.enter_context(nc.sbuf_tensor("onesrow", [1, 128], F32))
        t128 = ctx.enter_context(nc.sbuf_tensor("t128", [128, 128], F32))
        srow = ctx.enter_context(nc.sbuf_tensor("srow", [1, 16], F32))
        scol = ctx.enter_context(nc.sbuf_tensor("scol", [128, 16], F32))

        epscol = ccell[:, 0:1]

        bankfree = {i: None for i in range(8)}

        for dst, src in ((ident[:, :], ident_d[:, :]),
                         (bcols[:, :, :].rearrange("p a c -> p (a c)"), bcols_d[:, :]),
                         (b1cols[:, :], b1c_d[:, :]),
                         (onehot[:, :], onehot_d[:, :])):
            p.dma("sync", dst, src, "qL")
        tok_consts = p.tok("qL")
        p.op("vector", "memset", ccell[:, :], 0.0)
        p.op("vector", "memset", epscol, LN_EPS)
        p.op("vector", "memset", ccell[:, 1:2], 7.0)
        p.op("vector", "reciprocal", ccell[:, 2:3], ccell[:, 1:2])
        p.op("vector", "memset", ccell[:, 3:4], 32767.0)
        p.op("vector", "reciprocal", ccell[:, 4:5], ccell[:, 3:4])
        p.op("vector", "memset", onescol[:, :], 1.0)
        p.op("vector", "memset", onesrow[:, :], 1.0)
        p.op("vector", "memset", t128[:, :], 0.0, inc="vS")
        tok_vconsts = p.tok("vS")

        def dequant(src_d, nrc, ccols, store_fn, tag):
            nb = ccols // 32
            ntc = ccols // 128
            wch = A.alloc(ccols)
            bm = A.alloc(nb)
            sS_ = A.alloc(nb)
            rr = A.alloc(nb)
            stg = A.alloc(2, 128)
            tok_w_free = p.tok("tS")
            tok_stg_free = [p.tok("qG"), p.tok("qG")]
            for rc in range(nrc):
                ld = p.dma("sync", wch,
                           src_d[rc * 128:(rc + 1) * 128, :], "qL",
                           after=[tok_w_free])
                w3 = wch.rearrange("p (nb e) -> p nb e", e=32)
                t1 = p.op("vector", "tensor_reduce",
                          bm.rearrange("p (a o) -> p a o", o=1), w3,
                          axis=AX.X, op=ALU.max, apply_absolute_value=True,
                          after=[ld], inc="vS")
                t2 = p.op("vector", "tensor_scalar", sS_, bm, ccell[:, 2:3],
                          1e-12, op0=ALU.mult, op1=ALU.max, after=[t1],
                          inc="vS")
                t3 = p.op("vector", "reciprocal", rr, sS_, after=[t2], inc="vS")
                rbc = rr.rearrange("p (a o) -> p a o", o=1).broadcast_to((128, nb, 32))
                sbc = sS_.rearrange("p (a o) -> p a o", o=1).broadcast_to((128, nb, 32))
                t4 = p.op("vector", "tensor_tensor", w3, w3, rbc, op=ALU.mult,
                          after=[t3], inc="vS")
                t5 = p.op("scalar", "activation", iar_t[:, 0:ccols], wch, AF.Copy,
                          after=[t4], inc="sS")
                t5b = p.op("scalar", "activation", wch, iar_t[:, 0:ccols],
                           AF.Copy, after=[t5], inc="sS")
                t6 = p.op("vector", "tensor_tensor", w3, w3, sbc,
                          op=ALU.mult, after=[t5b], inc="vS")
                last_ev = None
                for tc in range(ntc):
                    bk = tc % 2
                    tt_ = p.op("tensor", "transpose", pb[bk][:, 0:128],
                               wch[:, tc * 128:(tc + 1) * 128], ident[:, :],
                               after=[t6, bankfree[bk], tok_consts], inc="tS")
                    ev = p.op("vector", "tensor_copy", stg[:, bk, :], pb[bk][:, 0:128],
                              after=[tt_, tok_stg_free[bk]], inc="vS")
                    bankfree[bk] = ev
                    st = p.dma("gpsimd", store_fn(rc, tc), stg[:, bk, :], "qG",
                               after=[ev])
                    tok_stg_free[bk] = st
                    last_ev = tt_
                tok_w_free = last_ev
            A.reset()

        if upto in ("full", "x3up", "svdump2", "svdump3", "odump"):
            dequant(wup_d, NIC, D,
                    lambda rc, tc: wupT_d[rc, :, tc, :], "up")
            dequant(wgt_d, NIC, D,
                    lambda rc, tc: wgtT_d[rc, :, tc, :], "gt")
            dequant(wdn_d, NCH, IFF,
                    lambda rc, tc: wdnT_d[tc, :, rc * 128:(rc + 1) * 128], "dn")
        tok_deq = p.tok("qG")
        A.reset()

        p.barrier()
        for i in range(8):
            bankfree[i] = None

        xlnT = A.alloc(NCH, GT)
        fb = A.alloc(NCH, GT)
        ab = A.alloc(NCH, GT)
        ob = A.alloc(NCH, GT)
        Pb = A.alloc(NCH, GT)
        hb = A.alloc(GT)
        w4a = A.alloc(2, NCH, 128)
        w4b = A.alloc(2, NCH, 128)
        x0st = A.alloc(2, D)
        bn = A.alloc(16)
        ccol = A.alloc(NCH, 2)
        zbuf = A.alloc(GT)
        tmp_i = A.alloc(GT)
        tmp_s = A.alloc(GT)
        carr = A.alloc(NCORE, 16)
        hina = A.alloc(NCH, NCORE)

        p.op("vector", "memset", zbuf, 0.0, inc="vS")
        tok_zbuf = p.tok("vS")

        tok_x0_free = [None, None]
        tok_xlnT_spill = None
        tok_P_spill = None
        tok_w4a_free = None
        tok_w4b_free = None
        tok_tmp_free = None
        tok_pass2 = None

        def emit_ln(src, dst, after=()):
            b1 = p.op("vector", "bn_stats", bn[:, 0:6], src[:, 0:512],
                      after=list(after), inc="vS")
            b2 = p.op("vector", "bn_stats", bn[:, 6:12], src[:, 512:1024], inc="vS")
            ag = p.op("vector", "bn_aggr", bn[:, 12:14], bn[:, 0:12],
                      after=[b1, b2], inc="vS")
            sq = p.op("scalar", "activation", bn[:, 14:15], bn[:, 13:14], AF.Sqrt,
                      bias=epscol, after=[ag, tok_vconsts], inc="sS")
            rs = p.op("vector", "reciprocal", bn[:, 15:16], bn[:, 14:15],
                      after=[sq], inc="vS")
            return p.op("vector", "tensor_scalar", dst, src, bn[:, 12:13],
                        bn[:, 15:16], op0=ALU.subtract, op1=ALU.mult,
                        after=[rs], inc="vS")

        for g in range(NG):
            ev = None
            for c in range(4):
                par = c % 2
                ld = p.dma("sync", x0st[:, par, :],
                           x0_d[(g * 4 + c) * 128:(g * 4 + c + 1) * 128, :],
                           "qL", after=[tok_x0_free[par]])
                ln = emit_ln(x0st[:, par, :], x0st[:, par, :], after=[ld])
                for dc in range(NCH):
                    bk = 6 + dc % 2
                    tt_ = p.op("tensor", "transpose", pb[bk][:, 0:128],
                               x0st[:, par, dc * 128:(dc + 1) * 128], ident[:, :],
                               after=[ln, bankfree[bk], tok_consts], inc="tS")
                    ev = p.op("vector", "tensor_copy",
                              xlnT[:, dc, c * 128:(c + 1) * 128], pb[bk][:, 0:128],
                              after=[tt_, tok_xlnT_spill], inc="vS")
                    bankfree[bk] = ev
                tok_x0_free[par] = tt_
            tok_xlnT_ready = ev

            for e8 in range(NCH):
                tif = [p.dma("sync", w4a[:, k, :, :], wg_d[k][:, e8, :, :], "qL",
                             after=[tok_w4a_free, tok_consts]) for k in range(2)]
                mm_tok = [None] * 4
                for gi in range(2):
                    for dc in range(NCH):
                        aft = [tif[1], tok_xlnT_ready, bankfree[gi]] if dc == 0 else []
                        mm_tok[gi] = p.op("tensor", "matmul", pb[gi][:, :],
                                          w4a[:, gi, dc, :], xlnT[:, dc, :],
                                          start=(dc == 0), stop=(dc == NCH - 1),
                                          after=aft,
                                          inc="tS" if dc == NCH - 1 else None)
                tok_w4a_free = mm_tok[1]
                tos = [p.dma("sync", w4b[:, k, :, :], wg_d[2 + k][:, e8, :, :], "qL",
                             after=[tok_w4b_free]) for k in range(2)]
                for gi in range(2, 4):
                    for dc in range(NCH):
                        aft = [tos[1], tok_xlnT_ready, bankfree[gi]] if dc == 0 else []
                        mm_tok[gi] = p.op("tensor", "matmul", pb[gi][:, :],
                                          w4b[:, gi - 2, dc, :], xlnT[:, dc, :],
                                          start=(dc == 0), stop=(dc == NCH - 1),
                                          after=aft,
                                          inc="tS" if dc == NCH - 1 else None)
                tok_w4b_free = mm_tok[3]
                ei = p.op("scalar", "activation", tmp_i, pb[0][:, :], AF.Sigmoid,
                          bias=bcols[:, 0, e8:e8 + 1],
                          after=[mm_tok[0], tok_tmp_free, tok_pass2], inc="sS")
                ef = p.op("scalar", "activation", fb[:, e8, :], pb[1][:, :], AF.Sigmoid,
                          bias=bcols[:, 1, e8:e8 + 1],
                          after=[mm_tok[1], tok_pass2], inc="sS")
                eo = p.op("scalar", "activation", ob[:, e8, :], pb[2][:, :], AF.Sigmoid,
                          bias=bcols[:, 2, e8:e8 + 1],
                          after=[mm_tok[2], tok_pass2], inc="sS")
                es = p.op("scalar", "activation", tmp_s, pb[3][:, :], AF.Tanh,
                          bias=bcols[:, 3, e8:e8 + 1],
                          after=[mm_tok[3], tok_tmp_free, tok_pass2], inc="sS")
                bankfree[0], bankfree[1], bankfree[2], bankfree[3] = ei, ef, eo, es
                ea = p.op("vector", "tensor_tensor", ab[:, e8, :], tmp_i, tmp_s,
                          op=ALU.mult, after=[ei, es, tok_P_spill], inc="vS")
                tok_tmp_free = ea

            tok_allmm = p.tok("tS")
            tok_acts = p.tok("sS")
            for e8 in range(NCH):
                init_h = 0.0 if g == 0 else ccol[:, e8, 0:1]
                sh = p.op("vector", "tensor_tensor_scan", hb, fb[:, e8, :],
                          ab[:, e8, :], init_h, op0=ALU.mult, op1=ALU.add,
                          after=[tok_acts], inc="vS")
                p.op("vector", "tensor_copy", ccol[:, e8, 0:1], hb[:, GT - 1:GT],
                     after=[sh], inc="vS")
                ty = p.op("vector", "tensor_tensor", hb, ob[:, e8, :], hb,
                          op=ALU.mult, inc="vS")
                p.op("vector", "tensor_tensor", xlnT[:, e8, :], xlnT[:, e8, :], hb,
                     op=ALU.add, after=[ty, tok_allmm], inc="vS")
                init_p = 1.0 if g == 0 else ccol[:, e8, 1:2]
                sp_ = p.op("vector", "tensor_tensor_scan", Pb[:, e8, :], fb[:, e8, :],
                           zbuf, init_p, op0=ALU.mult, op1=ALU.add,
                           after=[tok_zbuf, tok_P_spill], inc="vS")
                p.op("vector", "tensor_copy", ccol[:, e8, 1:2],
                     Pb[:, e8, GT - 1:GT], after=[sp_], inc="vS")
                p.op("vector", "tensor_tensor", Pb[:, e8, :], ob[:, e8, :],
                     Pb[:, e8, :], op=ALU.mult, inc="vS")
            tok_pass2 = p.tok("vS")
            tok_xlnT_spill = p.dma("gpsimd", zl_v[:, :, g * GT:(g + 1) * GT],
                                   xlnT[:, :, :], "qG", after=[tok_pass2])
            tok_P_spill = p.dma("gpsimd", c_v[:, :, g * GT:(g + 1) * GT],
                                Pb[:, :, :], "qG", after=[tok_pass2])

        tcb = p.dma("gpsimd", carry_bin[:, :],
                    ccol.rearrange("p c a -> p (c a)"), "qG",
                    after=[tok_pass2])
        cc1 = p.op("gpsimd", "collective", "AllGather", ALU.bypass,
                   [carry_bin], [carry_out], after=[tcb], inc="cc")
        tld = p.dma("sync", carr, carry_v, "qL", after=[cc1])
        zc = p.op("vector", "memset", hina[:, :, :], 0.0, inc="vS")
        prev = zc
        for e8 in range(NCH):
            for cb in (0, 4):
                for c in range(cb + 1, cb + 4):
                    prev = p.op("vector", "scalar_tensor_tensor",
                                hina[:, e8, c:c + 1], hina[:, e8, c - 1:c],
                                carr[:, c - 1, 2 * e8 + 1:2 * e8 + 2],
                                carr[:, c - 1, 2 * e8:2 * e8 + 1],
                                op0=ALU.mult, op1=ALU.add,
                                after=[tld, prev], inc="vS")
        for e8 in range(NCH):
            tm = p.op("vector", "tensor_tensor", hina[:, e8, :], hina[:, e8, :],
                      onehot[:, :], op=ALU.mult, after=[prev, tok_consts], inc="vS")
            p.op("vector", "tensor_reduce", hincol[:, e8:e8 + 1], hina[:, e8, :],
                 axis=AX.X, op=ALU.add, after=[tm], inc="vS")

        A.reset()
        p.barrier()
        for i in range(8):
            bankfree[i] = None

        if upto == "zloc":
            td = p.dma("gpsimd", out_d.rearrange("a b -> (a b)"),
                       zl_sp.rearrange("a b -> (a b)"), "qG",
                       after=[p.tok("qG"), p.tok("vS")])
            p.op("gpsimd", "wait_ge", "qG", p.ctr["qG"])
            emit_replay(nc, ctx, p, sems)
            return nc

        zsl = A.alloc(NCH, GT)
        csl = A.alloc(NCH, GT)
        ztm = A.alloc(D)
        x0st = A.alloc(2, D)
        x1lT = A.alloc(NCH, GT)
        v1t = A.alloc(NCH, 2, 128)
        u2t = A.alloc(NCH, 2, 128)
        ptq = A.alloc(4, GT)
        htq = A.alloc(2, GT)
        tmp5 = A.alloc(2, GT)
        gbuf = A.alloc(2 * D)
        b2buf = A.alloc(D)
        u1s = A.alloc(NCH, RANK)
        v2s = A.alloc(2, D)
        bn = A.alloc(16)

        lg = p.dma("sync", gbuf, gb_d[:, :], "qL")
        lb2 = p.dma("sync", b2buf, b2bc_d[:, :], "qL")
        lu1 = p.dma("sync", u1s, u1_d.rearrange("(c p) r -> p c r", p=128), "qL")
        lv2 = p.dma("sync", v2s, v2_d.rearrange("(r p) d -> p r d", p=128), "qL")
        tok_bconsts = p.tok("qL")

        tok_zsl_free = None
        tok_csl_free = None
        tok_x0_free = [None, None]
        tok_v1_free = None
        tok_u2_free = None
        tok_htq_free = [None, None]
        tok_tmp5_free = [None, None]
        tok_ptq_free = None

        do_ffn = upto != "x1"

        for g in range(NG):
            lz = p.dma("scalar", zsl[:, :, :], zl_v[:, :, g * GT:(g + 1) * GT],
                       "qA", after=[tok_zsl_free,
                                    tok_xlnT_spill if g == 0 else None])
            lc = p.dma("scalar", csl[:, :, :], c_v[:, :, g * GT:(g + 1) * GT],
                       "qA", after=[tok_csl_free,
                                    tok_P_spill if g == 0 else None])
            zctoks = []
            for e8 in range(NCH):
                zctoks.append(p.op(
                    "vector", "scalar_tensor_tensor", zsl[:, e8, :], csl[:, e8, :],
                    hincol[:, e8:e8 + 1], zsl[:, e8, :], op0=ALU.mult, op1=ALU.add,
                    after=[lz, lc] if e8 == 0 else [], inc="vS"))
            tok_csl_free = zctoks[-1]
            for c in range(4):
                gc_ = g * 4 + c
                par = c % 2
                evs = []
                for dc in range(NCH):
                    bk = 6 + dc % 2
                    tt_ = p.op("tensor", "transpose", pb[bk][:, 0:128],
                               zsl[:, dc, c * 128:(c + 1) * 128], ident[:, :],
                               after=[zctoks[dc], bankfree[bk]], inc="tS")
                    ev = p.op("vector", "tensor_copy", ztm[:, dc * 128:(dc + 1) * 128],
                              pb[bk][:, 0:128], after=[tt_], inc="vS")
                    bankfree[bk] = ev
                    evs.append(ev)
                if c == 3:
                    tok_zsl_free = tt_
                ln2 = emit_ln(ztm, ztm, after=[evs[-1]])
                tgm = p.op("vector", "tensor_tensor", ztm, ztm, gbuf[:, 0:D],
                           op=ALU.mult, after=[ln2, tok_bconsts], inc="vS")
                tbt = p.op("vector", "tensor_tensor", ztm, ztm, gbuf[:, D:2 * D],
                           op=ALU.add, after=[tgm], inc="vS")
                lx0 = p.dma("sync", x0st[:, par, :],
                            x0_d[gc_ * 128:(gc_ + 1) * 128, :], "qL",
                            after=[tok_x0_free[par]])
                tx1 = p.op("vector", "tensor_tensor", slab[:, gc_, :], ztm,
                           x0st[:, par, :], op=ALU.add, after=[tbt, lx0], inc="vS")
                tok_x0_free[par] = tx1
                if not do_ffn:
                    continue
                ln3 = emit_ln(slab[:, gc_, :], ztm, after=[tx1])
                for dc in range(NCH):
                    bk = 6 + dc % 2
                    tt_ = p.op("tensor", "transpose", pb[bk][:, 0:128],
                               ztm[:, dc * 128:(dc + 1) * 128], ident[:, :],
                               after=[ln3, bankfree[bk]], inc="tS")
                    ev = p.op("vector", "tensor_copy",
                              x1lT[:, dc, c * 128:(c + 1) * 128], pb[bk][:, 0:128],
                              after=[tt_], inc="vS")
                    bankfree[bk] = ev
            if not do_ffn:
                continue
            tok_x1lT = p.tok("vS")
            evP = []
            for rc in range(2):
                mm = None
                for dc in range(NCH):
                    aft = [tok_x1lT, tok_bconsts, bankfree[rc]] if dc == 0 else []
                    mm = p.op("tensor", "matmul", pb[rc][:, :],
                              u1s[:, dc, rc * 128:(rc + 1) * 128], x1lT[:, dc, :],
                              start=(dc == 0), stop=(dc == NCH - 1), after=aft,
                              inc="tS" if dc == NCH - 1 else None)
                evP.append(p.op("vector", "tensor_copy", ptq[:, rc, :], pb[rc][:, :],
                                after=[mm, tok_ptq_free], inc="vS"))
                bankfree[rc] = evP[-1]
            for fq in range(4):
                lv1 = p.dma("sync", v1t[:, :, :, :],
                            v1_d[:, fq * 8:(fq + 1) * 8, :, :], "qL",
                            after=[tok_v1_free])
                lu2 = p.dma("sync", u2t[:, :, :, :],
                            u2_d[:, fq * 8:(fq + 1) * 8, :, :], "qL",
                            after=[tok_u2_free])
                for fc in range(NCH):
                    fg = fq * 8 + fc
                    bkH = 2 + fc % 2
                    mmH = None
                    for rc in range(2):
                        aft = [lv1, evP[0], evP[1], bankfree[bkH]] if rc == 0 else []
                        mmH = p.op("tensor", "matmul", pb[bkH][:, :],
                                   v1t[:, fc, rc, :], ptq[:, rc, :],
                                   start=(rc == 0), stop=(rc == 1), after=aft,
                                   inc="tS" if rc == 1 else None)
                    eh = p.op("scalar", "activation", htq[:, fc % 2, :], pb[bkH][:, :],
                              AF.Gelu_apprx_tanh, bias=b1cols[:, fg:fg + 1],
                              after=[mmH, tok_htq_free[fc % 2]], inc="sS")
                    bankfree[bkH] = eh
                    mmQ = None
                    for rc in range(2):
                        aft = [eh, lu2]
                        if fg == 0:
                            aft += [bankfree[4 + rc]]
                        mmQ = p.op("tensor", "matmul", pb[4 + rc][:, :],
                                   u2t[:, fc, rc, :], htq[:, fc % 2, :],
                                   start=(fg == 0), stop=(fg == 31), after=aft,
                                   inc="tS" if rc == 1 else None)
                    tok_htq_free[fc % 2] = mmQ
                tok_v1_free = mmH
                tok_u2_free = mmQ
            tq = p.tok("tS")
            evQ = [p.op("vector", "tensor_copy", ptq[:, 2 + rc, :], pb[4 + rc][:, :],
                        after=[tq], inc="vS") for rc in range(2)]
            bankfree[4], bankfree[5] = evQ[0], evQ[1]
            tok_ptq_free = None
            for tc in range(4):
                gc_ = g * 4 + tc
                for dh in range(2):
                    bk = [0, 1, 6, 7][(tc * 2 + dh) % 4]
                    mm4 = None
                    for rc in range(2):
                        aft = [evQ[0], evQ[1], bankfree[bk], tok_bconsts] \
                            if rc == 0 else []
                        mm4 = p.op("tensor", "matmul", pb[bk][:, :],
                                   ptq[:, 2 + rc, tc * 128:(tc + 1) * 128],
                                   v2s[:, rc, dh * 512:(dh + 1) * 512],
                                   start=(rc == 0), stop=(rc == 1), after=aft,
                                   inc="tS" if rc == 1 else None)
                    tpar = (tc * 2 + dh) % 2
                    t1 = p.op("vector", "tensor_tensor", tmp5[:, tpar, :],
                              pb[bk][:, :], b2buf[:, dh * 512:(dh + 1) * 512],
                              op=ALU.add, after=[mm4, tok_tmp5_free[tpar]], inc="vS")
                    bankfree[bk] = t1
                    t2 = p.op("vector", "tensor_tensor",
                              slab[:, gc_, dh * 512:(dh + 1) * 512],
                              slab[:, gc_, dh * 512:(dh + 1) * 512],
                              tmp5[:, tpar, :], op=ALU.add, after=[t1], inc="vS")
                    tok_tmp5_free[tpar] = t2
                    tok_ptq_free = mm4

        A.reset()
        p.barrier()
        for i in range(8):
            bankfree[i] = None

        if upto in ("x1", "x2"):
            emit_dump_slab(p, out_d, slab)
            emit_replay(nc, ctx, p, sems)
            return nc

        emit_phase_c(p, A, nc, locals(), upto)

        if upto in ("qkv", "att"):
            emit_dump_slab(p, out_d, slab)
            emit_replay(nc, ctx, p, sems)
            return nc
        if upto in ("ydump", "qkdump", "vdump", "prefdump", "vbindump",
                    "voutdump"):
            emit_replay(nc, ctx, p, sems)
            return nc

        A.reset()
        p.barrier()
        for i in range(8):
            bankfree[i] = None

        if upto == "x3":
            emit_dump_slab(p, out_d, slab)
            emit_replay(nc, ctx, p, sems)
            return nc

        emit_phase_d(p, A, nc, locals())

        emit_replay(nc, ctx, p, sems)
    return nc


def emit_dump_slab(p, out_d, slab):
    td = p.dma("gpsimd", out_d.rearrange("(a p) d -> p a d", p=128),
               slab[:, :, :], "qG", after=[p.tok("vS"), p.tok("tS"), p.tok("sS")])
    p.op("gpsimd", "wait_ge", "qG", p.ctr["qG"])


def emit_replay(nc, ctx, p, sems):
    blk = ctx.enter_context(nc.Block())
    for eng in ENGINES:
        getattr(blk, eng)(lambda h, e=eng: p.replay(e, h, sems))


def emit_phase_c(p, A, nc, env, upto="full"):
    g = lambda k: env[k]
    pb, slab, ident, bankfree = g("pb"), g("slab"), g("ident"), g("bankfree")
    epscol, onescol = g("epscol"), g("onescol")
    wqT_d, wkT_d, wvT_d, woT_d = g("wqT_d"), g("wkT_d"), g("wvT_d"), g("woT_d")
    qk_bin, qk_out = g("qk_bin"), g("qk_out")
    v_bin, v_out = g("v_bin"), g("v_out")
    o_bin, o_out = g("o_bin"), g("o_out")
    masks_d = g("masks_d")
    tok_consts = g("tok_consts")
    tok_vconsts = g("tok_vconsts")

    bn = A.alloc(16)
    xlnT = A.alloc(NCH, TPC)
    wqt = A.alloc(NCH, NCH, 128)
    stg = A.alloc(2, GT)

    def emit_ln(src, dst, after=()):
        b1 = p.op("vector", "bn_stats", bn[:, 0:6], src[:, 0:512],
                  after=list(after), inc="vS")
        b2 = p.op("vector", "bn_stats", bn[:, 6:12], src[:, 512:1024], inc="vS")
        ag = p.op("vector", "bn_aggr", bn[:, 12:14], bn[:, 0:12],
                  after=[b1, b2], inc="vS")
        sq = p.op("scalar", "activation", bn[:, 14:15], bn[:, 13:14], AF.Sqrt,
                  bias=epscol, after=[ag], inc="sS")
        rs = p.op("vector", "reciprocal", bn[:, 15:16], bn[:, 14:15],
                  after=[sq], inc="vS")
        return p.op("vector", "tensor_scalar", dst, src, bn[:, 12:13],
                    bn[:, 15:16], op0=ALU.subtract, op1=ALU.mult,
                    after=[rs], inc="vS")

    xtmp = A.alloc(D)
    for ch in range(16):
        ln = emit_ln(slab[:, ch, :], xtmp)
        for dc in range(NCH):
            bk = 6 + dc % 2
            tt_ = p.op("tensor", "transpose", pb[bk][:, 0:128],
                       xtmp[:, dc * 128:(dc + 1) * 128], ident[:, :],
                       after=[ln, bankfree[bk]], inc="tS")
            ev = p.op("vector", "tensor_copy",
                      xlnT[:, dc, ch * 128:(ch + 1) * 128], pb[bk][:, 0:128],
                      after=[tt_], inc="vS")
            bankfree[bk] = ev
    tok_xlnT = p.tok("vS")

    tok_wqt_free = None
    tok_stg_free = [None, None]

    def proj_fm(w_d, qk_slot, scale):
        nonlocal tok_wqt_free
        lw = p.dma("sync", wqt, w_d[:, :, :, :], "qL", after=[tok_wqt_free])
        last_mm = None
        for ic in range(NCH):
            for tg in range(4):
                bk = (ic * 4 + tg) % 2
                mm = None
                for dc in range(NCH):
                    aft = [lw, tok_xlnT, bankfree[bk]] if dc == 0 else []
                    mm = p.op("tensor", "matmul", pb[bk][:, :],
                              wqt[:, ic, dc, :],
                              xlnT[:, dc, tg * GT:(tg + 1) * GT],
                              start=(dc == 0), stop=(dc == NCH - 1), after=aft,
                              inc="tS" if dc == NCH - 1 else None)
                spar = (ic * 4 + tg) % 2
                ev = p.op("scalar", "activation", stg[:, spar, :], pb[bk][:, :],
                          AF.Copy, scale=scale,
                          after=[mm, tok_stg_free[spar]], inc="sS")
                bankfree[bk] = ev
                st = p.dma("gpsimd",
                           qk_bin[ic, qk_slot, :, tg * GT:(tg + 1) * GT],
                           stg[:, spar, :], "qG", after=[ev])
                tok_stg_free[spar] = st
                last_mm = mm
        tok_wqt_free = last_mm

    proj_fm(wqT_d, 0, 0.125)
    proj_fm(wkT_d, 1, 1.0)

    wvt = wqt
    wvt2 = wvt.rearrange("p a b c -> p a (b c)")
    lwv = p.dma("sync", wvt2, wvT_d.rearrange("(c p) i -> p c i", p=128), "qL",
                after=[tok_wqt_free])
    for ts in range(16):
        for ih in range(2):
            bk = (ts * 2 + ih) % 2
            mm = None
            for dc in range(NCH):
                aft = [lwv, tok_xlnT, bankfree[bk]] if dc == 0 else []
                mm = p.op("tensor", "matmul", pb[bk][:, :],
                          xlnT[:, dc, ts * 128:(ts + 1) * 128],
                          wvt2[:, dc, ih * 512:(ih + 1) * 512],
                          start=(dc == 0), stop=(dc == NCH - 1), after=aft,
                          inc="tS" if dc == NCH - 1 else None)
            spar = (ts * 2 + ih) % 2
            ev = p.op("scalar", "activation", stg[:, spar, :], pb[bk][:, :],
                      AF.Copy, after=[mm, tok_stg_free[spar]], inc="sS")
            bankfree[bk] = ev
            st = None
            for vs in range(4):
                st = p.dma("gpsimd",
                           v_bin[ih * 4 + vs, ts * 128:(ts + 1) * 128, :],
                           stg[:, spar, vs * 128:(vs + 1) * 128], "qG",
                           after=[ev])
            tok_stg_free[spar] = st

    if upto == "vbindump":
        out_d = g("out_d")
        for s_ in range(8):
            p.dma("gpsimd", out_d[:, s_ * 128:(s_ + 1) * 128],
                  v_bin[s_, :, :], "qG", after=[p.tok("qG")])
        p.op("gpsimd", "wait_ge", "qG", p.ctr["qG"])
        return

    a2a_qk = p.op("gpsimd", "collective", "AllToAll", ALU.bypass,
                  [qk_bin], [qk_out], after=[p.tok("qG")], inc="cc")
    a2a_v = p.op("gpsimd", "collective", "AllToAll", ALU.bypass,
                 [v_bin], [v_out], after=[a2a_qk], inc="cc")

    if upto == "qkv":
        p.op("gpsimd", "wait_ge", "cc", p.ctr["cc"])
        return

    if upto == "voutdump":
        out_d = g("out_d")
        for s_ in range(8):
            p.dma("gpsimd", out_d[:, s_ * 128:(s_ + 1) * 128],
                  v_out[s_, :, :], "qG", after=[a2a_v])
        p.op("gpsimd", "wait_ge", "qG", p.ctr["qG"])
        return

    A.reset()
    bn = A.alloc(16)
    masks = A.alloc(4, 512)
    QT2 = A.alloc(S)
    KT2 = A.alloc(S)
    vbuf = A.alloc(64, 130)
    absb = A.alloc(2, 2, 512)
    osb = A.alloc(2, 512)
    otm = A.alloc(2, 4, 64)
    pdif = A.alloc(2, 16)
    pref = A.alloc(2, 16)
    rzc = A.alloc(4)

    tok_masks = p.dma("sync", masks.rearrange("p a b -> p (a b)"),
                      masks_d[:, :], "qL", after=[p.tok("tS")])
    mz = p.op("vector", "memset", osb.rearrange("p a b -> p (a b)"), 0.0, inc="vS")
    tok_osb_zero = mz

    NQB = S // GT
    tok_qt_free = None
    tok_vb_free = None
    tok_otm_free = [None, None]
    abfree = {}
    osb_free = {}
    tslot_free = {}

    for bsw in range(2):
        lq = p.dma("sync", QT2.rearrange("p (s t) -> p s t", s=4),
                   qk_out[4 * bsw:4 * bsw + 4, 0, :, :].rearrange("s p t -> p s t"),
                   "qL", after=[a2a_qk, tok_qt_free])
        lk = p.dma("sync", KT2.rearrange("p (s t) -> p s t", s=4),
                   qk_out[4 * bsw:4 * bsw + 4, 1, :, :].rearrange("s p t -> p s t"),
                   "qL", after=[])
        lvs = []
        for h in range(2):
            for src in range(4):
                lvs.append(p.dma(
                    "scalar", vbuf[:, src * 16:(src + 1) * 16, 65 * h:65 * h + 64],
                    v_out[4 * bsw + src, :, h * 64:(h + 1) * 64]
                    .rearrange("(k p) d -> p k d", p=128), "qA",
                    after=[a2a_v, tok_vb_free]))
        m1 = p.op("vector", "memset", vbuf[:, :, 64:65], 1.0,
                  after=[lvs[-1]], inc="vS")
        m2 = p.op("vector", "memset", vbuf[:, :, 129:130], 1.0, inc="vS")
        tok_vb = m2

        if upto == "qkdump" and bsw == 0:
            out_d = g("out_d")
            p.dma("gpsimd",
                  out_d[0:1024, :].rearrange("(a p) c -> p a c", p=128),
                  QT2.rearrange("p (a c) -> p a c", c=1024), "qG",
                  after=[lq, lk])
            p.dma("gpsimd",
                  out_d[1024:2048, :].rearrange("(a p) c -> p a c", p=128),
                  KT2.rearrange("p (a c) -> p a c", c=1024), "qG",
                  after=[lk])
            p.op("gpsimd", "wait_ge", "qG", p.ctr["qG"])
            return
        if upto == "vdump" and bsw == 0:
            out_d = g("out_d")
            vfl = vbuf.rearrange("p a b -> p (a b)")
            p.dma("gpsimd",
                  out_d[0:1024, :].rearrange("(a p) c -> p a c", p=128),
                  vfl[:, 0:8192].rearrange("p (a c) -> p a c", c=1024), "qG",
                  after=[tok_vb])
            p.dma("gpsimd", out_d[1024:1152, 0:128], vfl[:, 8192:8320],
                  "qG", after=[tok_vb])
            p.op("gpsimd", "wait_ge", "qG", p.ctr["qG"])
            return

        tok_sS_pre = p.tok("sS")
        for h in (() if "prefix" in K_SKIP() else range(2)):
            for qb in range(NQB):
                for j in range(4):
                    kt = qb * 4 + j
                    aft = [tok_vb, tok_vconsts] if j == 0 else []
                    if qb == 0 and j == 0:
                        aft.append(bankfree[5])
                    p.op("tensor", "matmul", pb[5][0:65, qb:qb + 1],
                         vbuf[:, kt, 65 * h:65 * h + 65], onescol[:, :],
                         start=(j == 0), stop=(j == 3), after=aft,
                         inc="tS" if j == 3 else None)
            ev = p.op("vector", "tensor_copy", pdif[0:65, h, :], pb[5][0:65, 0:16],
                      after=[p.tok("tS")], inc="vS")
            bankfree[5] = ev
            pz = p.op("vector", "memset", pref[:, h, :], 0.0,
                      after=[tok_sS_pre], inc="vS")
            prev = pz
            for qb in range(1, NQB):
                prev = p.op("vector", "tensor_tensor", pref[0:65, h, qb:qb + 1],
                            pref[0:65, h, qb - 1:qb], pdif[0:65, h, qb - 1:qb],
                            op=ALU.add, after=[ev, prev], inc="vS")
        tok_pref = p.tok("vS")

        if upto == "prefdump" and bsw == 0:
            out_d = g("out_d")
            p.dma("gpsimd", out_d[0:128, 0:32],
                  pref.rearrange("p a b -> p (a b)"), "qG", after=[tok_pref])
            p.dma("gpsimd", out_d[0:128, 32:64],
                  pdif.rearrange("p a b -> p (a b)"), "qG", after=[tok_pref])
            p.op("gpsimd", "wait_ge", "qG", p.ctr["qG"])
            return

        for qb in range(NQB):
            nkt = qb * 4 + 4
            for kt in range(nkt):
                sA = kt % 2
                sB = 2 + kt % 2
                mmS = []
                for h, bk in ((0, sA), (1, sB)):
                    mm = p.op("tensor", "matmul", pb[bk][:, :],
                              KT2[64 * h:64 * h + 64, kt * 128:(kt + 1) * 128],
                              QT2[64 * h:64 * h + 64, qb * GT:(qb + 1) * GT],
                              start=True, stop=True,
                              tile_position=(64 * h, 0) if h else None,
                              after=[lq, lk, bankfree[bk]], inc="tS")
                    mmS.append(mm)
                diag = kt >= qb * 4
                abtoks = []
                for h, bk in (() if "sq" in K_SKIP() else ((0, sA), (1, sB))):
                    par = kt % 2
                    ev = p.op("scalar", "activation", absb[:, h, par, :],
                              pb[bk][:, :], AF.Square, bias=1.0,
                              after=[mmS[h], abfree.get((h, par))], inc="sS")
                    bankfree[bk] = ev
                    if diag and "diag" not in K_SKIP():
                        ev = p.op("vector", "scalar_tensor_tensor",
                                  absb[:, h, par, :], absb[:, h, par, :], 1.0,
                                  masks[:, kt - qb * 4, :], op0=ALU.add,
                                  op1=ALU.mult, after=[ev, tok_masks], inc="vS")
                    abtoks.append(ev)
                for h, obk in (() if "oacc" in K_SKIP() else ((0, 4), (1, 6))):
                    par = kt % 2
                    aft = [abtoks[h], tok_vb]
                    if kt == 0:
                        aft.append(bankfree[obk])
                    mmO = p.op("tensor", "matmul", pb[obk][0:65, :],
                               vbuf[:, kt, 65 * h:65 * h + 65], absb[:, h, par, :],
                               start=(kt == 0), stop=(kt == nkt - 1), after=aft,
                               inc="tS" if kt == nkt - 1 else None)
                    abfree[(h, par)] = mmO
            tok_av = p.tok("tS")
            for h, obk in (() if ("dance" in K_SKIP() or "oacc" in K_SKIP())
                           else ((0, 4), (1, 6))):
                evO = p.op("vector", "tensor_scalar", osb[0:65, h, :],
                           pb[obk][0:65, :], pref[0:65, h, qb:qb + 1], None,
                           op0=ALU.add,
                           after=[tok_av, tok_pref, tok_osb_zero,
                                  osb_free.get(h)], inc="vS")
                bankfree[obk] = evO
                dv = None
                tt_ = None
                if "tr" in K_SKIP():
                    osb_free[h] = evO
                    continue
                for qs in range(4):
                    sl = (h * 4 + qs) % 4
                    slot = pb[7][:, 128 * sl:128 * sl + 128]
                    tt_ = p.op("tensor", "transpose", slot,
                               osb[:, h, qs * 128:(qs + 1) * 128], ident[:, :],
                               after=[evO, tslot_free.get(sl), p.tok("vS")],
                               inc="tS")
                    if "div" in K_SKIP():
                        tslot_free[sl] = tt_
                        dv = None
                        continue
                    r1 = p.op("vector", "tensor_scalar", rzc[:, qs:qs + 1],
                              slot[:, 64:65], ATT_EPS2, None, op0=ALU.add,
                              after=[tt_], inc="vS")
                    r2 = p.op("vector", "reciprocal", rzc[:, qs:qs + 1],
                              rzc[:, qs:qs + 1], after=[r1], inc="vS")
                    dv = p.op("vector", "tensor_scalar", otm[:, h, qs, :],
                              slot[:, 0:64], rzc[:, qs:qs + 1], None,
                              op0=ALU.mult, after=[r2, tok_otm_free[h]], inc="vS")
                    tslot_free[sl] = dv
                osb_free[h] = tt_
                if "div" in K_SKIP() or "store" in K_SKIP():
                    continue
                dest = 4 * bsw + qb // 4
                r0 = (qb % 4) * 512
                st = p.dma("gpsimd", o_bin[dest, r0:r0 + 512, h * 64:(h + 1) * 64]
                           .rearrange("(a p) d -> p a d", p=128),
                           otm[:, h, :, :], "qG", after=[dv])
                tok_otm_free[h] = st
        tok_qt_free = p.tok("tS")
        tok_vb_free = p.tok("tS")

    a2a_o = p.op("gpsimd", "collective", "AllToAll", ALU.bypass,
                 [o_bin], [o_out], after=[p.tok("qG")], inc="cc")

    if upto == "att":
        p.op("gpsimd", "wait_ge", "cc", p.ctr["cc"])
        return

    if upto == "ydump":
        out_d = g("out_d")
        for s_ in range(8):
            p.dma("gpsimd", out_d.rearrange("t (s d) -> t s d", s=8)[:, s_, :],
                  o_out[s_, :, :], "qG", after=[a2a_o])
        p.op("gpsimd", "wait_ge", "qG", p.ctr["qG"])
        return

    A.reset()
    bn = A.alloc(16)
    ych = A.alloc(2, D)
    ytT = A.alloc(NCH, GT)
    wot = A.alloc(NCH, D)
    lwo = p.dma("sync", wot, woT_d[:, :, :], "qL", after=[p.tok("tS")])
    tok_ych_free = [None, None]
    tok_ytT_free = None
    for tg in range(4):
        evs = None
        for c in range(4):
            ch = tg * 4 + c
            par = c % 2
            ly = p.dma("sync", ych[:, par, :].rearrange("p (s d) -> p s d", s=8),
                       o_out[:, ch * 128:(ch + 1) * 128, :]
                       .rearrange("s p d -> p s d"),
                       "qL", after=[a2a_o, tok_ych_free[par]])
            for jc in range(NCH):
                bk = 6 + jc % 2
                tt_ = p.op("tensor", "transpose", pb[bk][:, 0:128],
                           ych[:, par, jc * 128:(jc + 1) * 128], ident[:, :],
                           after=[ly, bankfree[bk], tok_ytT_free], inc="tS")
                ev = p.op("vector", "tensor_copy",
                          ytT[:, jc, c * 128:(c + 1) * 128], pb[bk][:, 0:128],
                          after=[tt_], inc="vS")
                bankfree[bk] = ev
            tok_ych_free[par] = tt_
            evs = ev
        tok_ytT = evs
        last_mm = None
        for tcl in range(4):
            ch = tg * 4 + tcl
            for dh in range(2):
                bk = (tcl * 2 + dh) % 4
                mm = None
                for jc in range(NCH):
                    aft = [lwo, tok_ytT, bankfree[bk]] if jc == 0 else []
                    mm = p.op("tensor", "matmul", pb[bk][:, :],
                              ytT[:, jc, tcl * 128:(tcl + 1) * 128],
                              wot[:, jc, dh * 512:(dh + 1) * 512],
                              start=(jc == 0), stop=(jc == NCH - 1), after=aft,
                              inc="tS" if jc == NCH - 1 else None)
                t2 = p.op("vector", "tensor_tensor",
                          slab[:, ch, dh * 512:(dh + 1) * 512],
                          slab[:, ch, dh * 512:(dh + 1) * 512],
                          pb[bk][:, :], op=ALU.add, after=[mm], inc="vS")
                bankfree[bk] = t2
                last_mm = mm
        tok_ytT_free = last_mm


def emit_phase_d(p, A, nc, env):
    g = lambda k: env[k]
    pb, slab, ident, bankfree = g("pb"), g("slab"), g("ident"), g("bankfree")
    epscol, onesrow = g("epscol"), g("onesrow")
    ccell = g("ccell")
    t128, srow, scol = g("t128"), g("srow"), g("scol")
    iar_t = g("iar_t")
    out_d = g("out_d")
    wupT_d, wgtT_d, wdnT_d = g("wupT_d"), g("wgtT_d"), g("wdnT_d")
    up_sp, gt_sp = g("up_sp"), g("gt_sp")
    upq_sp, g2_sp, o_sp = g("upq_sp"), g("g2_sp"), g("o_sp")
    ar_bin, ar_out = g("ar_bin"), g("ar_out")

    bn = A.alloc(16)
    xqT = A.alloc(NCH, TPC)
    xtmp = A.alloc(D)
    mx = A.alloc(4)
    sv = A.alloc(8)

    def emit_ln(src, dst, after=()):
        b1 = p.op("vector", "bn_stats", bn[:, 0:6], src[:, 0:512],
                  after=list(after), inc="vS")
        b2 = p.op("vector", "bn_stats", bn[:, 6:12], src[:, 512:1024], inc="vS")
        ag = p.op("vector", "bn_aggr", bn[:, 12:14], bn[:, 0:12],
                  after=[b1, b2], inc="vS")
        sq = p.op("scalar", "activation", bn[:, 14:15], bn[:, 13:14], AF.Sqrt,
                  bias=epscol, after=[ag], inc="sS")
        rs = p.op("vector", "reciprocal", bn[:, 15:16], bn[:, 14:15],
                  after=[sq], inc="vS")
        return p.op("vector", "tensor_scalar", dst, src, bn[:, 12:13],
                    bn[:, 15:16], op0=ALU.subtract, op1=ALU.mult,
                    after=[rs], inc="vS")

    mz = p.op("vector", "memset", mx[:, :], 0.0, inc="vS")
    for ch in range(16):
        ln = emit_ln(slab[:, ch, :], xtmp)
        rd = p.op("vector", "tensor_reduce", mx[:, 1:2], xtmp, axis=AX.X,
                  op=ALU.max, apply_absolute_value=True, after=[ln, mz], inc="vS")
        p.op("vector", "tensor_tensor", mx[:, 0:1], mx[:, 0:1], mx[:, 1:2],
             op=ALU.max, after=[rd], inc="vS")
        for dc in range(NCH):
            bk = 6 + dc % 2
            tt_ = p.op("tensor", "transpose", pb[bk][:, 0:128],
                       xtmp[:, dc * 128:(dc + 1) * 128], ident[:, :],
                       after=[ln, bankfree[bk]], inc="tS")
            ev = p.op("vector", "tensor_copy",
                      xqT[:, dc, ch * 128:(ch + 1) * 128], pb[bk][:, 0:128],
                      after=[tt_], inc="vS")
            bankfree[bk] = ev

    def global_max_to_scalars(cols, bin_i, nvals, after=()):
        ts_ = list(after)
        for i, cap in enumerate(cols):
            ts_.append(p.op("vector", "tensor_copy", t128[:, i:i + 1], cap,
                            inc="vS"))
        tt_ = p.op("tensor", "transpose", pb[5][:, 0:128], t128[:, :], ident[:, :],
                   after=ts_ + [bankfree[5]], inc="tS")
        rd = p.op("vector", "tensor_reduce", scol[0:4, 0:1],
                  pb[5][0:4, 0:128], axis=AX.X, op=ALU.max,
                  after=[tt_], inc="vS")
        bankfree[5] = rd
        std = p.dma("gpsimd", ar_bin[bin_i].rearrange("a b -> b a"),
                    scol[0:4, 0:1], "qG", after=[rd])
        cc_ = p.op("gpsimd", "collective", "AllReduce", ALU.max,
                   [ar_bin[bin_i]], [ar_out[bin_i]], after=[std], inc="cc")
        lda = p.dma("sync", srow[0:1, 8:12], ar_out[bin_i][0:1, 0:4], "qL",
                    after=[cc_])
        outs = []
        for i in range(nvals):
            s1 = p.op("vector", "tensor_scalar", srow[0:1, i:i + 1],
                      srow[0:1, 8 + i:9 + i], ccell[0:1, 4:5], 1e-12,
                      op0=ALU.mult, op1=ALU.max, after=[lda], inc="vS")
            s2 = p.op("vector", "reciprocal", srow[0:1, 4 + i:5 + i],
                      srow[0:1, i:i + 1], after=[s1], inc="vS")
            outs.append(s2)
        tt2 = p.op("tensor", "matmul", pb[5][:, 0:8], onesrow[0:1, :],
                   srow[0:1, 0:8], start=True, stop=True,
                   after=outs + [bankfree[5]], inc="tS")
        evb = p.op("vector", "tensor_copy", sv[:, 0:8], pb[5][:, 0:8],
                   after=[tt2], inc="vS")
        bankfree[5] = evb
        return evb

    tok_sx = global_max_to_scalars([mx[:, 0:1]], 0, 1, after=[p.tok("vS")])

    if g("upto") == "svdump":
        p.dma("gpsimd", out_d[0:128, 0:8], sv[:, 0:8], "qG", after=[tok_sx])
        p.dma("gpsimd", out_d[0:128, 8:12], mx[:, 0:4], "qG", after=[tok_sx])
        p.dma("gpsimd", out_d[0:1, 16:32], srow[0:1, :], "qG", after=[tok_sx])
        p.dma("gpsimd", out_d[0:128, 32:48], scol[:, :], "qG", after=[tok_sx])
        p.op("gpsimd", "wait_ge", "qG", p.ctr["qG"])
        return

    qtoks = []
    for dc in range(NCH):
        v1 = p.op("vector", "tensor_scalar", xqT[:, dc, :], xqT[:, dc, :],
                  sv[:, 4:5], None, op0=ALU.mult, after=[tok_sx], inc="vS")
        cst = p.op("scalar", "activation", iar_t[:, 0:TPC], xqT[:, dc, :],
                   AF.Copy, after=[v1, qtoks[-1] if qtoks else None], inc="sS")
        cst2 = p.op("scalar", "activation", xqT[:, dc, :], iar_t[:, 0:TPC],
                    AF.Copy, after=[cst], inc="sS")
        v2 = p.op("vector", "tensor_scalar", xqT[:, dc, :], xqT[:, dc, :],
                  sv[:, 0:1], None, op0=ALU.mult, after=[cst2], inc="vS")
        qtoks.append(v2)
    tok_xq = qtoks[-1]

    wtl = A.alloc(2, NCH, 128)
    ustg = A.alloc(2, GT)
    mz2 = p.op("vector", "memset", mx[:, 0:2], 0.0, after=[p.tok("vS")], inc="vS")
    tok_wtl_free = [None, None]
    tok_ustg_free = [None, None]
    for mi, (wT_d, sp_d) in enumerate(((wupT_d, up_sp), (wgtT_d, gt_sp))):
        for ic in range(NIC):
            wpar = ic % 2
            lw = p.dma("sync", wtl[:, wpar, :, :], wT_d[ic, :, :, :], "qL",
                       after=[tok_wtl_free[wpar], g("tok_deq")])
            for tg in range(4):
                bk = (ic * 4 + tg) % 4
                mm = None
                for dc in range(NCH):
                    aft = [lw, tok_xq, bankfree[bk]] if dc == 0 else []
                    mm = p.op("tensor", "matmul", pb[bk][:, :],
                              wtl[:, wpar, dc, :],
                              xqT[:, dc, tg * GT:(tg + 1) * GT],
                              start=(dc == 0), stop=(dc == NCH - 1), after=aft,
                              inc="tS" if dc == NCH - 1 else None)
                spar = (ic * 4 + tg) % 2
                ev = p.op("vector", "tensor_copy", ustg[:, spar, :], pb[bk][:, :],
                          after=[mm, tok_ustg_free[spar]], inc="vS")
                bankfree[bk] = ev
                rd = p.op("vector", "tensor_reduce", mx[:, 2:3], ustg[:, spar, :],
                          axis=AX.X, op=ALU.max, apply_absolute_value=True,
                          after=[ev, mz2], inc="vS")
                p.op("vector", "tensor_tensor", mx[:, mi:mi + 1], mx[:, mi:mi + 1],
                     mx[:, 2:3], op=ALU.max, after=[rd], inc="vS")
                st = p.dma("gpsimd", sp_d[ic, :, tg * GT:(tg + 1) * GT],
                           ustg[:, spar, :], "qG", after=[ev])
                tok_ustg_free[spar] = st
            tok_wtl_free[wpar] = mm
    tok_d2 = p.tok("qG")

    tok_sug = global_max_to_scalars([mx[:, 0:1], mx[:, 1:2]], 1, 2,
                                    after=[p.tok("vS")])

    if g("upto") == "svdump2":
        p.dma("gpsimd", out_d[0:128, 0:8], sv[:, 0:8], "qG", after=[tok_sug])
        p.dma("gpsimd", out_d[0:128, 8:12], mx[:, 0:4], "qG", after=[tok_sug])
        p.dma("gpsimd", out_d[0:1, 16:32], srow[0:1, :], "qG", after=[tok_sug])
        p.dma("gpsimd", out_d[0:128, 32:160],
              wupT_d[0, :, 0, :], "qG", after=[tok_sug])
        p.dma("gpsimd", out_d[0:128, 160:288],
              up_sp[0, :, 0:128], "qG", after=[tok_sug])
        p.op("gpsimd", "wait_ge", "qG", p.ctr["qG"])
        return

    A2 = A
    utile = A.alloc(2, GT)
    gtile = A.alloc(2, GT)
    w1 = A.alloc(GT)
    w2 = A.alloc(GT)
    mz3 = p.op("vector", "memset", mx[:, 2:3], 0.0, after=[tok_sug], inc="vS")
    tok_ut_free = [None, None]
    tok_gt_free = [None, None]
    prev_cast = None
    prev_sig = None
    tok_w_free = None
    for ti in range(NIC * 4):
        ic, tg = divmod(ti, 4)
        par = ti % 2
        lu = p.dma("sync", utile[:, par, :], up_sp[ic, :, tg * GT:(tg + 1) * GT],
                   "qL", after=[tok_ut_free[par], tok_d2])
        lg_ = p.dma("scalar", gtile[:, par, :], gt_sp[ic, :, tg * GT:(tg + 1) * GT],
                    "qA", after=[tok_gt_free[par], tok_d2])
        v1 = p.op("vector", "tensor_scalar", utile[:, par, :], utile[:, par, :],
                  sv[:, 4:5], None, op0=ALU.mult, after=[lu, tok_sug], inc="vS")
        c1 = p.op("scalar", "activation", iar_t[:, 0:GT], utile[:, par, :], AF.Copy,
                  after=[v1, prev_cast], inc="sS")
        c1b = p.op("scalar", "activation", utile[:, par, :], iar_t[:, 0:GT],
                   AF.Copy, after=[c1], inc="sS")
        v2 = p.op("vector", "tensor_scalar", utile[:, par, :], utile[:, par, :],
                  sv[:, 0:1], None, op0=ALU.mult, after=[c1b], inc="vS")
        s1 = p.dma("gpsimd", upq_sp[ic, :, tg * GT:(tg + 1) * GT], utile[:, par, :],
                   "qG", after=[v2])
        tok_ut_free[par] = s1
        v3 = p.op("vector", "tensor_scalar", gtile[:, par, :], gtile[:, par, :],
                  sv[:, 5:6], None, op0=ALU.mult, after=[lg_, tok_sug], inc="vS")
        c2 = p.op("scalar", "activation", iar_t[:, 512:512 + GT], gtile[:, par, :],
                  AF.Copy, after=[v3, c1], inc="sS")
        c2b = p.op("scalar", "activation", gtile[:, par, :],
                   iar_t[:, 512:512 + GT], AF.Copy, after=[c2], inc="sS")
        v4 = p.op("vector", "tensor_scalar", gtile[:, par, :],
                  gtile[:, par, :], sv[:, 1:2], None, op0=ALU.mult,
                  after=[c2b], inc="vS")
        sg = p.op("scalar", "activation", w1, gtile[:, par, :], AF.Sigmoid,
                  after=[v4, prev_sig, tok_w_free], inc="sS")
        v5 = p.op("vector", "tensor_scalar", w2, w1, 65536.0, None,
                  op0=ALU.mult, after=[sg, tok_w_free], inc="vS")
        c3 = p.op("scalar", "activation", iar_t[:, 1024:1024 + GT], w2, AF.Copy,
                  after=[v5], inc="sS")
        c3b = p.op("scalar", "activation", w1, iar_t[:, 1024:1024 + GT],
                   AF.Copy, after=[c3], inc="sS")
        v6 = p.op("vector", "tensor_scalar", w1, w1,
                  65535.0, 1.0 / 65536.0, op0=ALU.min, op1=ALU.mult,
                  after=[c3b], inc="vS")
        prev_sig = v6
        v7 = p.op("vector", "tensor_tensor", gtile[:, par, :], gtile[:, par, :],
                  w1, op=ALU.mult, after=[v6], inc="vS")
        tok_w_free = v7
        rd = p.op("vector", "tensor_reduce", mx[:, 3:4], gtile[:, par, :],
                  axis=AX.X, op=ALU.max, apply_absolute_value=True,
                  after=[v7, mz3], inc="vS")
        p.op("vector", "tensor_tensor", mx[:, 2:3], mx[:, 2:3], mx[:, 3:4],
             op=ALU.max, after=[rd], inc="vS")
        s2 = p.dma("gpsimd", g2_sp[ic, :, tg * GT:(tg + 1) * GT], gtile[:, par, :],
                   "qG", after=[v7])
        tok_gt_free[par] = s2
        prev_cast = c3b

    tok_d4 = p.tok("qG")
    tok_sg2 = global_max_to_scalars([mx[:, 2:3]], 2, 1, after=[p.tok("vS")])

    mz4 = p.op("vector", "memset", mx[:, 0:1], 0.0, after=[tok_sg2], inc="vS")
    prev_cast = None
    for ti in range(NIC * 4):
        ic, tg = divmod(ti, 4)
        par = ti % 2
        lgq = p.dma("sync", gtile[:, par, :], g2_sp[ic, :, tg * GT:(tg + 1) * GT],
                    "qL", after=[tok_gt_free[par], tok_d4])
        luq = p.dma("scalar", utile[:, par, :], upq_sp[ic, :, tg * GT:(tg + 1) * GT],
                    "qA", after=[tok_ut_free[par], tok_d4])
        v1 = p.op("vector", "tensor_scalar", gtile[:, par, :], gtile[:, par, :],
                  sv[:, 4:5], None, op0=ALU.mult, after=[lgq, tok_sg2], inc="vS")
        c1 = p.op("scalar", "activation", iar_t[:, 0:GT], gtile[:, par, :], AF.Copy,
                  after=[v1, prev_cast], inc="sS")
        c1b = p.op("scalar", "activation", gtile[:, par, :], iar_t[:, 0:GT],
                   AF.Copy, after=[c1], inc="sS")
        v2 = p.op("vector", "tensor_scalar", gtile[:, par, :], gtile[:, par, :],
                  sv[:, 0:1], None, op0=ALU.mult, after=[c1b], inc="vS")
        prev_cast = c1b
        v3 = p.op("vector", "tensor_tensor", gtile[:, par, :], gtile[:, par, :],
                  utile[:, par, :], op=ALU.mult, after=[v2, luq], inc="vS")
        tok_ut_free[par] = v3
        rd = p.op("vector", "tensor_reduce", mx[:, 1:2], gtile[:, par, :],
                  axis=AX.X, op=ALU.max, apply_absolute_value=True,
                  after=[v3, mz4], inc="vS")
        p.op("vector", "tensor_tensor", mx[:, 0:1], mx[:, 0:1], mx[:, 1:2],
             op=ALU.max, after=[rd], inc="vS")
        st = p.dma("gpsimd", o_sp[ic, :, tg * GT:(tg + 1) * GT], gtile[:, par, :],
                   "qG", after=[v3])
        tok_gt_free[par] = st

    tok_d6 = p.tok("qG")
    tok_so = global_max_to_scalars([mx[:, 0:1]], 3, 1, after=[p.tok("vS")])

    if g("upto") == "odump":
        for ic_ in range(16):
            p.dma("gpsimd", out_d[ic_ * 128:(ic_ + 1) * 128, :]
                  .rearrange("p (a d) -> p a d", d=128),
                  o_sp[ic_, :, 0:1024].rearrange("p (a d) -> p a d", d=128),
                  "qG", after=[tok_so])
        p.op("gpsimd", "wait_ge", "qG", p.ctr["qG"])
        return

    if g("upto") == "svdump3":
        p.dma("gpsimd", out_d[0:128, 0:8], sv[:, 0:8], "qG", after=[tok_so])
        p.dma("gpsimd", out_d[0:128, 8:72], upq_sp[0, :, 0:64], "qG",
              after=[tok_so])
        p.dma("gpsimd", out_d[0:128, 72:136], g2_sp[0, :, 0:64], "qG",
              after=[tok_so])
        p.dma("gpsimd", out_d[0:128, 136:200], o_sp[0, :, 0:64], "qG",
              after=[tok_so])
        p.op("gpsimd", "wait_ge", "qG", p.ctr["qG"])
        return

    wdr = A.alloc(2, D)
    oqt = A.alloc(2, GT)
    tok_wdr_free = [None, None]
    tok_oqt_free = [None, None]
    prev_cast = None
    tok_out = None
    for tg in range(4):
        mmfin = {}
        for ic in range(NIC):
            par = ic % 2
            lo = p.dma("sync", oqt[:, par, :], o_sp[ic, :, tg * GT:(tg + 1) * GT],
                       "qL", after=[tok_oqt_free[par], tok_d6, p.tok("tS")])
            lw = p.dma("scalar", wdr[:, par, :], wdnT_d[ic, :, :], "qA",
                       after=[tok_wdr_free[par], tok_d6, p.tok("tS")])
            v1 = p.op("vector", "tensor_scalar", oqt[:, par, :], oqt[:, par, :],
                      sv[:, 4:5], None, op0=ALU.mult, after=[lo, tok_so], inc="vS")
            c1 = p.op("scalar", "activation", iar_t[:, 0:GT], oqt[:, par, :],
                      AF.Copy, after=[v1, prev_cast], inc="sS")
            c1b = p.op("scalar", "activation", oqt[:, par, :], iar_t[:, 0:GT],
                       AF.Copy, after=[c1], inc="sS")
            v2 = p.op("vector", "tensor_scalar", oqt[:, par, :], oqt[:, par, :],
                      sv[:, 0:1], None, op0=ALU.mult, after=[c1b], inc="vS")
            prev_cast = c1b
            for ts in range(4):
                for dh in range(2):
                    bk = ts * 2 + dh
                    aft = [v2, lw]
                    if ic == 0:
                        aft.append(bankfree[bk])
                    mmfin[bk] = p.op(
                        "tensor", "matmul", pb[bk][:, :],
                        oqt[:, par, ts * 128:(ts + 1) * 128],
                        wdr[:, par, dh * 512:(dh + 1) * 512],
                        start=(ic == 0), stop=(ic == NIC - 1), after=aft,
                        inc="tS" if (ic == NIC - 1 or (ts == 3 and dh == 1))
                        else None)
            tok_oqt_free[par] = mmfin[7]
            tok_wdr_free[par] = mmfin[7]
        for ts in range(4):
            ch = tg * 4 + ts
            for dh in range(2):
                bk = ts * 2 + dh
                t2 = p.op("vector", "tensor_tensor",
                          slab[:, ch, dh * 512:(dh + 1) * 512],
                          slab[:, ch, dh * 512:(dh + 1) * 512],
                          pb[bk][:, :], op=ALU.add, after=[mmfin[bk]], inc="vS")
                bankfree[bk] = t2
            tok_out = p.dma("gpsimd", out_d[ch * 128:(ch + 1) * 128, :],
                            slab[:, ch, :], "qG", after=[p.tok("vS")])
    p.op("gpsimd", "wait_ge", "qG", p.ctr["qG"])


def _mk_masks():
    m = np.zeros((128, 4, 512), np.float32)
    tk = np.arange(128)[:, None]
    tq = np.arange(512)[None, :]
    for j in range(4):
        m[:, j, :] = (128 * j + tk <= tq).astype(np.float32)
    return np.ascontiguousarray(m.reshape(128, 4 * 512))


def _blk_gate(W):
    return np.ascontiguousarray(
        np.asarray(W, np.float32).reshape(8, 128, 8, 128).transpose(1, 2, 0, 3))


def _blk_T(W):
    return np.ascontiguousarray(
        np.asarray(W, np.float32).reshape(8, 128, 8, 128).transpose(3, 0, 2, 1))


def _prep_inputs(inputs):
    f = lambda k: np.asarray(inputs[k], np.float32)
    x = np.ascontiguousarray(f("L0_attn_x").reshape(B * S, D))
    gamma, beta = f("L0_attn_gamma"), f("L0_attn_beta")
    gb = np.ascontiguousarray(
        np.concatenate([np.tile(gamma, (128, 1)), np.tile(beta, (128, 1))], axis=1))
    bcols = np.ascontiguousarray(
        np.stack([f(f"L0_attn_b{k}").reshape(8, 128).T for k in "ifos"],
                 axis=1).reshape(128, 32))
    common = dict(
        Wi_b=_blk_gate(f("L0_attn_Wi")), Wf_b=_blk_gate(f("L0_attn_Wf")),
        Wog_b=_blk_gate(f("L0_attn_Wo")), Ws_b=_blk_gate(f("L0_attn_Ws")),
        bcols=bcols, gb_bc=gb,
        U1=np.ascontiguousarray(f("L0_ffn_U1")),
        V1_b=np.ascontiguousarray(
            f("L0_ffn_V1").reshape(2, 128, 32, 128).transpose(1, 2, 0, 3)),
        b1cols=np.ascontiguousarray(f("L0_ffn_b1").reshape(32, 128).T),
        U2_b=np.ascontiguousarray(
            f("L0_ffn_U2").reshape(32, 128, 2, 128).transpose(1, 0, 2, 3)),
        V2=np.ascontiguousarray(f("L0_ffn_V2")),
        b2_bc=np.ascontiguousarray(np.tile(f("L0_ffn_b2"), (128, 1))),
        WqT_b=_blk_T(f("L1_attn_Wq")), WkT_b=_blk_T(f("L1_attn_Wk")),
        WvT=np.ascontiguousarray(f("L1_attn_Wv").T),
        WoT_b=np.ascontiguousarray(
            f("L1_attn_Wo").T.reshape(8, 128, 1024).transpose(1, 0, 2)),
        masks=_mk_masks(),
        ident_in=np.eye(128, dtype=np.float32),
        wup=np.ascontiguousarray(f("L1_ffn_w_up")),
        wgt=np.ascontiguousarray(f("L1_ffn_w_gate")),
        wdn=np.ascontiguousarray(f("L1_ffn_w_down")),
    )
    in_maps = []
    for c in range(NCORE):
        oh = np.zeros((128, NCORE), np.float32)
        oh[:, c] = 1.0
        m = dict(common)
        m["x0"] = np.ascontiguousarray(x[c * TPC:(c + 1) * TPC])
        m["onehot"] = oh
        in_maps.append(m)
    return in_maps


_UPTO = "full"
_ATT_SKIP = set()


def kernel(**inputs):
    in_maps = _prep_inputs(inputs)
    nc = build(upto=_UPTO)
    res = run_bass_kernel_spmd(nc, in_maps, core_ids=list(range(NCORE)))
    out = np.concatenate([res.results[c]["out"] for c in range(NCORE)], axis=0)
    return out.reshape(B, S, D)


# version 14
# speedup vs baseline: 1.0543x; 1.0543x over previous
import numpy as np
from contextlib import ExitStack

import concourse.bass as bass
import concourse.mybir as mybir
from concourse.bass_utils import run_bass_kernel_spmd
from concourse.aot_env import allow_ici_all_to_all

F32 = mybir.dt.float32
I32 = mybir.dt.int32
AF = mybir.ActivationFunctionType
ALU = mybir.AluOpType
AX = mybir.AxisListType

NCORE = 8
B, S, D = 2, 8192, 1024
TPC = 2048
NG, GT = 4, 512
NCH = 8
DFF, RANK, IFF = 4096, 256, 3072
NIC = IFF // 128
LN_EPS = 1e-5
ATT_EPS2 = 2e-6
HEADS, HD = 16, 64

ENGINES = ("sync", "tensor", "scalar", "vector", "gpsimd")


def K_SKIP():
    return globals().get("_ATT_SKIP", set())
SEM_NAMES = ["tS", "sS", "vS", "gS", "qL", "qA", "qG", "qV", "cc", "bar"]


class Plan:
    def __init__(self):
        self.ops = {e: [] for e in ENGINES}
        self.ctr = {}

    def tok(self, sem):
        return (sem, self.ctr.get(sem, 0))

    def op(self, eng, meth, *args, after=(), inc=None, incn=1, **kw):
        waits = {}
        for t in after:
            if t is None:
                continue
            s, v = t
            if v > 0:
                waits[s] = max(waits.get(s, 0), v)
        if inc is not None:
            self.ctr[inc] = self.ctr.get(inc, 0) + incn
        self.ops[eng].append((meth, args, kw, sorted(waits.items()), inc, incn))
        return (inc, self.ctr[inc]) if inc is not None else None

    def dma(self, eng, out, in_, sem, after=()):
        return self.op(eng, "dma_start", after=after, inc=sem, incn=16, out=out, in_=in_)

    def barrier(self):
        self.ctr["bar"] = self.ctr.get("bar", 0) + len(ENGINES)
        tgt = self.ctr["bar"]
        for e in ENGINES:
            self.ops[e].append(("sem_inc", ("bar", 1), {}, [], None, None))
            self.ops[e].append(("wait_ge", ("bar", tgt), {}, [], None, None))
        return ("bar", tgt)

    def replay(self, eng, h, sems):
        for meth, args, kw, waits, inc, incn in self.ops[eng]:
            for s, v in waits:
                h.wait_ge(sems[s], v)
            if meth == "sem_inc":
                h.sem_inc(sems[args[0]], args[1])
                continue
            if meth == "wait_ge":
                h.wait_ge(sems[args[0]], args[1])
                continue
            if meth == "collective":
                kind, op_, ins_, outs_ = args
                with allow_ici_all_to_all():
                    inst = h.collective_compute(
                        kind, op_, replica_groups=[list(range(NCORE))],
                        ins=ins_, outs=outs_)
            else:
                inst = getattr(h, meth)(*args, **kw)
            if inc is not None:
                inst.then_inc(sems[inc], incn)


class Arena:

    def __init__(self, ap, total):
        self.ap = ap
        self.total = total
        self.off = 0

    def alloc(self, *shape):
        n = int(np.prod(shape))
        assert self.off + n <= self.total, (self.off, n, self.total)
        v = self.ap[:, self.off:self.off + n]
        self.off += n
        if len(shape) > 1:
            names = " ".join(f"a{i}" for i in range(len(shape)))
            v = v.rearrange(f"p ({names}) -> p {names}",
                            **{f"a{i}": s for i, s in enumerate(shape)})
        return v

    def reset(self):
        self.off = 0


def build(upto="full"):
    nc = bass.Bass("TRN2", target_bir_lowering=False, debug=False,
                   detect_race_conditions=False)

    def din(name, shape):
        return nc.dram_tensor(name, list(shape), F32, kind="ExternalInput").ap()

    x0_d = din("x0", (TPC, D))
    wg_d = [din(n, (128, NCH, NCH, 128)) for n in ("Wi_b", "Wf_b", "Wog_b", "Ws_b")]
    bcols_d = din("bcols", (128, 4 * NCH))
    gb_d = din("gb_bc", (128, 2 * D))
    u1_d = din("U1", (D, RANK))
    v1_d = din("V1_b", (128, 32, 2, 128))
    b1c_d = din("b1cols", (128, 32))
    u2_d = din("U2_b", (128, 32, 2, 128))
    v2_d = din("V2", (RANK, D))
    b2bc_d = din("b2_bc", (128, D))
    wqT_d = din("WqT_b", (128, NCH, NCH, 128))
    wkT_d = din("WkT_b", (128, NCH, NCH, 128))
    wvT_d = din("WvT", (D, D))
    woT_d = din("WoT_b", (128, NCH, D))
    masks_d = din("masks", (128, 4 * 512))
    onehot_d = din("onehot", (128, NCORE))
    ident_d = din("ident_in", (128, 128))
    wup_d = din("wup", (IFF, D))
    wgt_d = din("wgt", (IFF, D))
    wdn_d = din("wdn", (D, IFF))

    out_d = nc.dram_tensor("out", [TPC, D], F32, kind="ExternalOutput").ap()

    def dint(name, shape):
        return nc.dram_tensor(name, list(shape), F32).ap()

    zl_sp = dint("zl_sp", (D, TPC))
    c_sp = dint("c_sp", (D, TPC))
    carry_bin = dint("carry_bin", (128, 16))
    carry_out = dint("carry_out", (NCORE * 128, 16))
    qk_bin = dint("qk_bin", (NCORE, 2, 128, TPC))
    qk_out = dint("qk_out", (NCORE, 2, 128, TPC))
    v_bin = dint("v_bin", (NCORE, TPC, 128))
    v_out = dint("v_out", (NCORE, TPC, 128))
    o_bin = dint("o_bin", (NCORE, TPC, 128))
    o_out = dint("o_out", (NCORE, TPC, 128))
    wupT_d = dint("wupT", (NIC, 128, NCH, 128))
    wgtT_d = dint("wgtT", (NIC, 128, NCH, 128))
    wdnT_d = dint("wdnT", (NIC, 128, D))
    up_sp = dint("up_sp", (NIC, 128, TPC))
    gt_sp = dint("gt_sp", (NIC, 128, TPC))
    upq_sp = dint("upq_sp", (NIC, 128, TPC))
    g2_sp = dint("g2_sp", (NIC, 128, TPC))
    o_sp = dint("o_sp", (NIC, 128, TPC))
    ar_bin = [dint(f"ar{i}_bin", (1, 4)) for i in range(4)]
    ar_out = [dint(f"ar{i}_out", (1, 4)) for i in range(4)]

    zl_v = zl_sp.rearrange("(c p) t -> p c t", p=128)
    c_v = c_sp.rearrange("(c p) t -> p c t", p=128)
    carry_v = carry_out.rearrange("(c p) k -> p c k", p=128)

    p = Plan()

    with ExitStack() as ctx:
        sems = {s: ctx.enter_context(nc.semaphore(s)) for s in SEM_NAMES}
        pb = [ctx.enter_context(nc.psum_tensor(f"pb{i}", [128, 512], F32))
              for i in range(8)]
        slab_t = ctx.enter_context(nc.sbuf_tensor("slab", [128, 16 * D], F32))
        slab = slab_t[:, :].rearrange("p (a d) -> p a d", a=16)
        AR_COLS = 32000
        arena_t = ctx.enter_context(nc.sbuf_tensor("arena", [128, AR_COLS], F32))
        A = Arena(arena_t[:, :], AR_COLS)
        iar_t = ctx.enter_context(nc.sbuf_tensor("iar", [128, 3072], I32))
        ident = ctx.enter_context(nc.sbuf_tensor("ident_s", [128, 128], F32))
        bcols = ctx.enter_context(nc.sbuf_tensor("bcols_s", [128, 4, NCH], F32))
        b1cols = ctx.enter_context(nc.sbuf_tensor("b1cols_s", [128, 32], F32))
        onehot = ctx.enter_context(nc.sbuf_tensor("onehot_s", [128, NCORE], F32))
        hincol = ctx.enter_context(nc.sbuf_tensor("hincol", [128, NCH], F32))
        ccell = ctx.enter_context(nc.sbuf_tensor("ccell", [128, 8], F32))
        onescol = ctx.enter_context(nc.sbuf_tensor("onescol", [128, 1], F32))
        onesrow = ctx.enter_context(nc.sbuf_tensor("onesrow", [1, 128], F32))
        t128 = ctx.enter_context(nc.sbuf_tensor("t128", [128, 128], F32))
        srow = ctx.enter_context(nc.sbuf_tensor("srow", [1, 16], F32))
        scol = ctx.enter_context(nc.sbuf_tensor("scol", [128, 16], F32))

        epscol = ccell[:, 0:1]

        bankfree = {i: None for i in range(8)}

        for dst, src in ((ident[:, :], ident_d[:, :]),
                         (bcols[:, :, :].rearrange("p a c -> p (a c)"), bcols_d[:, :]),
                         (b1cols[:, :], b1c_d[:, :]),
                         (onehot[:, :], onehot_d[:, :])):
            p.dma("sync", dst, src, "qL")
        tok_consts = p.tok("qL")
        p.op("vector", "memset", ccell[:, :], 0.0)
        p.op("vector", "memset", epscol, LN_EPS)
        p.op("vector", "memset", ccell[:, 1:2], 7.0)
        p.op("vector", "reciprocal", ccell[:, 2:3], ccell[:, 1:2])
        p.op("vector", "memset", ccell[:, 3:4], 32767.0)
        p.op("vector", "reciprocal", ccell[:, 4:5], ccell[:, 3:4])
        p.op("vector", "memset", onescol[:, :], 1.0)
        p.op("vector", "memset", onesrow[:, :], 1.0)
        p.op("vector", "memset", t128[:, :], 0.0, inc="vS")
        tok_vconsts = p.tok("vS")

        def dequant(src_d, nrc, ccols, store_fn, tag):
            nb = ccols // 32
            ntc = ccols // 128
            wch = A.alloc(ccols)
            bm = A.alloc(nb)
            sS_ = A.alloc(nb)
            rr = A.alloc(nb)
            stg = A.alloc(2, 128)
            tok_w_free = p.tok("tS")
            tok_stg_free = [p.tok("qG"), p.tok("qG")]
            for rc in range(nrc):
                ld = p.dma("sync", wch,
                           src_d[rc * 128:(rc + 1) * 128, :], "qL",
                           after=[tok_w_free])
                w3 = wch.rearrange("p (nb e) -> p nb e", e=32)
                t1 = p.op("vector", "tensor_reduce",
                          bm.rearrange("p (a o) -> p a o", o=1), w3,
                          axis=AX.X, op=ALU.max, apply_absolute_value=True,
                          after=[ld], inc="vS")
                t2 = p.op("vector", "tensor_scalar", sS_, bm, ccell[:, 2:3],
                          1e-12, op0=ALU.mult, op1=ALU.max, after=[t1],
                          inc="vS")
                t3 = p.op("vector", "reciprocal", rr, sS_, after=[t2], inc="vS")
                rbc = rr.rearrange("p (a o) -> p a o", o=1).broadcast_to((128, nb, 32))
                sbc = sS_.rearrange("p (a o) -> p a o", o=1).broadcast_to((128, nb, 32))
                t4 = p.op("vector", "tensor_tensor", w3, w3, rbc, op=ALU.mult,
                          after=[t3], inc="vS")
                t5 = p.op("scalar", "activation", iar_t[:, 0:ccols], wch, AF.Copy,
                          after=[t4], inc="sS")
                t5b = p.op("scalar", "activation", wch, iar_t[:, 0:ccols],
                           AF.Copy, after=[t5], inc="sS")
                t6 = p.op("vector", "tensor_tensor", w3, w3, sbc,
                          op=ALU.mult, after=[t5b], inc="vS")
                last_ev = None
                for tc in range(ntc):
                    bk = tc % 2
                    tt_ = p.op("tensor", "transpose", pb[bk][:, 0:128],
                               wch[:, tc * 128:(tc + 1) * 128], ident[:, :],
                               after=[t6, bankfree[bk], tok_consts], inc="tS")
                    ev = p.op("vector", "tensor_copy", stg[:, bk, :], pb[bk][:, 0:128],
                              after=[tt_, tok_stg_free[bk]], inc="vS")
                    bankfree[bk] = ev
                    st = p.dma("gpsimd", store_fn(rc, tc), stg[:, bk, :], "qG",
                               after=[ev])
                    tok_stg_free[bk] = st
                    last_ev = tt_
                tok_w_free = last_ev
            A.reset()

        if upto in ("full", "x3up", "svdump2", "svdump3", "odump"):
            dequant(wup_d, NIC, D,
                    lambda rc, tc: wupT_d[rc, :, tc, :], "up")
            dequant(wgt_d, NIC, D,
                    lambda rc, tc: wgtT_d[rc, :, tc, :], "gt")
            dequant(wdn_d, NCH, IFF,
                    lambda rc, tc: wdnT_d[tc, :, rc * 128:(rc + 1) * 128], "dn")
        tok_deq = p.tok("qG")
        A.reset()

        p.barrier()
        for i in range(8):
            bankfree[i] = None

        xlnT = A.alloc(NCH, GT)
        fb = A.alloc(NCH, GT)
        ab = A.alloc(NCH, GT)
        ob = A.alloc(NCH, GT)
        Pb = A.alloc(NCH, GT)
        hb = A.alloc(GT)
        w4a = A.alloc(2, NCH, 128)
        w4b = A.alloc(2, NCH, 128)
        x0st = A.alloc(2, D)
        bn = A.alloc(16)
        ccol = A.alloc(NCH, 2)
        zbuf = A.alloc(GT)
        tmp_i = A.alloc(GT)
        tmp_s = A.alloc(GT)
        carr = A.alloc(NCORE, 16)
        hina = A.alloc(NCH, NCORE)

        p.op("vector", "memset", zbuf, 0.0, inc="vS")
        tok_zbuf = p.tok("vS")

        tok_x0_free = [None, None]
        tok_xlnT_spill = None
        tok_P_spill = None
        tok_w4a_free = None
        tok_w4b_free = None
        tok_tmp_free = None
        tok_pass2 = None

        def emit_ln(src, dst, after=()):
            b1 = p.op("vector", "bn_stats", bn[:, 0:6], src[:, 0:512],
                      after=list(after), inc="vS")
            b2 = p.op("vector", "bn_stats", bn[:, 6:12], src[:, 512:1024], inc="vS")
            ag = p.op("vector", "bn_aggr", bn[:, 12:14], bn[:, 0:12],
                      after=[b1, b2], inc="vS")
            sq = p.op("scalar", "activation", bn[:, 14:15], bn[:, 13:14], AF.Sqrt,
                      bias=epscol, after=[ag, tok_vconsts], inc="sS")
            rs = p.op("vector", "reciprocal", bn[:, 15:16], bn[:, 14:15],
                      after=[sq], inc="vS")
            return p.op("vector", "tensor_scalar", dst, src, bn[:, 12:13],
                        bn[:, 15:16], op0=ALU.subtract, op1=ALU.mult,
                        after=[rs], inc="vS")

        for g in range(NG):
            ev = None
            for c in range(4):
                par = c % 2
                ld = p.dma("sync", x0st[:, par, :],
                           x0_d[(g * 4 + c) * 128:(g * 4 + c + 1) * 128, :],
                           "qL", after=[tok_x0_free[par]])
                ln = emit_ln(x0st[:, par, :], x0st[:, par, :], after=[ld])
                for dc in range(NCH):
                    bk = 6 + dc % 2
                    tt_ = p.op("tensor", "transpose", pb[bk][:, 0:128],
                               x0st[:, par, dc * 128:(dc + 1) * 128], ident[:, :],
                               after=[ln, bankfree[bk], tok_consts], inc="tS")
                    ev = p.op("vector", "tensor_copy",
                              xlnT[:, dc, c * 128:(c + 1) * 128], pb[bk][:, 0:128],
                              after=[tt_, tok_xlnT_spill], inc="vS")
                    bankfree[bk] = ev
                tok_x0_free[par] = tt_
            tok_xlnT_ready = ev

            for e8 in range(NCH):
                tif = [p.dma("sync", w4a[:, k, :, :], wg_d[k][:, e8, :, :], "qL",
                             after=[tok_w4a_free, tok_consts]) for k in range(2)]
                mm_tok = [None] * 4
                for gi in range(2):
                    for dc in range(NCH):
                        aft = [tif[1], tok_xlnT_ready, bankfree[gi]] if dc == 0 else []
                        mm_tok[gi] = p.op("tensor", "matmul", pb[gi][:, :],
                                          w4a[:, gi, dc, :], xlnT[:, dc, :],
                                          start=(dc == 0), stop=(dc == NCH - 1),
                                          after=aft,
                                          inc="tS" if dc == NCH - 1 else None)
                tok_w4a_free = mm_tok[1]
                tos = [p.dma("sync", w4b[:, k, :, :], wg_d[2 + k][:, e8, :, :], "qL",
                             after=[tok_w4b_free]) for k in range(2)]
                for gi in range(2, 4):
                    for dc in range(NCH):
                        aft = [tos[1], tok_xlnT_ready, bankfree[gi]] if dc == 0 else []
                        mm_tok[gi] = p.op("tensor", "matmul", pb[gi][:, :],
                                          w4b[:, gi - 2, dc, :], xlnT[:, dc, :],
                                          start=(dc == 0), stop=(dc == NCH - 1),
                                          after=aft,
                                          inc="tS" if dc == NCH - 1 else None)
                tok_w4b_free = mm_tok[3]
                ei = p.op("scalar", "activation", tmp_i, pb[0][:, :], AF.Sigmoid,
                          bias=bcols[:, 0, e8:e8 + 1],
                          after=[mm_tok[0], tok_tmp_free, tok_pass2], inc="sS")
                ef = p.op("scalar", "activation", fb[:, e8, :], pb[1][:, :], AF.Sigmoid,
                          bias=bcols[:, 1, e8:e8 + 1],
                          after=[mm_tok[1], tok_pass2], inc="sS")
                eo = p.op("scalar", "activation", ob[:, e8, :], pb[2][:, :], AF.Sigmoid,
                          bias=bcols[:, 2, e8:e8 + 1],
                          after=[mm_tok[2], tok_pass2], inc="sS")
                es = p.op("scalar", "activation", tmp_s, pb[3][:, :], AF.Tanh,
                          bias=bcols[:, 3, e8:e8 + 1],
                          after=[mm_tok[3], tok_tmp_free, tok_pass2], inc="sS")
                bankfree[0], bankfree[1], bankfree[2], bankfree[3] = ei, ef, eo, es
                ea = p.op("vector", "tensor_tensor", ab[:, e8, :], tmp_i, tmp_s,
                          op=ALU.mult, after=[ei, es, tok_P_spill], inc="vS")
                tok_tmp_free = ea

            tok_allmm = p.tok("tS")
            tok_acts = p.tok("sS")
            for e8 in range(NCH):
                init_h = 0.0 if g == 0 else ccol[:, e8, 0:1]
                sh = p.op("vector", "tensor_tensor_scan", hb, fb[:, e8, :],
                          ab[:, e8, :], init_h, op0=ALU.mult, op1=ALU.add,
                          after=[tok_acts], inc="vS")
                p.op("vector", "tensor_copy", ccol[:, e8, 0:1], hb[:, GT - 1:GT],
                     after=[sh], inc="vS")
                ty = p.op("vector", "tensor_tensor", hb, ob[:, e8, :], hb,
                          op=ALU.mult, inc="vS")
                p.op("vector", "tensor_tensor", xlnT[:, e8, :], xlnT[:, e8, :], hb,
                     op=ALU.add, after=[ty, tok_allmm], inc="vS")
                init_p = 1.0 if g == 0 else ccol[:, e8, 1:2]
                sp_ = p.op("vector", "tensor_tensor_scan", Pb[:, e8, :], fb[:, e8, :],
                           zbuf, init_p, op0=ALU.mult, op1=ALU.add,
                           after=[tok_zbuf, tok_P_spill], inc="vS")
                p.op("vector", "tensor_copy", ccol[:, e8, 1:2],
                     Pb[:, e8, GT - 1:GT], after=[sp_], inc="vS")
                p.op("vector", "tensor_tensor", Pb[:, e8, :], ob[:, e8, :],
                     Pb[:, e8, :], op=ALU.mult, inc="vS")
            tok_pass2 = p.tok("vS")
            tok_xlnT_spill = p.dma("gpsimd", zl_v[:, :, g * GT:(g + 1) * GT],
                                   xlnT[:, :, :], "qG", after=[tok_pass2])
            tok_P_spill = p.dma("gpsimd", c_v[:, :, g * GT:(g + 1) * GT],
                                Pb[:, :, :], "qG", after=[tok_pass2])

        tcb = p.dma("gpsimd", carry_bin[:, :],
                    ccol.rearrange("p c a -> p (c a)"), "qG",
                    after=[tok_pass2])
        cc1 = p.op("gpsimd", "collective", "AllGather", ALU.bypass,
                   [carry_bin], [carry_out], after=[tcb], inc="cc")
        tld = p.dma("sync", carr, carry_v, "qL", after=[cc1])
        zc = p.op("vector", "memset", hina[:, :, :], 0.0, inc="vS")
        prev = zc
        for e8 in range(NCH):
            for cb in (0, 4):
                for c in range(cb + 1, cb + 4):
                    prev = p.op("vector", "scalar_tensor_tensor",
                                hina[:, e8, c:c + 1], hina[:, e8, c - 1:c],
                                carr[:, c - 1, 2 * e8 + 1:2 * e8 + 2],
                                carr[:, c - 1, 2 * e8:2 * e8 + 1],
                                op0=ALU.mult, op1=ALU.add,
                                after=[tld, prev], inc="vS")
        for e8 in range(NCH):
            tm = p.op("vector", "tensor_tensor", hina[:, e8, :], hina[:, e8, :],
                      onehot[:, :], op=ALU.mult, after=[prev, tok_consts], inc="vS")
            p.op("vector", "tensor_reduce", hincol[:, e8:e8 + 1], hina[:, e8, :],
                 axis=AX.X, op=ALU.add, after=[tm], inc="vS")

        A.reset()
        p.barrier()
        for i in range(8):
            bankfree[i] = None

        if upto == "zloc":
            td = p.dma("gpsimd", out_d.rearrange("a b -> (a b)"),
                       zl_sp.rearrange("a b -> (a b)"), "qG",
                       after=[p.tok("qG"), p.tok("vS")])
            p.op("gpsimd", "wait_ge", "qG", p.ctr["qG"])
            emit_replay(nc, ctx, p, sems)
            return nc

        zsl = A.alloc(NCH, GT)
        csl = A.alloc(NCH, GT)
        ztm = A.alloc(D)
        x0st = A.alloc(2, D)
        x1lT = A.alloc(NCH, GT)
        v1t = A.alloc(NCH, 2, 128)
        u2t = A.alloc(NCH, 2, 128)
        ptq = A.alloc(4, GT)
        htq = A.alloc(2, GT)
        tmp5 = A.alloc(2, GT)
        gbuf = A.alloc(2 * D)
        b2buf = A.alloc(D)
        u1s = A.alloc(NCH, RANK)
        v2s = A.alloc(2, D)
        bn = A.alloc(16)

        lg = p.dma("sync", gbuf, gb_d[:, :], "qL")
        lb2 = p.dma("sync", b2buf, b2bc_d[:, :], "qL")
        lu1 = p.dma("sync", u1s, u1_d.rearrange("(c p) r -> p c r", p=128), "qL")
        lv2 = p.dma("sync", v2s, v2_d.rearrange("(r p) d -> p r d", p=128), "qL")
        tok_bconsts = p.tok("qL")

        tok_zsl_free = None
        tok_csl_free = None
        tok_x0_free = [None, None]
        tok_v1_free = None
        tok_u2_free = None
        tok_htq_free = [None, None]
        tok_tmp5_free = [None, None]
        tok_ptq_free = None

        do_ffn = upto != "x1"

        for g in range(NG):
            lz = p.dma("scalar", zsl[:, :, :], zl_v[:, :, g * GT:(g + 1) * GT],
                       "qA", after=[tok_zsl_free,
                                    tok_xlnT_spill if g == 0 else None])
            lc = p.dma("scalar", csl[:, :, :], c_v[:, :, g * GT:(g + 1) * GT],
                       "qA", after=[tok_csl_free,
                                    tok_P_spill if g == 0 else None])
            zctoks = []
            for e8 in range(NCH):
                zctoks.append(p.op(
                    "vector", "scalar_tensor_tensor", zsl[:, e8, :], csl[:, e8, :],
                    hincol[:, e8:e8 + 1], zsl[:, e8, :], op0=ALU.mult, op1=ALU.add,
                    after=[lz, lc] if e8 == 0 else [], inc="vS"))
            tok_csl_free = zctoks[-1]
            for c in range(4):
                gc_ = g * 4 + c
                par = c % 2
                evs = []
                for dc in range(NCH):
                    bk = 6 + dc % 2
                    tt_ = p.op("tensor", "transpose", pb[bk][:, 0:128],
                               zsl[:, dc, c * 128:(c + 1) * 128], ident[:, :],
                               after=[zctoks[dc], bankfree[bk]], inc="tS")
                    ev = p.op("vector", "tensor_copy", ztm[:, dc * 128:(dc + 1) * 128],
                              pb[bk][:, 0:128], after=[tt_], inc="vS")
                    bankfree[bk] = ev
                    evs.append(ev)
                if c == 3:
                    tok_zsl_free = tt_
                ln2 = emit_ln(ztm, ztm, after=[evs[-1]])
                tgm = p.op("vector", "tensor_tensor", ztm, ztm, gbuf[:, 0:D],
                           op=ALU.mult, after=[ln2, tok_bconsts], inc="vS")
                tbt = p.op("vector", "tensor_tensor", ztm, ztm, gbuf[:, D:2 * D],
                           op=ALU.add, after=[tgm], inc="vS")
                lx0 = p.dma("sync", x0st[:, par, :],
                            x0_d[gc_ * 128:(gc_ + 1) * 128, :], "qL",
                            after=[tok_x0_free[par]])
                tx1 = p.op("vector", "tensor_tensor", slab[:, gc_, :], ztm,
                           x0st[:, par, :], op=ALU.add, after=[tbt, lx0], inc="vS")
                tok_x0_free[par] = tx1
                if not do_ffn:
                    continue
                ln3 = emit_ln(slab[:, gc_, :], ztm, after=[tx1])
                for dc in range(NCH):
                    bk = 6 + dc % 2
                    tt_ = p.op("tensor", "transpose", pb[bk][:, 0:128],
                               ztm[:, dc * 128:(dc + 1) * 128], ident[:, :],
                               after=[ln3, bankfree[bk]], inc="tS")
                    ev = p.op("vector", "tensor_copy",
                              x1lT[:, dc, c * 128:(c + 1) * 128], pb[bk][:, 0:128],
                              after=[tt_], inc="vS")
                    bankfree[bk] = ev
            if not do_ffn:
                continue
            tok_x1lT = p.tok("vS")
            evP = []
            for rc in range(2):
                mm = None
                for dc in range(NCH):
                    aft = [tok_x1lT, tok_bconsts, bankfree[rc]] if dc == 0 else []
                    mm = p.op("tensor", "matmul", pb[rc][:, :],
                              u1s[:, dc, rc * 128:(rc + 1) * 128], x1lT[:, dc, :],
                              start=(dc == 0), stop=(dc == NCH - 1), after=aft,
                              inc="tS" if dc == NCH - 1 else None)
                evP.append(p.op("vector", "tensor_copy", ptq[:, rc, :], pb[rc][:, :],
                                after=[mm, tok_ptq_free], inc="vS"))
                bankfree[rc] = evP[-1]
            for fq in range(4):
                lv1 = p.dma("sync", v1t[:, :, :, :],
                            v1_d[:, fq * 8:(fq + 1) * 8, :, :], "qL",
                            after=[tok_v1_free])
                lu2 = p.dma("sync", u2t[:, :, :, :],
                            u2_d[:, fq * 8:(fq + 1) * 8, :, :], "qL",
                            after=[tok_u2_free])
                for fc in range(NCH):
                    fg = fq * 8 + fc
                    bkH = 2 + fc % 2
                    mmH = None
                    for rc in range(2):
                        aft = [lv1, evP[0], evP[1], bankfree[bkH]] if rc == 0 else []
                        mmH = p.op("tensor", "matmul", pb[bkH][:, :],
                                   v1t[:, fc, rc, :], ptq[:, rc, :],
                                   start=(rc == 0), stop=(rc == 1), after=aft,
                                   inc="tS" if rc == 1 else None)
                    eh = p.op("scalar", "activation", htq[:, fc % 2, :], pb[bkH][:, :],
                              AF.Gelu_apprx_tanh, bias=b1cols[:, fg:fg + 1],
                              after=[mmH, tok_htq_free[fc % 2]], inc="sS")
                    bankfree[bkH] = eh
                    mmQ = None
                    for rc in range(2):
                        aft = [eh, lu2]
                        if fg == 0:
                            aft += [bankfree[4 + rc]]
                        mmQ = p.op("tensor", "matmul", pb[4 + rc][:, :],
                                   u2t[:, fc, rc, :], htq[:, fc % 2, :],
                                   start=(fg == 0), stop=(fg == 31), after=aft,
                                   inc="tS" if rc == 1 else None)
                    tok_htq_free[fc % 2] = mmQ
                tok_v1_free = mmH
                tok_u2_free = mmQ
            tq = p.tok("tS")
            evQ = [p.op("vector", "tensor_copy", ptq[:, 2 + rc, :], pb[4 + rc][:, :],
                        after=[tq], inc="vS") for rc in range(2)]
            bankfree[4], bankfree[5] = evQ[0], evQ[1]
            tok_ptq_free = None
            for tc in range(4):
                gc_ = g * 4 + tc
                for dh in range(2):
                    bk = [0, 1, 6, 7][(tc * 2 + dh) % 4]
                    mm4 = None
                    for rc in range(2):
                        aft = [evQ[0], evQ[1], bankfree[bk], tok_bconsts] \
                            if rc == 0 else []
                        mm4 = p.op("tensor", "matmul", pb[bk][:, :],
                                   ptq[:, 2 + rc, tc * 128:(tc + 1) * 128],
                                   v2s[:, rc, dh * 512:(dh + 1) * 512],
                                   start=(rc == 0), stop=(rc == 1), after=aft,
                                   inc="tS" if rc == 1 else None)
                    tpar = (tc * 2 + dh) % 2
                    t1 = p.op("vector", "tensor_tensor", tmp5[:, tpar, :],
                              pb[bk][:, :], b2buf[:, dh * 512:(dh + 1) * 512],
                              op=ALU.add, after=[mm4, tok_tmp5_free[tpar]], inc="vS")
                    bankfree[bk] = t1
                    t2 = p.op("vector", "tensor_tensor",
                              slab[:, gc_, dh * 512:(dh + 1) * 512],
                              slab[:, gc_, dh * 512:(dh + 1) * 512],
                              tmp5[:, tpar, :], op=ALU.add, after=[t1], inc="vS")
                    tok_tmp5_free[tpar] = t2
                    tok_ptq_free = mm4

        A.reset()
        p.barrier()
        for i in range(8):
            bankfree[i] = None

        if upto in ("x1", "x2"):
            emit_dump_slab(p, out_d, slab)
            emit_replay(nc, ctx, p, sems)
            return nc

        emit_phase_c(p, A, nc, locals(), upto)

        if upto in ("qkv", "att"):
            emit_dump_slab(p, out_d, slab)
            emit_replay(nc, ctx, p, sems)
            return nc
        if upto in ("ydump", "qkdump", "vdump", "prefdump", "vbindump",
                    "voutdump"):
            emit_replay(nc, ctx, p, sems)
            return nc

        A.reset()
        p.barrier()
        for i in range(8):
            bankfree[i] = None

        if upto == "x3":
            emit_dump_slab(p, out_d, slab)
            emit_replay(nc, ctx, p, sems)
            return nc

        emit_phase_d(p, A, nc, locals())

        emit_replay(nc, ctx, p, sems)
    return nc


def emit_dump_slab(p, out_d, slab):
    td = p.dma("gpsimd", out_d.rearrange("(a p) d -> p a d", p=128),
               slab[:, :, :], "qG", after=[p.tok("vS"), p.tok("tS"), p.tok("sS")])
    p.op("gpsimd", "wait_ge", "qG", p.ctr["qG"])


def emit_replay(nc, ctx, p, sems):
    blk = ctx.enter_context(nc.Block())
    for eng in ENGINES:
        getattr(blk, eng)(lambda h, e=eng: p.replay(e, h, sems))


def emit_phase_c(p, A, nc, env, upto="full"):
    g = lambda k: env[k]
    pb, slab, ident, bankfree = g("pb"), g("slab"), g("ident"), g("bankfree")
    epscol, onescol = g("epscol"), g("onescol")
    wqT_d, wkT_d, wvT_d, woT_d = g("wqT_d"), g("wkT_d"), g("wvT_d"), g("woT_d")
    qk_bin, qk_out = g("qk_bin"), g("qk_out")
    v_bin, v_out = g("v_bin"), g("v_out")
    o_bin, o_out = g("o_bin"), g("o_out")
    masks_d = g("masks_d")
    tok_consts = g("tok_consts")
    tok_vconsts = g("tok_vconsts")

    bn = A.alloc(16)
    xlnT = A.alloc(NCH, TPC)
    wqt = A.alloc(NCH, NCH, 128)
    stg = A.alloc(2, GT)

    def emit_ln(src, dst, after=()):
        b1 = p.op("vector", "bn_stats", bn[:, 0:6], src[:, 0:512],
                  after=list(after), inc="vS")
        b2 = p.op("vector", "bn_stats", bn[:, 6:12], src[:, 512:1024], inc="vS")
        ag = p.op("vector", "bn_aggr", bn[:, 12:14], bn[:, 0:12],
                  after=[b1, b2], inc="vS")
        sq = p.op("scalar", "activation", bn[:, 14:15], bn[:, 13:14], AF.Sqrt,
                  bias=epscol, after=[ag], inc="sS")
        rs = p.op("vector", "reciprocal", bn[:, 15:16], bn[:, 14:15],
                  after=[sq], inc="vS")
        return p.op("vector", "tensor_scalar", dst, src, bn[:, 12:13],
                    bn[:, 15:16], op0=ALU.subtract, op1=ALU.mult,
                    after=[rs], inc="vS")

    xtmp = A.alloc(D)
    for ch in range(16):
        ln = emit_ln(slab[:, ch, :], xtmp)
        for dc in range(NCH):
            bk = 6 + dc % 2
            tt_ = p.op("tensor", "transpose", pb[bk][:, 0:128],
                       xtmp[:, dc * 128:(dc + 1) * 128], ident[:, :],
                       after=[ln, bankfree[bk]], inc="tS")
            ev = p.op("vector", "tensor_copy",
                      xlnT[:, dc, ch * 128:(ch + 1) * 128], pb[bk][:, 0:128],
                      after=[tt_], inc="vS")
            bankfree[bk] = ev
    tok_xlnT = p.tok("vS")

    tok_wqt_free = None
    tok_stg_free = [None, None]

    def proj_fm(w_d, qk_slot, scale):
        nonlocal tok_wqt_free
        lw = p.dma("sync", wqt, w_d[:, :, :, :], "qL", after=[tok_wqt_free])
        last_mm = None
        for ic in range(NCH):
            for tg in range(4):
                bk = (ic * 4 + tg) % 2
                mm = None
                for dc in range(NCH):
                    aft = [lw, tok_xlnT, bankfree[bk]] if dc == 0 else []
                    mm = p.op("tensor", "matmul", pb[bk][:, :],
                              wqt[:, ic, dc, :],
                              xlnT[:, dc, tg * GT:(tg + 1) * GT],
                              start=(dc == 0), stop=(dc == NCH - 1), after=aft,
                              inc="tS" if dc == NCH - 1 else None)
                spar = (ic * 4 + tg) % 2
                ev = p.op("scalar", "activation", stg[:, spar, :], pb[bk][:, :],
                          AF.Copy, scale=scale,
                          after=[mm, tok_stg_free[spar]], inc="sS")
                bankfree[bk] = ev
                st = p.dma("gpsimd",
                           qk_bin[ic, qk_slot, :, tg * GT:(tg + 1) * GT],
                           stg[:, spar, :], "qG", after=[ev])
                tok_stg_free[spar] = st
                last_mm = mm
        tok_wqt_free = last_mm

    proj_fm(wqT_d, 0, 0.125)
    proj_fm(wkT_d, 1, 1.0)

    wvt = wqt
    wvt2 = wvt.rearrange("p a b c -> p a (b c)")
    lwv = p.dma("sync", wvt2, wvT_d.rearrange("(c p) i -> p c i", p=128), "qL",
                after=[tok_wqt_free])
    for ts in range(16):
        for ih in range(2):
            bk = (ts * 2 + ih) % 2
            mm = None
            for dc in range(NCH):
                aft = [lwv, tok_xlnT, bankfree[bk]] if dc == 0 else []
                mm = p.op("tensor", "matmul", pb[bk][:, :],
                          xlnT[:, dc, ts * 128:(ts + 1) * 128],
                          wvt2[:, dc, ih * 512:(ih + 1) * 512],
                          start=(dc == 0), stop=(dc == NCH - 1), after=aft,
                          inc="tS" if dc == NCH - 1 else None)
            spar = (ts * 2 + ih) % 2
            ev = p.op("scalar", "activation", stg[:, spar, :], pb[bk][:, :],
                      AF.Copy, after=[mm, tok_stg_free[spar]], inc="sS")
            bankfree[bk] = ev
            st = None
            for vs in range(4):
                st = p.dma("gpsimd",
                           v_bin[ih * 4 + vs, ts * 128:(ts + 1) * 128, :],
                           stg[:, spar, vs * 128:(vs + 1) * 128], "qG",
                           after=[ev])
            tok_stg_free[spar] = st

    if upto == "vbindump":
        out_d = g("out_d")
        for s_ in range(8):
            p.dma("gpsimd", out_d[:, s_ * 128:(s_ + 1) * 128],
                  v_bin[s_, :, :], "qG", after=[p.tok("qG")])
        p.op("gpsimd", "wait_ge", "qG", p.ctr["qG"])
        return

    a2a_qk = p.op("gpsimd", "collective", "AllToAll", ALU.bypass,
                  [qk_bin], [qk_out], after=[p.tok("qG")], inc="cc")
    a2a_v = p.op("gpsimd", "collective", "AllToAll", ALU.bypass,
                 [v_bin], [v_out], after=[a2a_qk], inc="cc")

    if upto == "qkv":
        p.op("gpsimd", "wait_ge", "cc", p.ctr["cc"])
        return

    if upto == "voutdump":
        out_d = g("out_d")
        for s_ in range(8):
            p.dma("gpsimd", out_d[:, s_ * 128:(s_ + 1) * 128],
                  v_out[s_, :, :], "qG", after=[a2a_v])
        p.op("gpsimd", "wait_ge", "qG", p.ctr["qG"])
        return

    A.reset()
    bn = A.alloc(16)
    masks = A.alloc(4, 512)
    QT2 = A.alloc(S)
    KT2 = A.alloc(S)
    vbuf = A.alloc(64, 130)
    absb = A.alloc(2, 2, 512)
    osb = A.alloc(2, 512)
    otm = A.alloc(2, 4, 64)
    pdif = A.alloc(2, 16)
    pref = A.alloc(2, 16)
    rzc = A.alloc(4)

    tok_masks = p.dma("sync", masks.rearrange("p a b -> p (a b)"),
                      masks_d[:, :], "qL", after=[p.tok("tS")])
    mz = p.op("vector", "memset", osb.rearrange("p a b -> p (a b)"), 0.0, inc="vS")
    tok_osb_zero = mz

    NQB = S // GT
    tok_qt_free = None
    tok_vb_free = None
    tok_otm_free = [None, None]
    abfree = {}
    osb_free = {}
    tslot_free = {}

    for bsw in range(2):
        lq = p.dma("sync", QT2.rearrange("p (s t) -> p s t", s=4),
                   qk_out[4 * bsw:4 * bsw + 4, 0, :, :].rearrange("s p t -> p s t"),
                   "qL", after=[a2a_qk, tok_qt_free])
        lk = p.dma("sync", KT2.rearrange("p (s t) -> p s t", s=4),
                   qk_out[4 * bsw:4 * bsw + 4, 1, :, :].rearrange("s p t -> p s t"),
                   "qL", after=[])
        lvs = []
        for h in range(2):
            for src in range(4):
                lvs.append(p.dma(
                    "scalar", vbuf[:, src * 16:(src + 1) * 16, 65 * h:65 * h + 64],
                    v_out[4 * bsw + src, :, h * 64:(h + 1) * 64]
                    .rearrange("(k p) d -> p k d", p=128), "qA",
                    after=[a2a_v, tok_vb_free]))
        m1 = p.op("vector", "memset", vbuf[:, :, 64:65], 1.0,
                  after=[lvs[-1]], inc="vS")
        m2 = p.op("vector", "memset", vbuf[:, :, 129:130], 1.0, inc="vS")
        tok_vb = m2

        if upto == "qkdump" and bsw == 0:
            out_d = g("out_d")
            p.dma("gpsimd",
                  out_d[0:1024, :].rearrange("(a p) c -> p a c", p=128),
                  QT2.rearrange("p (a c) -> p a c", c=1024), "qG",
                  after=[lq, lk])
            p.dma("gpsimd",
                  out_d[1024:2048, :].rearrange("(a p) c -> p a c", p=128),
                  KT2.rearrange("p (a c) -> p a c", c=1024), "qG",
                  after=[lk])
            p.op("gpsimd", "wait_ge", "qG", p.ctr["qG"])
            return
        if upto == "vdump" and bsw == 0:
            out_d = g("out_d")
            vfl = vbuf.rearrange("p a b -> p (a b)")
            p.dma("gpsimd",
                  out_d[0:1024, :].rearrange("(a p) c -> p a c", p=128),
                  vfl[:, 0:8192].rearrange("p (a c) -> p a c", c=1024), "qG",
                  after=[tok_vb])
            p.dma("gpsimd", out_d[1024:1152, 0:128], vfl[:, 8192:8320],
                  "qG", after=[tok_vb])
            p.op("gpsimd", "wait_ge", "qG", p.ctr["qG"])
            return

        tok_sS_pre = p.tok("sS")
        for h in (() if "prefix" in K_SKIP() else range(2)):
            for qb in range(NQB):
                for j in range(4):
                    kt = qb * 4 + j
                    aft = [tok_vb, tok_vconsts] if j == 0 else []
                    if qb == 0 and j == 0:
                        aft.append(bankfree[5])
                    p.op("tensor", "matmul", pb[5][0:65, qb:qb + 1],
                         vbuf[:, kt, 65 * h:65 * h + 65], onescol[:, :],
                         start=(j == 0), stop=(j == 3), after=aft,
                         inc="tS" if j == 3 else None)
            ev = p.op("vector", "tensor_copy", pdif[0:65, h, :], pb[5][0:65, 0:16],
                      after=[p.tok("tS")], inc="vS")
            bankfree[5] = ev
            pz = p.op("vector", "memset", pref[:, h, :], 0.0,
                      after=[tok_sS_pre], inc="vS")
            prev = pz
            for qb in range(1, NQB):
                prev = p.op("vector", "tensor_tensor", pref[0:65, h, qb:qb + 1],
                            pref[0:65, h, qb - 1:qb], pdif[0:65, h, qb - 1:qb],
                            op=ALU.add, after=[ev, prev], inc="vS")
        tok_pref = p.tok("vS")

        if upto == "prefdump" and bsw == 0:
            out_d = g("out_d")
            p.dma("gpsimd", out_d[0:128, 0:32],
                  pref.rearrange("p a b -> p (a b)"), "qG", after=[tok_pref])
            p.dma("gpsimd", out_d[0:128, 32:64],
                  pdif.rearrange("p a b -> p (a b)"), "qG", after=[tok_pref])
            p.op("gpsimd", "wait_ge", "qG", p.ctr["qG"])
            return

        for qb in range(NQB):
            nkt = qb * 4 + 4
            for kt in range(nkt):
                sA = kt % 2
                sB = 2 + kt % 2
                mmS = []
                for h, bk in ((0, sA), (1, sB)):
                    mm = p.op("tensor", "matmul", pb[bk][:, :],
                              KT2[64 * h:64 * h + 64, kt * 128:(kt + 1) * 128],
                              QT2[64 * h:64 * h + 64, qb * GT:(qb + 1) * GT],
                              start=True, stop=True,
                              tile_position=(64 * h, 0) if h else None,
                              after=[lq, lk, bankfree[bk]], inc="tS")
                    mmS.append(mm)
                diag = kt >= qb * 4
                abtoks = []
                for h, bk in (() if "sq" in K_SKIP() else ((0, sA), (1, sB))):
                    par = kt % 2
                    ev = p.op("scalar", "activation", absb[:, h, par, :],
                              pb[bk][:, :], AF.Square, bias=1.0,
                              after=[mmS[h], abfree.get((h, par))], inc="sS")
                    bankfree[bk] = ev
                    if diag and "diag" not in K_SKIP():
                        ev = p.op("vector", "scalar_tensor_tensor",
                                  absb[:, h, par, :], absb[:, h, par, :], 1.0,
                                  masks[:, kt - qb * 4, :], op0=ALU.add,
                                  op1=ALU.mult, after=[ev, tok_masks], inc="vS")
                    abtoks.append(ev)
                for h, obk in (() if "oacc" in K_SKIP() else ((0, 4), (1, 6))):
                    par = kt % 2
                    aft = [abtoks[h], tok_vb]
                    if kt == 0:
                        aft.append(bankfree[obk])
                    mmO = p.op("tensor", "matmul", pb[obk][0:65, :],
                               vbuf[:, kt, 65 * h:65 * h + 65], absb[:, h, par, :],
                               start=(kt == 0), stop=(kt == nkt - 1), after=aft,
                               inc="tS" if kt == nkt - 1 else None)
                    abfree[(h, par)] = mmO
            tok_av = p.tok("tS")
            for h, obk in (() if ("dance" in K_SKIP() or "oacc" in K_SKIP())
                           else ((0, 4), (1, 6))):
                evO = p.op("vector", "tensor_scalar", osb[0:65, h, :],
                           pb[obk][0:65, :], pref[0:65, h, qb:qb + 1], None,
                           op0=ALU.add,
                           after=[tok_av, tok_pref, tok_osb_zero,
                                  osb_free.get(h)], inc="vS")
                bankfree[obk] = evO
                dv = None
                tt_ = None
                if "tr" in K_SKIP():
                    osb_free[h] = evO
                    continue
                for qs in range(4):
                    sl = (h * 4 + qs) % 4
                    slot = pb[7][:, 128 * sl:128 * sl + 128]
                    tt_ = p.op("tensor", "transpose", slot,
                               osb[:, h, qs * 128:(qs + 1) * 128], ident[:, :],
                               after=[evO, tslot_free.get(sl), p.tok("vS")],
                               inc="tS")
                    if "div" in K_SKIP():
                        tslot_free[sl] = tt_
                        dv = None
                        continue
                    r1 = p.op("vector", "tensor_scalar", rzc[:, qs:qs + 1],
                              slot[:, 64:65], ATT_EPS2, None, op0=ALU.add,
                              after=[tt_], inc="vS")
                    r2 = p.op("vector", "reciprocal", rzc[:, qs:qs + 1],
                              rzc[:, qs:qs + 1], after=[r1], inc="vS")
                    dv = p.op("vector", "tensor_scalar", otm[:, h, qs, :],
                              slot[:, 0:64], rzc[:, qs:qs + 1], None,
                              op0=ALU.mult, after=[r2, tok_otm_free[h]], inc="vS")
                    tslot_free[sl] = dv
                osb_free[h] = tt_
                if "div" in K_SKIP() or "store" in K_SKIP():
                    continue
                dest = 4 * bsw + qb // 4
                r0 = (qb % 4) * 512
                st = p.dma("gpsimd", o_bin[dest, r0:r0 + 512, h * 64:(h + 1) * 64]
                           .rearrange("(a p) d -> p a d", p=128),
                           otm[:, h, :, :], "qG", after=[dv])
                tok_otm_free[h] = st
        tok_qt_free = p.tok("tS")
        tok_vb_free = p.tok("tS")

    a2a_o = p.op("gpsimd", "collective", "AllToAll", ALU.bypass,
                 [o_bin], [o_out], after=[p.tok("qG")], inc="cc")

    if upto == "att":
        p.op("gpsimd", "wait_ge", "cc", p.ctr["cc"])
        return

    if upto == "ydump":
        out_d = g("out_d")
        for s_ in range(8):
            p.dma("gpsimd", out_d.rearrange("t (s d) -> t s d", s=8)[:, s_, :],
                  o_out[s_, :, :], "qG", after=[a2a_o])
        p.op("gpsimd", "wait_ge", "qG", p.ctr["qG"])
        return

    A.reset()
    bn = A.alloc(16)
    ych = A.alloc(2, D)
    ytT = A.alloc(NCH, GT)
    wot = A.alloc(NCH, D)
    lwo = p.dma("sync", wot, woT_d[:, :, :], "qL", after=[p.tok("tS")])
    tok_ych_free = [None, None]
    tok_ytT_free = None
    for tg in range(4):
        evs = None
        for c in range(4):
            ch = tg * 4 + c
            par = c % 2
            ly = p.dma("sync", ych[:, par, :].rearrange("p (s d) -> p s d", s=8),
                       o_out[:, ch * 128:(ch + 1) * 128, :]
                       .rearrange("s p d -> p s d"),
                       "qL", after=[a2a_o, tok_ych_free[par]])
            for jc in range(NCH):
                bk = 6 + jc % 2
                tt_ = p.op("tensor", "transpose", pb[bk][:, 0:128],
                           ych[:, par, jc * 128:(jc + 1) * 128], ident[:, :],
                           after=[ly, bankfree[bk], tok_ytT_free], inc="tS")
                ev = p.op("vector", "tensor_copy",
                          ytT[:, jc, c * 128:(c + 1) * 128], pb[bk][:, 0:128],
                          after=[tt_], inc="vS")
                bankfree[bk] = ev
            tok_ych_free[par] = tt_
            evs = ev
        tok_ytT = evs
        last_mm = None
        for tcl in range(4):
            ch = tg * 4 + tcl
            for dh in range(2):
                bk = (tcl * 2 + dh) % 4
                mm = None
                for jc in range(NCH):
                    aft = [lwo, tok_ytT, bankfree[bk]] if jc == 0 else []
                    mm = p.op("tensor", "matmul", pb[bk][:, :],
                              ytT[:, jc, tcl * 128:(tcl + 1) * 128],
                              wot[:, jc, dh * 512:(dh + 1) * 512],
                              start=(jc == 0), stop=(jc == NCH - 1), after=aft,
                              inc="tS" if jc == NCH - 1 else None)
                t2 = p.op("vector", "tensor_tensor",
                          slab[:, ch, dh * 512:(dh + 1) * 512],
                          slab[:, ch, dh * 512:(dh + 1) * 512],
                          pb[bk][:, :], op=ALU.add, after=[mm], inc="vS")
                bankfree[bk] = t2
                last_mm = mm
        tok_ytT_free = last_mm


def emit_phase_d(p, A, nc, env):
    g = lambda k: env[k]
    pb, slab, ident, bankfree = g("pb"), g("slab"), g("ident"), g("bankfree")
    epscol, onesrow = g("epscol"), g("onesrow")
    ccell = g("ccell")
    t128, srow, scol = g("t128"), g("srow"), g("scol")
    iar_t = g("iar_t")
    out_d = g("out_d")
    wupT_d, wgtT_d, wdnT_d = g("wupT_d"), g("wgtT_d"), g("wdnT_d")
    up_sp, gt_sp = g("up_sp"), g("gt_sp")
    upq_sp, g2_sp, o_sp = g("upq_sp"), g("g2_sp"), g("o_sp")
    ar_bin, ar_out = g("ar_bin"), g("ar_out")

    bn = A.alloc(16)
    xqT = A.alloc(NCH, TPC)
    xtmp = A.alloc(D)
    mx = A.alloc(4)
    sv = A.alloc(8)

    def emit_ln(src, dst, after=()):
        b1 = p.op("vector", "bn_stats", bn[:, 0:6], src[:, 0:512],
                  after=list(after), inc="vS")
        b2 = p.op("vector", "bn_stats", bn[:, 6:12], src[:, 512:1024], inc="vS")
        ag = p.op("vector", "bn_aggr", bn[:, 12:14], bn[:, 0:12],
                  after=[b1, b2], inc="vS")
        sq = p.op("scalar", "activation", bn[:, 14:15], bn[:, 13:14], AF.Sqrt,
                  bias=epscol, after=[ag], inc="sS")
        rs = p.op("vector", "reciprocal", bn[:, 15:16], bn[:, 14:15],
                  after=[sq], inc="vS")
        return p.op("vector", "tensor_scalar", dst, src, bn[:, 12:13],
                    bn[:, 15:16], op0=ALU.subtract, op1=ALU.mult,
                    after=[rs], inc="vS")

    mz = p.op("vector", "memset", mx[:, :], 0.0, inc="vS")
    for ch in range(16):
        ln = emit_ln(slab[:, ch, :], xtmp)
        rd = p.op("vector", "tensor_reduce", mx[:, 1:2], xtmp, axis=AX.X,
                  op=ALU.max, apply_absolute_value=True, after=[ln, mz], inc="vS")
        p.op("vector", "tensor_tensor", mx[:, 0:1], mx[:, 0:1], mx[:, 1:2],
             op=ALU.max, after=[rd], inc="vS")
        for dc in range(NCH):
            bk = 6 + dc % 2
            tt_ = p.op("tensor", "transpose", pb[bk][:, 0:128],
                       xtmp[:, dc * 128:(dc + 1) * 128], ident[:, :],
                       after=[ln, bankfree[bk]], inc="tS")
            ev = p.op("vector", "tensor_copy",
                      xqT[:, dc, ch * 128:(ch + 1) * 128], pb[bk][:, 0:128],
                      after=[tt_], inc="vS")
            bankfree[bk] = ev

    def global_max_to_scalars(cols, bin_i, nvals, after=()):
        ts_ = list(after)
        for i, cap in enumerate(cols):
            ts_.append(p.op("vector", "tensor_copy", t128[:, i:i + 1], cap,
                            inc="vS"))
        tt_ = p.op("tensor", "transpose", pb[5][:, 0:128], t128[:, :], ident[:, :],
                   after=ts_ + [bankfree[5]], inc="tS")
        rd = p.op("vector", "tensor_reduce", scol[0:4, 0:1],
                  pb[5][0:4, 0:128], axis=AX.X, op=ALU.max,
                  after=[tt_], inc="vS")
        bankfree[5] = rd
        std = p.dma("gpsimd", ar_bin[bin_i].rearrange("a b -> b a"),
                    scol[0:4, 0:1], "qG", after=[rd])
        cc_ = p.op("gpsimd", "collective", "AllReduce", ALU.max,
                   [ar_bin[bin_i]], [ar_out[bin_i]], after=[std], inc="cc")
        lda = p.dma("sync", srow[0:1, 8:12], ar_out[bin_i][0:1, 0:4], "qL",
                    after=[cc_])
        outs = []
        for i in range(nvals):
            s1 = p.op("vector", "tensor_scalar", srow[0:1, i:i + 1],
                      srow[0:1, 8 + i:9 + i], ccell[0:1, 4:5], 1e-12,
                      op0=ALU.mult, op1=ALU.max, after=[lda], inc="vS")
            s2 = p.op("vector", "reciprocal", srow[0:1, 4 + i:5 + i],
                      srow[0:1, i:i + 1], after=[s1], inc="vS")
            outs.append(s2)
        tt2 = p.op("tensor", "matmul", pb[5][:, 0:8], onesrow[0:1, :],
                   srow[0:1, 0:8], start=True, stop=True,
                   after=outs + [bankfree[5]], inc="tS")
        evb = p.op("vector", "tensor_copy", sv[:, 0:8], pb[5][:, 0:8],
                   after=[tt2], inc="vS")
        bankfree[5] = evb
        return evb

    tok_sx = global_max_to_scalars([mx[:, 0:1]], 0, 1, after=[p.tok("vS")])

    if g("upto") == "svdump":
        p.dma("gpsimd", out_d[0:128, 0:8], sv[:, 0:8], "qG", after=[tok_sx])
        p.dma("gpsimd", out_d[0:128, 8:12], mx[:, 0:4], "qG", after=[tok_sx])
        p.dma("gpsimd", out_d[0:1, 16:32], srow[0:1, :], "qG", after=[tok_sx])
        p.dma("gpsimd", out_d[0:128, 32:48], scol[:, :], "qG", after=[tok_sx])
        p.op("gpsimd", "wait_ge", "qG", p.ctr["qG"])
        return

    qtoks = []
    for dc in range(NCH):
        v1 = p.op("vector", "tensor_scalar", xqT[:, dc, :], xqT[:, dc, :],
                  sv[:, 4:5], None, op0=ALU.mult, after=[tok_sx], inc="vS")
        cst = p.op("scalar", "activation", iar_t[:, 0:TPC], xqT[:, dc, :],
                   AF.Copy, after=[v1, qtoks[-1] if qtoks else None], inc="sS")
        cst2 = p.op("scalar", "activation", xqT[:, dc, :], iar_t[:, 0:TPC],
                    AF.Copy, after=[cst], inc="sS")
        v2 = p.op("vector", "tensor_scalar", xqT[:, dc, :], xqT[:, dc, :],
                  sv[:, 0:1], None, op0=ALU.mult, after=[cst2], inc="vS")
        qtoks.append(v2)
    tok_xq = qtoks[-1]

    wtl = A.alloc(2, NCH, 128)
    ustg = A.alloc(2, GT)
    mz2 = p.op("vector", "memset", mx[:, 0:2], 0.0, after=[p.tok("vS")], inc="vS")
    tok_wtl_free = [None, None]
    tok_ustg_free = [None, None]
    for mi, (wT_d, sp_d) in enumerate(((wupT_d, up_sp), (wgtT_d, gt_sp))):
        for ic in range(NIC):
            wpar = ic % 2
            lw = p.dma("sync", wtl[:, wpar, :, :], wT_d[ic, :, :, :], "qL",
                       after=[tok_wtl_free[wpar], g("tok_deq")])
            for tg in range(4):
                bk = (ic * 4 + tg) % 4
                mm = None
                for dc in range(NCH):
                    aft = [lw, tok_xq, bankfree[bk]] if dc == 0 else []
                    mm = p.op("tensor", "matmul", pb[bk][:, :],
                              wtl[:, wpar, dc, :],
                              xqT[:, dc, tg * GT:(tg + 1) * GT],
                              start=(dc == 0), stop=(dc == NCH - 1), after=aft,
                              inc="tS" if dc == NCH - 1 else None)
                spar = (ic * 4 + tg) % 2
                ev = p.op("vector", "tensor_copy", ustg[:, spar, :], pb[bk][:, :],
                          after=[mm, tok_ustg_free[spar]], inc="vS")
                bankfree[bk] = ev
                rd = p.op("vector", "tensor_reduce", mx[:, 2:3], ustg[:, spar, :],
                          axis=AX.X, op=ALU.max, apply_absolute_value=True,
                          after=[ev, mz2], inc="vS")
                p.op("vector", "tensor_tensor", mx[:, mi:mi + 1], mx[:, mi:mi + 1],
                     mx[:, 2:3], op=ALU.max, after=[rd], inc="vS")
                st = p.dma("gpsimd", sp_d[ic, :, tg * GT:(tg + 1) * GT],
                           ustg[:, spar, :], "qG", after=[ev])
                tok_ustg_free[spar] = st
            tok_wtl_free[wpar] = mm
    tok_d2 = p.tok("qG")

    tok_sug = global_max_to_scalars([mx[:, 0:1], mx[:, 1:2]], 1, 2,
                                    after=[p.tok("vS")])

    if g("upto") == "svdump2":
        p.dma("gpsimd", out_d[0:128, 0:8], sv[:, 0:8], "qG", after=[tok_sug])
        p.dma("gpsimd", out_d[0:128, 8:12], mx[:, 0:4], "qG", after=[tok_sug])
        p.dma("gpsimd", out_d[0:1, 16:32], srow[0:1, :], "qG", after=[tok_sug])
        p.dma("gpsimd", out_d[0:128, 32:160],
              wupT_d[0, :, 0, :], "qG", after=[tok_sug])
        p.dma("gpsimd", out_d[0:128, 160:288],
              up_sp[0, :, 0:128], "qG", after=[tok_sug])
        p.op("gpsimd", "wait_ge", "qG", p.ctr["qG"])
        return

    A2 = A
    utile = A.alloc(2, GT)
    gtile = A.alloc(2, GT)
    w1 = A.alloc(GT)
    w2 = A.alloc(GT)
    mz3 = p.op("vector", "memset", mx[:, 2:3], 0.0, after=[tok_sug], inc="vS")
    tok_ut_free = [None, None]
    tok_gt_free = [None, None]
    prev_cast = None
    prev_sig = None
    tok_w_free = None
    for ti in range(NIC * 4):
        ic, tg = divmod(ti, 4)
        par = ti % 2
        lu = p.dma("sync", utile[:, par, :], up_sp[ic, :, tg * GT:(tg + 1) * GT],
                   "qL", after=[tok_ut_free[par], tok_d2])
        lg_ = p.dma("scalar", gtile[:, par, :], gt_sp[ic, :, tg * GT:(tg + 1) * GT],
                    "qA", after=[tok_gt_free[par], tok_d2])
        v1 = p.op("vector", "tensor_scalar", utile[:, par, :], utile[:, par, :],
                  sv[:, 4:5], None, op0=ALU.mult, after=[lu, tok_sug], inc="vS")
        c1 = p.op("scalar", "activation", iar_t[:, 0:GT], utile[:, par, :], AF.Copy,
                  after=[v1, prev_cast], inc="sS")
        c1b = p.op("scalar", "activation", utile[:, par, :], iar_t[:, 0:GT],
                   AF.Copy, after=[c1], inc="sS")
        v2 = p.op("vector", "tensor_scalar", utile[:, par, :], utile[:, par, :],
                  sv[:, 0:1], None, op0=ALU.mult, after=[c1b], inc="vS")
        s1 = p.dma("gpsimd", upq_sp[ic, :, tg * GT:(tg + 1) * GT], utile[:, par, :],
                   "qG", after=[v2])
        tok_ut_free[par] = s1
        v3 = p.op("vector", "tensor_scalar", gtile[:, par, :], gtile[:, par, :],
                  sv[:, 5:6], None, op0=ALU.mult, after=[lg_, tok_sug], inc="vS")
        c2 = p.op("scalar", "activation", iar_t[:, 512:512 + GT], gtile[:, par, :],
                  AF.Copy, after=[v3, c1], inc="sS")
        c2b = p.op("scalar", "activation", gtile[:, par, :],
                   iar_t[:, 512:512 + GT], AF.Copy, after=[c2], inc="sS")
        v4 = p.op("vector", "tensor_scalar", gtile[:, par, :],
                  gtile[:, par, :], sv[:, 1:2], None, op0=ALU.mult,
                  after=[c2b], inc="vS")
        sg = p.op("scalar", "activation", w1, gtile[:, par, :], AF.Sigmoid,
                  after=[v4, prev_sig, tok_w_free], inc="sS")
        v5 = p.op("vector", "tensor_scalar", w2, w1, 65536.0, None,
                  op0=ALU.mult, after=[sg, tok_w_free], inc="vS")
        c3 = p.op("scalar", "activation", iar_t[:, 1024:1024 + GT], w2, AF.Copy,
                  after=[v5], inc="sS")
        c3b = p.op("scalar", "activation", w1, iar_t[:, 1024:1024 + GT],
                   AF.Copy, after=[c3], inc="sS")
        v6 = p.op("vector", "tensor_scalar", w1, w1,
                  65535.0, 1.0 / 65536.0, op0=ALU.min, op1=ALU.mult,
                  after=[c3b], inc="vS")
        prev_sig = v6
        v7 = p.op("vector", "tensor_tensor", gtile[:, par, :], gtile[:, par, :],
                  w1, op=ALU.mult, after=[v6], inc="vS")
        tok_w_free = v7
        rd = p.op("vector", "tensor_reduce", mx[:, 3:4], gtile[:, par, :],
                  axis=AX.X, op=ALU.max, apply_absolute_value=True,
                  after=[v7, mz3], inc="vS")
        p.op("vector", "tensor_tensor", mx[:, 2:3], mx[:, 2:3], mx[:, 3:4],
             op=ALU.max, after=[rd], inc="vS")
        s2 = p.dma("gpsimd", g2_sp[ic, :, tg * GT:(tg + 1) * GT], gtile[:, par, :],
                   "qG", after=[v7])
        tok_gt_free[par] = s2
        prev_cast = c3b

    tok_d4 = p.tok("qG")
    tok_sg2 = global_max_to_scalars([mx[:, 2:3]], 2, 1, after=[p.tok("vS")])

    mz4 = p.op("vector", "memset", mx[:, 0:1], 0.0, after=[tok_sg2], inc="vS")
    prev_cast = None
    for ti in range(NIC * 4):
        ic, tg = divmod(ti, 4)
        par = ti % 2
        lgq = p.dma("sync", gtile[:, par, :], g2_sp[ic, :, tg * GT:(tg + 1) * GT],
                    "qL", after=[tok_gt_free[par], tok_d4])
        luq = p.dma("scalar", utile[:, par, :], upq_sp[ic, :, tg * GT:(tg + 1) * GT],
                    "qA", after=[tok_ut_free[par], tok_d4])
        v1 = p.op("vector", "tensor_scalar", gtile[:, par, :], gtile[:, par, :],
                  sv[:, 4:5], None, op0=ALU.mult, after=[lgq, tok_sg2], inc="vS")
        c1 = p.op("scalar", "activation", iar_t[:, 0:GT], gtile[:, par, :], AF.Copy,
                  after=[v1, prev_cast], inc="sS")
        c1b = p.op("scalar", "activation", gtile[:, par, :], iar_t[:, 0:GT],
                   AF.Copy, after=[c1], inc="sS")
        v2 = p.op("vector", "tensor_scalar", gtile[:, par, :], gtile[:, par, :],
                  sv[:, 0:1], None, op0=ALU.mult, after=[c1b], inc="vS")
        prev_cast = c1b
        v3 = p.op("vector", "tensor_tensor", gtile[:, par, :], gtile[:, par, :],
                  utile[:, par, :], op=ALU.mult, after=[v2, luq], inc="vS")
        tok_ut_free[par] = v3
        rd = p.op("vector", "tensor_reduce", mx[:, 1:2], gtile[:, par, :],
                  axis=AX.X, op=ALU.max, apply_absolute_value=True,
                  after=[v3, mz4], inc="vS")
        p.op("vector", "tensor_tensor", mx[:, 0:1], mx[:, 0:1], mx[:, 1:2],
             op=ALU.max, after=[rd], inc="vS")
        st = p.dma("gpsimd", o_sp[ic, :, tg * GT:(tg + 1) * GT], gtile[:, par, :],
                   "qG", after=[v3])
        tok_gt_free[par] = st

    tok_d6 = p.tok("qG")
    tok_so = global_max_to_scalars([mx[:, 0:1]], 3, 1, after=[p.tok("vS")])

    if g("upto") == "odump":
        for ic_ in range(16):
            p.dma("gpsimd", out_d[ic_ * 128:(ic_ + 1) * 128, :]
                  .rearrange("p (a d) -> p a d", d=128),
                  o_sp[ic_, :, 0:1024].rearrange("p (a d) -> p a d", d=128),
                  "qG", after=[tok_so])
        p.op("gpsimd", "wait_ge", "qG", p.ctr["qG"])
        return

    if g("upto") == "svdump3":
        p.dma("gpsimd", out_d[0:128, 0:8], sv[:, 0:8], "qG", after=[tok_so])
        p.dma("gpsimd", out_d[0:128, 8:72], upq_sp[0, :, 0:64], "qG",
              after=[tok_so])
        p.dma("gpsimd", out_d[0:128, 72:136], g2_sp[0, :, 0:64], "qG",
              after=[tok_so])
        p.dma("gpsimd", out_d[0:128, 136:200], o_sp[0, :, 0:64], "qG",
              after=[tok_so])
        p.op("gpsimd", "wait_ge", "qG", p.ctr["qG"])
        return

    wdr = A.alloc(2, D)
    oqt = A.alloc(2, GT)
    tok_wdr_free = [None, None]
    tok_oqt_free = [None, None]
    prev_cast = None
    tok_out = None
    for tg in range(4):
        mmfin = {}
        for ic in range(NIC):
            par = ic % 2
            lo = p.dma("sync", oqt[:, par, :], o_sp[ic, :, tg * GT:(tg + 1) * GT],
                       "qL", after=[tok_oqt_free[par], tok_d6])
            lw = p.dma("scalar", wdr[:, par, :], wdnT_d[ic, :, :], "qA",
                       after=[tok_wdr_free[par], tok_d6])
            v1 = p.op("vector", "tensor_scalar", oqt[:, par, :], oqt[:, par, :],
                      sv[:, 4:5], None, op0=ALU.mult, after=[lo, tok_so], inc="vS")
            c1 = p.op("scalar", "activation", iar_t[:, 0:GT], oqt[:, par, :],
                      AF.Copy, after=[v1, prev_cast], inc="sS")
            c1b = p.op("scalar", "activation", oqt[:, par, :], iar_t[:, 0:GT],
                       AF.Copy, after=[c1], inc="sS")
            v2 = p.op("vector", "tensor_scalar", oqt[:, par, :], oqt[:, par, :],
                      sv[:, 0:1], None, op0=ALU.mult, after=[c1b], inc="vS")
            prev_cast = c1b
            for ts in range(4):
                for dh in range(2):
                    bk = ts * 2 + dh
                    aft = [v2, lw]
                    if ic == 0:
                        aft.append(bankfree[bk])
                    mmfin[bk] = p.op(
                        "tensor", "matmul", pb[bk][:, :],
                        oqt[:, par, ts * 128:(ts + 1) * 128],
                        wdr[:, par, dh * 512:(dh + 1) * 512],
                        start=(ic == 0), stop=(ic == NIC - 1), after=aft,
                        inc="tS" if (ic == NIC - 1 or (ts == 3 and dh == 1))
                        else None)
            tok_oqt_free[par] = mmfin[7]
            tok_wdr_free[par] = mmfin[7]
        for ts in range(4):
            ch = tg * 4 + ts
            for dh in range(2):
                bk = ts * 2 + dh
                t2 = p.op("vector", "tensor_tensor",
                          slab[:, ch, dh * 512:(dh + 1) * 512],
                          slab[:, ch, dh * 512:(dh + 1) * 512],
                          pb[bk][:, :], op=ALU.add, after=[mmfin[bk]], inc="vS")
                bankfree[bk] = t2
            tok_out = p.dma("gpsimd", out_d[ch * 128:(ch + 1) * 128, :],
                            slab[:, ch, :], "qG", after=[p.tok("vS")])
    p.op("gpsimd", "wait_ge", "qG", p.ctr["qG"])


def _mk_masks():
    m = np.zeros((128, 4, 512), np.float32)
    tk = np.arange(128)[:, None]
    tq = np.arange(512)[None, :]
    for j in range(4):
        m[:, j, :] = (128 * j + tk <= tq).astype(np.float32)
    return np.ascontiguousarray(m.reshape(128, 4 * 512))


def _blk_gate(W):
    return np.ascontiguousarray(
        np.asarray(W, np.float32).reshape(8, 128, 8, 128).transpose(1, 2, 0, 3))


def _blk_T(W):
    return np.ascontiguousarray(
        np.asarray(W, np.float32).reshape(8, 128, 8, 128).transpose(3, 0, 2, 1))


def _prep_inputs(inputs):
    f = lambda k: np.asarray(inputs[k], np.float32)
    x = np.ascontiguousarray(f("L0_attn_x").reshape(B * S, D))
    gamma, beta = f("L0_attn_gamma"), f("L0_attn_beta")
    gb = np.ascontiguousarray(
        np.concatenate([np.tile(gamma, (128, 1)), np.tile(beta, (128, 1))], axis=1))
    bcols = np.ascontiguousarray(
        np.stack([f(f"L0_attn_b{k}").reshape(8, 128).T for k in "ifos"],
                 axis=1).reshape(128, 32))
    common = dict(
        Wi_b=_blk_gate(f("L0_attn_Wi")), Wf_b=_blk_gate(f("L0_attn_Wf")),
        Wog_b=_blk_gate(f("L0_attn_Wo")), Ws_b=_blk_gate(f("L0_attn_Ws")),
        bcols=bcols, gb_bc=gb,
        U1=np.ascontiguousarray(f("L0_ffn_U1")),
        V1_b=np.ascontiguousarray(
            f("L0_ffn_V1").reshape(2, 128, 32, 128).transpose(1, 2, 0, 3)),
        b1cols=np.ascontiguousarray(f("L0_ffn_b1").reshape(32, 128).T),
        U2_b=np.ascontiguousarray(
            f("L0_ffn_U2").reshape(32, 128, 2, 128).transpose(1, 0, 2, 3)),
        V2=np.ascontiguousarray(f("L0_ffn_V2")),
        b2_bc=np.ascontiguousarray(np.tile(f("L0_ffn_b2"), (128, 1))),
        WqT_b=_blk_T(f("L1_attn_Wq")), WkT_b=_blk_T(f("L1_attn_Wk")),
        WvT=np.ascontiguousarray(f("L1_attn_Wv").T),
        WoT_b=np.ascontiguousarray(
            f("L1_attn_Wo").T.reshape(8, 128, 1024).transpose(1, 0, 2)),
        masks=_mk_masks(),
        ident_in=np.eye(128, dtype=np.float32),
        wup=np.ascontiguousarray(f("L1_ffn_w_up")),
        wgt=np.ascontiguousarray(f("L1_ffn_w_gate")),
        wdn=np.ascontiguousarray(f("L1_ffn_w_down")),
    )
    in_maps = []
    for c in range(NCORE):
        oh = np.zeros((128, NCORE), np.float32)
        oh[:, c] = 1.0
        m = dict(common)
        m["x0"] = np.ascontiguousarray(x[c * TPC:(c + 1) * TPC])
        m["onehot"] = oh
        in_maps.append(m)
    return in_maps


_UPTO = "full"
_ATT_SKIP = set()


def kernel(**inputs):
    in_maps = _prep_inputs(inputs)
    nc = build(upto=_UPTO)
    res = run_bass_kernel_spmd(nc, in_maps, core_ids=list(range(NCORE)))
    out = np.concatenate([res.results[c]["out"] for c in range(NCORE)], axis=0)
    return out.reshape(B, S, D)


# version 15 (final)
# speedup vs baseline: 1.0555x; 1.0011x over previous
import numpy as np
from contextlib import ExitStack

import concourse.bass as bass
import concourse.mybir as mybir
from concourse.bass_utils import run_bass_kernel_spmd
from concourse.aot_env import allow_ici_all_to_all

F32 = mybir.dt.float32
I32 = mybir.dt.int32
AF = mybir.ActivationFunctionType
ALU = mybir.AluOpType
AX = mybir.AxisListType

NCORE = 8
B, S, D = 2, 8192, 1024
TPC = 2048
NG, GT = 4, 512
NCH = 8
DFF, RANK, IFF = 4096, 256, 3072
NIC = IFF // 128
LN_EPS = 1e-5
ATT_EPS2 = 2e-6
HEADS, HD = 16, 64

ENGINES = ("sync", "tensor", "scalar", "vector", "gpsimd")


def K_SKIP():
    return globals().get("_ATT_SKIP", set())
SEM_NAMES = ["tS", "sS", "vS", "gS", "qL", "qA", "qG", "qV", "cc", "bar"]


class Plan:
    def __init__(self):
        self.ops = {e: [] for e in ENGINES}
        self.ctr = {}

    def tok(self, sem):
        return (sem, self.ctr.get(sem, 0))

    def op(self, eng, meth, *args, after=(), inc=None, incn=1, **kw):
        waits = {}
        for t in after:
            if t is None:
                continue
            s, v = t
            if v > 0:
                waits[s] = max(waits.get(s, 0), v)
        if inc is not None:
            self.ctr[inc] = self.ctr.get(inc, 0) + incn
        self.ops[eng].append((meth, args, kw, sorted(waits.items()), inc, incn))
        return (inc, self.ctr[inc]) if inc is not None else None

    def dma(self, eng, out, in_, sem, after=()):
        return self.op(eng, "dma_start", after=after, inc=sem, incn=16, out=out, in_=in_)

    def barrier(self):
        self.ctr["bar"] = self.ctr.get("bar", 0) + len(ENGINES)
        tgt = self.ctr["bar"]
        for e in ENGINES:
            self.ops[e].append(("sem_inc", ("bar", 1), {}, [], None, None))
            self.ops[e].append(("wait_ge", ("bar", tgt), {}, [], None, None))
        return ("bar", tgt)

    def replay(self, eng, h, sems):
        for meth, args, kw, waits, inc, incn in self.ops[eng]:
            for s, v in waits:
                h.wait_ge(sems[s], v)
            if meth == "sem_inc":
                h.sem_inc(sems[args[0]], args[1])
                continue
            if meth == "wait_ge":
                h.wait_ge(sems[args[0]], args[1])
                continue
            if meth == "collective":
                kind, op_, ins_, outs_ = args
                with allow_ici_all_to_all():
                    inst = h.collective_compute(
                        kind, op_, replica_groups=[list(range(NCORE))],
                        ins=ins_, outs=outs_)
            else:
                inst = getattr(h, meth)(*args, **kw)
            if inc is not None:
                inst.then_inc(sems[inc], incn)


class Arena:

    def __init__(self, ap, total):
        self.ap = ap
        self.total = total
        self.off = 0

    def alloc(self, *shape):
        n = int(np.prod(shape))
        assert self.off + n <= self.total, (self.off, n, self.total)
        v = self.ap[:, self.off:self.off + n]
        self.off += n
        if len(shape) > 1:
            names = " ".join(f"a{i}" for i in range(len(shape)))
            v = v.rearrange(f"p ({names}) -> p {names}",
                            **{f"a{i}": s for i, s in enumerate(shape)})
        return v

    def reset(self):
        self.off = 0


def build(upto="full"):
    nc = bass.Bass("TRN2", target_bir_lowering=False, debug=False,
                   detect_race_conditions=False)

    def din(name, shape):
        return nc.dram_tensor(name, list(shape), F32, kind="ExternalInput").ap()

    x0_d = din("x0", (TPC, D))
    wg_d = [din(n, (128, NCH, NCH, 128)) for n in ("Wi_b", "Wf_b", "Wog_b", "Ws_b")]
    bcols_d = din("bcols", (128, 4 * NCH))
    gb_d = din("gb_bc", (128, 2 * D))
    u1_d = din("U1", (D, RANK))
    v1_d = din("V1_b", (128, 32, 2, 128))
    b1c_d = din("b1cols", (128, 32))
    u2_d = din("U2_b", (128, 32, 2, 128))
    v2_d = din("V2", (RANK, D))
    b2bc_d = din("b2_bc", (128, D))
    wqT_d = din("WqT_b", (128, NCH, NCH, 128))
    wkT_d = din("WkT_b", (128, NCH, NCH, 128))
    wvT_d = din("WvT", (D, D))
    woT_d = din("WoT_b", (128, NCH, D))
    masks_d = din("masks", (128, 4 * 512))
    onehot_d = din("onehot", (128, NCORE))
    ident_d = din("ident_in", (128, 128))
    wup_d = din("wup", (IFF, D))
    wgt_d = din("wgt", (IFF, D))
    wdn_d = din("wdn", (D, IFF))

    out_d = nc.dram_tensor("out", [TPC, D], F32, kind="ExternalOutput").ap()

    def dint(name, shape):
        return nc.dram_tensor(name, list(shape), F32).ap()

    zl_sp = dint("zl_sp", (D, TPC))
    c_sp = dint("c_sp", (D, TPC))
    carry_bin = dint("carry_bin", (128, 16))
    carry_out = dint("carry_out", (NCORE * 128, 16))
    qk_bin = dint("qk_bin", (NCORE, 2, 128, TPC))
    qk_out = dint("qk_out", (NCORE, 2, 128, TPC))
    v_bin = dint("v_bin", (NCORE, TPC, 128))
    v_out = dint("v_out", (NCORE, TPC, 128))
    o_bin = dint("o_bin", (NCORE, TPC, 128))
    o_out = dint("o_out", (NCORE, TPC, 128))
    wupT_d = dint("wupT", (NIC, 128, NCH, 128))
    wgtT_d = dint("wgtT", (NIC, 128, NCH, 128))
    wdnT_d = dint("wdnT", (NIC, 128, D))
    up_sp = dint("up_sp", (NIC, 128, TPC))
    gt_sp = dint("gt_sp", (NIC, 128, TPC))
    upq_sp = dint("upq_sp", (NIC, 128, TPC))
    g2_sp = dint("g2_sp", (NIC, 128, TPC))
    o_sp = dint("o_sp", (NIC, 128, TPC))
    ar_bin = [dint(f"ar{i}_bin", (1, 4)) for i in range(4)]
    ar_out = [dint(f"ar{i}_out", (1, 4)) for i in range(4)]

    zl_v = zl_sp.rearrange("(c p) t -> p c t", p=128)
    c_v = c_sp.rearrange("(c p) t -> p c t", p=128)
    carry_v = carry_out.rearrange("(c p) k -> p c k", p=128)

    p = Plan()

    with ExitStack() as ctx:
        sems = {s: ctx.enter_context(nc.semaphore(s)) for s in SEM_NAMES}
        pb = [ctx.enter_context(nc.psum_tensor(f"pb{i}", [128, 512], F32))
              for i in range(8)]
        slab_t = ctx.enter_context(nc.sbuf_tensor("slab", [128, 16 * D], F32))
        slab = slab_t[:, :].rearrange("p (a d) -> p a d", a=16)
        AR_COLS = 32000
        arena_t = ctx.enter_context(nc.sbuf_tensor("arena", [128, AR_COLS], F32))
        A = Arena(arena_t[:, :], AR_COLS)
        iar_t = ctx.enter_context(nc.sbuf_tensor("iar", [128, 3072], I32))
        ident = ctx.enter_context(nc.sbuf_tensor("ident_s", [128, 128], F32))
        bcols = ctx.enter_context(nc.sbuf_tensor("bcols_s", [128, 4, NCH], F32))
        b1cols = ctx.enter_context(nc.sbuf_tensor("b1cols_s", [128, 32], F32))
        onehot = ctx.enter_context(nc.sbuf_tensor("onehot_s", [128, NCORE], F32))
        hincol = ctx.enter_context(nc.sbuf_tensor("hincol", [128, NCH], F32))
        ccell = ctx.enter_context(nc.sbuf_tensor("ccell", [128, 8], F32))
        onescol = ctx.enter_context(nc.sbuf_tensor("onescol", [128, 1], F32))
        onesrow = ctx.enter_context(nc.sbuf_tensor("onesrow", [1, 128], F32))
        t128 = ctx.enter_context(nc.sbuf_tensor("t128", [128, 128], F32))
        srow = ctx.enter_context(nc.sbuf_tensor("srow", [1, 16], F32))
        scol = ctx.enter_context(nc.sbuf_tensor("scol", [128, 16], F32))

        epscol = ccell[:, 0:1]

        bankfree = {i: None for i in range(8)}

        for dst, src in ((ident[:, :], ident_d[:, :]),
                         (bcols[:, :, :].rearrange("p a c -> p (a c)"), bcols_d[:, :]),
                         (b1cols[:, :], b1c_d[:, :]),
                         (onehot[:, :], onehot_d[:, :])):
            p.dma("sync", dst, src, "qL")
        tok_consts = p.tok("qL")
        p.op("vector", "memset", ccell[:, :], 0.0)
        p.op("vector", "memset", epscol, LN_EPS)
        p.op("vector", "memset", ccell[:, 1:2], 7.0)
        p.op("vector", "reciprocal", ccell[:, 2:3], ccell[:, 1:2])
        p.op("vector", "memset", ccell[:, 3:4], 32767.0)
        p.op("vector", "reciprocal", ccell[:, 4:5], ccell[:, 3:4])
        p.op("vector", "memset", onescol[:, :], 1.0)
        p.op("vector", "memset", onesrow[:, :], 1.0)
        p.op("vector", "memset", t128[:, :], 0.0, inc="vS")
        tok_vconsts = p.tok("vS")

        def dequant(src_d, nrc, ccols, store_fn, tag):
            nb = ccols // 32
            ntc = ccols // 128
            wch = A.alloc(ccols)
            bm = A.alloc(nb)
            sS_ = A.alloc(nb)
            rr = A.alloc(nb)
            stg = A.alloc(2, 128)
            tok_w_free = p.tok("tS")
            tok_stg_free = [p.tok("qG"), p.tok("qG")]
            for rc in range(nrc):
                ld = p.dma("sync", wch,
                           src_d[rc * 128:(rc + 1) * 128, :], "qL",
                           after=[tok_w_free])
                w3 = wch.rearrange("p (nb e) -> p nb e", e=32)
                t1 = p.op("vector", "tensor_reduce",
                          bm.rearrange("p (a o) -> p a o", o=1), w3,
                          axis=AX.X, op=ALU.max, apply_absolute_value=True,
                          after=[ld], inc="vS")
                t2 = p.op("vector", "tensor_scalar", sS_, bm, ccell[:, 2:3],
                          1e-12, op0=ALU.mult, op1=ALU.max, after=[t1],
                          inc="vS")
                t3 = p.op("vector", "reciprocal", rr, sS_, after=[t2], inc="vS")
                rbc = rr.rearrange("p (a o) -> p a o", o=1).broadcast_to((128, nb, 32))
                sbc = sS_.rearrange("p (a o) -> p a o", o=1).broadcast_to((128, nb, 32))
                t4 = p.op("vector", "tensor_tensor", w3, w3, rbc, op=ALU.mult,
                          after=[t3], inc="vS")
                t5 = p.op("scalar", "activation", iar_t[:, 0:ccols], wch, AF.Copy,
                          after=[t4], inc="sS")
                t5b = p.op("scalar", "activation", wch, iar_t[:, 0:ccols],
                           AF.Copy, after=[t5], inc="sS")
                t6 = p.op("vector", "tensor_tensor", w3, w3, sbc,
                          op=ALU.mult, after=[t5b], inc="vS")
                last_ev = None
                for tc in range(ntc):
                    bk = tc % 2
                    tt_ = p.op("tensor", "transpose", pb[bk][:, 0:128],
                               wch[:, tc * 128:(tc + 1) * 128], ident[:, :],
                               after=[t6, bankfree[bk], tok_consts], inc="tS")
                    ev = p.op("vector", "tensor_copy", stg[:, bk, :], pb[bk][:, 0:128],
                              after=[tt_, tok_stg_free[bk]], inc="vS")
                    bankfree[bk] = ev
                    st = p.dma("gpsimd", store_fn(rc, tc), stg[:, bk, :], "qG",
                               after=[ev])
                    tok_stg_free[bk] = st
                    last_ev = tt_
                tok_w_free = last_ev
            A.reset()

        if upto in ("full", "x3up", "svdump2", "svdump3", "odump"):
            dequant(wup_d, NIC, D,
                    lambda rc, tc: wupT_d[rc, :, tc, :], "up")
            dequant(wgt_d, NIC, D,
                    lambda rc, tc: wgtT_d[rc, :, tc, :], "gt")
            dequant(wdn_d, NCH, IFF,
                    lambda rc, tc: wdnT_d[tc, :, rc * 128:(rc + 1) * 128], "dn")
        tok_deq = p.tok("qG")
        A.reset()

        p.barrier()
        for i in range(8):
            bankfree[i] = None

        xlnT = A.alloc(NCH, GT)
        fb = A.alloc(NCH, GT)
        ab = A.alloc(NCH, GT)
        ob = A.alloc(NCH, GT)
        Pb = A.alloc(NCH, GT)
        hb = A.alloc(GT)
        w4a = A.alloc(2, NCH, 128)
        w4b = A.alloc(2, NCH, 128)
        x0st = A.alloc(2, D)
        bn = A.alloc(16)
        ccol = A.alloc(NCH, 2)
        zbuf = A.alloc(GT)
        tmp_i = A.alloc(GT)
        tmp_s = A.alloc(GT)
        carr = A.alloc(NCORE, 16)
        hina = A.alloc(NCH, NCORE)

        p.op("vector", "memset", zbuf, 0.0, inc="vS")
        tok_zbuf = p.tok("vS")

        tok_x0_free = [None, None]
        tok_xlnT_spill = None
        tok_P_spill = None
        tok_w4a_free = None
        tok_w4b_free = None
        tok_tmp_free = None
        tok_pass2 = None

        def emit_ln(src, dst, after=()):
            b1 = p.op("vector", "bn_stats", bn[:, 0:6], src[:, 0:512],
                      after=list(after), inc="vS")
            b2 = p.op("vector", "bn_stats", bn[:, 6:12], src[:, 512:1024], inc="vS")
            ag = p.op("vector", "bn_aggr", bn[:, 12:14], bn[:, 0:12],
                      after=[b1, b2], inc="vS")
            sq = p.op("scalar", "activation", bn[:, 14:15], bn[:, 13:14], AF.Sqrt,
                      bias=epscol, after=[ag, tok_vconsts], inc="sS")
            rs = p.op("vector", "reciprocal", bn[:, 15:16], bn[:, 14:15],
                      after=[sq], inc="vS")
            return p.op("vector", "tensor_scalar", dst, src, bn[:, 12:13],
                        bn[:, 15:16], op0=ALU.subtract, op1=ALU.mult,
                        after=[rs], inc="vS")

        for g in range(NG):
            ev = None
            for c in range(4):
                par = c % 2
                ld = p.dma("sync", x0st[:, par, :],
                           x0_d[(g * 4 + c) * 128:(g * 4 + c + 1) * 128, :],
                           "qL", after=[tok_x0_free[par]])
                ln = emit_ln(x0st[:, par, :], x0st[:, par, :], after=[ld])
                for dc in range(NCH):
                    bk = 6 + dc % 2
                    tt_ = p.op("tensor", "transpose", pb[bk][:, 0:128],
                               x0st[:, par, dc * 128:(dc + 1) * 128], ident[:, :],
                               after=[ln, bankfree[bk], tok_consts], inc="tS")
                    ev = p.op("vector", "tensor_copy",
                              xlnT[:, dc, c * 128:(c + 1) * 128], pb[bk][:, 0:128],
                              after=[tt_, tok_xlnT_spill], inc="vS")
                    bankfree[bk] = ev
                tok_x0_free[par] = tt_
            tok_xlnT_ready = ev

            for e8 in range(NCH):
                tif = [p.dma("sync", w4a[:, k, :, :], wg_d[k][:, e8, :, :], "qL",
                             after=[tok_w4a_free, tok_consts]) for k in range(2)]
                mm_tok = [None] * 4
                for gi in range(2):
                    for dc in range(NCH):
                        aft = [tif[1], tok_xlnT_ready, bankfree[gi]] if dc == 0 else []
                        mm_tok[gi] = p.op("tensor", "matmul", pb[gi][:, :],
                                          w4a[:, gi, dc, :], xlnT[:, dc, :],
                                          start=(dc == 0), stop=(dc == NCH - 1),
                                          after=aft,
                                          inc="tS" if dc == NCH - 1 else None)
                tok_w4a_free = mm_tok[1]
                tos = [p.dma("sync", w4b[:, k, :, :], wg_d[2 + k][:, e8, :, :], "qL",
                             after=[tok_w4b_free]) for k in range(2)]
                for gi in range(2, 4):
                    for dc in range(NCH):
                        aft = [tos[1], tok_xlnT_ready, bankfree[gi]] if dc == 0 else []
                        mm_tok[gi] = p.op("tensor", "matmul", pb[gi][:, :],
                                          w4b[:, gi - 2, dc, :], xlnT[:, dc, :],
                                          start=(dc == 0), stop=(dc == NCH - 1),
                                          after=aft,
                                          inc="tS" if dc == NCH - 1 else None)
                tok_w4b_free = mm_tok[3]
                ei = p.op("scalar", "activation", tmp_i, pb[0][:, :], AF.Sigmoid,
                          bias=bcols[:, 0, e8:e8 + 1],
                          after=[mm_tok[0], tok_tmp_free, tok_pass2], inc="sS")
                ef = p.op("scalar", "activation", fb[:, e8, :], pb[1][:, :], AF.Sigmoid,
                          bias=bcols[:, 1, e8:e8 + 1],
                          after=[mm_tok[1], tok_pass2], inc="sS")
                eo = p.op("scalar", "activation", ob[:, e8, :], pb[2][:, :], AF.Sigmoid,
                          bias=bcols[:, 2, e8:e8 + 1],
                          after=[mm_tok[2], tok_pass2], inc="sS")
                es = p.op("scalar", "activation", tmp_s, pb[3][:, :], AF.Tanh,
                          bias=bcols[:, 3, e8:e8 + 1],
                          after=[mm_tok[3], tok_tmp_free, tok_pass2], inc="sS")
                bankfree[0], bankfree[1], bankfree[2], bankfree[3] = ei, ef, eo, es
                ea = p.op("vector", "tensor_tensor", ab[:, e8, :], tmp_i, tmp_s,
                          op=ALU.mult, after=[ei, es, tok_P_spill], inc="vS")
                tok_tmp_free = ea

            tok_allmm = p.tok("tS")
            tok_acts = p.tok("sS")
            for e8 in range(NCH):
                init_h = 0.0 if g == 0 else ccol[:, e8, 0:1]
                sh = p.op("vector", "tensor_tensor_scan", hb, fb[:, e8, :],
                          ab[:, e8, :], init_h, op0=ALU.mult, op1=ALU.add,
                          after=[tok_acts], inc="vS")
                p.op("vector", "tensor_copy", ccol[:, e8, 0:1], hb[:, GT - 1:GT],
                     after=[sh], inc="vS")
                ty = p.op("vector", "tensor_tensor", hb, ob[:, e8, :], hb,
                          op=ALU.mult, inc="vS")
                p.op("vector", "tensor_tensor", xlnT[:, e8, :], xlnT[:, e8, :], hb,
                     op=ALU.add, after=[ty, tok_allmm], inc="vS")
                init_p = 1.0 if g == 0 else ccol[:, e8, 1:2]
                sp_ = p.op("vector", "tensor_tensor_scan", Pb[:, e8, :], fb[:, e8, :],
                           zbuf, init_p, op0=ALU.mult, op1=ALU.add,
                           after=[tok_zbuf, tok_P_spill], inc="vS")
                p.op("vector", "tensor_copy", ccol[:, e8, 1:2],
                     Pb[:, e8, GT - 1:GT], after=[sp_], inc="vS")
                p.op("vector", "tensor_tensor", Pb[:, e8, :], ob[:, e8, :],
                     Pb[:, e8, :], op=ALU.mult, inc="vS")
            tok_pass2 = p.tok("vS")
            tok_xlnT_spill = p.dma("gpsimd", zl_v[:, :, g * GT:(g + 1) * GT],
                                   xlnT[:, :, :], "qG", after=[tok_pass2])
            tok_P_spill = p.dma("gpsimd", c_v[:, :, g * GT:(g + 1) * GT],
                                Pb[:, :, :], "qG", after=[tok_pass2])

        tcb = p.dma("gpsimd", carry_bin[:, :],
                    ccol.rearrange("p c a -> p (c a)"), "qG",
                    after=[tok_pass2])
        cc1 = p.op("gpsimd", "collective", "AllGather", ALU.bypass,
                   [carry_bin], [carry_out], after=[tcb], inc="cc")
        tld = p.dma("sync", carr, carry_v, "qL", after=[cc1])
        zc = p.op("vector", "memset", hina[:, :, :], 0.0, inc="vS")
        prev = zc
        for e8 in range(NCH):
            for cb in (0, 4):
                for c in range(cb + 1, cb + 4):
                    prev = p.op("vector", "scalar_tensor_tensor",
                                hina[:, e8, c:c + 1], hina[:, e8, c - 1:c],
                                carr[:, c - 1, 2 * e8 + 1:2 * e8 + 2],
                                carr[:, c - 1, 2 * e8:2 * e8 + 1],
                                op0=ALU.mult, op1=ALU.add,
                                after=[tld, prev], inc="vS")
        for e8 in range(NCH):
            tm = p.op("vector", "tensor_tensor", hina[:, e8, :], hina[:, e8, :],
                      onehot[:, :], op=ALU.mult, after=[prev, tok_consts], inc="vS")
            p.op("vector", "tensor_reduce", hincol[:, e8:e8 + 1], hina[:, e8, :],
                 axis=AX.X, op=ALU.add, after=[tm], inc="vS")

        A.reset()
        p.barrier()
        for i in range(8):
            bankfree[i] = None

        if upto == "zloc":
            td = p.dma("gpsimd", out_d.rearrange("a b -> (a b)"),
                       zl_sp.rearrange("a b -> (a b)"), "qG",
                       after=[p.tok("qG"), p.tok("vS")])
            p.op("gpsimd", "wait_ge", "qG", p.ctr["qG"])
            emit_replay(nc, ctx, p, sems)
            return nc

        zsl = A.alloc(NCH, GT)
        csl = A.alloc(NCH, GT)
        ztm = A.alloc(D)
        x0st = A.alloc(2, D)
        x1lT = A.alloc(NCH, GT)
        v1t = A.alloc(NCH, 2, 128)
        u2t = A.alloc(NCH, 2, 128)
        ptq = A.alloc(4, GT)
        htq = A.alloc(2, GT)
        tmp5 = A.alloc(2, GT)
        gbuf = A.alloc(2 * D)
        b2buf = A.alloc(D)
        u1s = A.alloc(NCH, RANK)
        v2s = A.alloc(2, D)
        bn = A.alloc(16)

        lg = p.dma("sync", gbuf, gb_d[:, :], "qL")
        lb2 = p.dma("sync", b2buf, b2bc_d[:, :], "qL")
        lu1 = p.dma("sync", u1s, u1_d.rearrange("(c p) r -> p c r", p=128), "qL")
        lv2 = p.dma("sync", v2s, v2_d.rearrange("(r p) d -> p r d", p=128), "qL")
        tok_bconsts = p.tok("qL")

        tok_zsl_free = None
        tok_csl_free = None
        tok_x0_free = [None, None]
        tok_v1_free = None
        tok_u2_free = None
        tok_htq_free = [None, None]
        tok_tmp5_free = [None, None]
        tok_ptq_free = None

        do_ffn = upto != "x1"

        for g in range(NG):
            lz = p.dma("scalar", zsl[:, :, :], zl_v[:, :, g * GT:(g + 1) * GT],
                       "qA", after=[tok_zsl_free,
                                    tok_xlnT_spill if g == 0 else None])
            lc = p.dma("scalar", csl[:, :, :], c_v[:, :, g * GT:(g + 1) * GT],
                       "qA", after=[tok_csl_free,
                                    tok_P_spill if g == 0 else None])
            zctoks = []
            for e8 in range(NCH):
                zctoks.append(p.op(
                    "vector", "scalar_tensor_tensor", zsl[:, e8, :], csl[:, e8, :],
                    hincol[:, e8:e8 + 1], zsl[:, e8, :], op0=ALU.mult, op1=ALU.add,
                    after=[lz, lc] if e8 == 0 else [], inc="vS"))
            tok_csl_free = zctoks[-1]
            for c in range(4):
                gc_ = g * 4 + c
                par = c % 2
                evs = []
                for dc in range(NCH):
                    bk = 6 + dc % 2
                    tt_ = p.op("tensor", "transpose", pb[bk][:, 0:128],
                               zsl[:, dc, c * 128:(c + 1) * 128], ident[:, :],
                               after=[zctoks[dc], bankfree[bk]], inc="tS")
                    ev = p.op("vector", "tensor_copy", ztm[:, dc * 128:(dc + 1) * 128],
                              pb[bk][:, 0:128], after=[tt_], inc="vS")
                    bankfree[bk] = ev
                    evs.append(ev)
                if c == 3:
                    tok_zsl_free = tt_
                ln2 = emit_ln(ztm, ztm, after=[evs[-1]])
                tgm = p.op("vector", "tensor_tensor", ztm, ztm, gbuf[:, 0:D],
                           op=ALU.mult, after=[ln2, tok_bconsts], inc="vS")
                tbt = p.op("vector", "tensor_tensor", ztm, ztm, gbuf[:, D:2 * D],
                           op=ALU.add, after=[tgm], inc="vS")
                lx0 = p.dma("sync", x0st[:, par, :],
                            x0_d[gc_ * 128:(gc_ + 1) * 128, :], "qL",
                            after=[tok_x0_free[par]])
                tx1 = p.op("vector", "tensor_tensor", slab[:, gc_, :], ztm,
                           x0st[:, par, :], op=ALU.add, after=[tbt, lx0], inc="vS")
                tok_x0_free[par] = tx1
                if not do_ffn:
                    continue
                ln3 = emit_ln(slab[:, gc_, :], ztm, after=[tx1])
                for dc in range(NCH):
                    bk = 6 + dc % 2
                    tt_ = p.op("tensor", "transpose", pb[bk][:, 0:128],
                               ztm[:, dc * 128:(dc + 1) * 128], ident[:, :],
                               after=[ln3, bankfree[bk]], inc="tS")
                    ev = p.op("vector", "tensor_copy",
                              x1lT[:, dc, c * 128:(c + 1) * 128], pb[bk][:, 0:128],
                              after=[tt_], inc="vS")
                    bankfree[bk] = ev
            if not do_ffn:
                continue
            tok_x1lT = p.tok("vS")
            evP = []
            for rc in range(2):
                mm = None
                for dc in range(NCH):
                    aft = [tok_x1lT, tok_bconsts, bankfree[rc]] if dc == 0 else []
                    mm = p.op("tensor", "matmul", pb[rc][:, :],
                              u1s[:, dc, rc * 128:(rc + 1) * 128], x1lT[:, dc, :],
                              start=(dc == 0), stop=(dc == NCH - 1), after=aft,
                              inc="tS" if dc == NCH - 1 else None)
                evP.append(p.op("vector", "tensor_copy", ptq[:, rc, :], pb[rc][:, :],
                                after=[mm, tok_ptq_free], inc="vS"))
                bankfree[rc] = evP[-1]
            for fq in range(4):
                lv1 = p.dma("sync", v1t[:, :, :, :],
                            v1_d[:, fq * 8:(fq + 1) * 8, :, :], "qL",
                            after=[tok_v1_free])
                lu2 = p.dma("sync", u2t[:, :, :, :],
                            u2_d[:, fq * 8:(fq + 1) * 8, :, :], "qL",
                            after=[tok_u2_free])
                for fc in range(NCH):
                    fg = fq * 8 + fc
                    bkH = 2 + fc % 2
                    mmH = None
                    for rc in range(2):
                        aft = [lv1, evP[0], evP[1], bankfree[bkH]] if rc == 0 else []
                        mmH = p.op("tensor", "matmul", pb[bkH][:, :],
                                   v1t[:, fc, rc, :], ptq[:, rc, :],
                                   start=(rc == 0), stop=(rc == 1), after=aft,
                                   inc="tS" if rc == 1 else None)
                    eh = p.op("scalar", "activation", htq[:, fc % 2, :], pb[bkH][:, :],
                              AF.Gelu_apprx_tanh, bias=b1cols[:, fg:fg + 1],
                              after=[mmH, tok_htq_free[fc % 2]], inc="sS")
                    bankfree[bkH] = eh
                    mmQ = None
                    for rc in range(2):
                        aft = [eh, lu2]
                        if fg == 0:
                            aft += [bankfree[4 + rc]]
                        mmQ = p.op("tensor", "matmul", pb[4 + rc][:, :],
                                   u2t[:, fc, rc, :], htq[:, fc % 2, :],
                                   start=(fg == 0), stop=(fg == 31), after=aft,
                                   inc="tS" if rc == 1 else None)
                    tok_htq_free[fc % 2] = mmQ
                tok_v1_free = mmH
                tok_u2_free = mmQ
            tq = p.tok("tS")
            evQ = [p.op("vector", "tensor_copy", ptq[:, 2 + rc, :], pb[4 + rc][:, :],
                        after=[tq], inc="vS") for rc in range(2)]
            bankfree[4], bankfree[5] = evQ[0], evQ[1]
            tok_ptq_free = None
            for tc in range(4):
                gc_ = g * 4 + tc
                for dh in range(2):
                    bk = [0, 1, 6, 7][(tc * 2 + dh) % 4]
                    mm4 = None
                    for rc in range(2):
                        aft = [evQ[0], evQ[1], bankfree[bk], tok_bconsts] \
                            if rc == 0 else []
                        mm4 = p.op("tensor", "matmul", pb[bk][:, :],
                                   ptq[:, 2 + rc, tc * 128:(tc + 1) * 128],
                                   v2s[:, rc, dh * 512:(dh + 1) * 512],
                                   start=(rc == 0), stop=(rc == 1), after=aft,
                                   inc="tS" if rc == 1 else None)
                    tpar = (tc * 2 + dh) % 2
                    t1 = p.op("vector", "tensor_tensor", tmp5[:, tpar, :],
                              pb[bk][:, :], b2buf[:, dh * 512:(dh + 1) * 512],
                              op=ALU.add, after=[mm4, tok_tmp5_free[tpar]], inc="vS")
                    bankfree[bk] = t1
                    t2 = p.op("vector", "tensor_tensor",
                              slab[:, gc_, dh * 512:(dh + 1) * 512],
                              slab[:, gc_, dh * 512:(dh + 1) * 512],
                              tmp5[:, tpar, :], op=ALU.add, after=[t1], inc="vS")
                    tok_tmp5_free[tpar] = t2
                    tok_ptq_free = mm4

        A.reset()
        p.barrier()
        for i in range(8):
            bankfree[i] = None

        if upto in ("x1", "x2"):
            emit_dump_slab(p, out_d, slab)
            emit_replay(nc, ctx, p, sems)
            return nc

        emit_phase_c(p, A, nc, locals(), upto)

        if upto in ("qkv", "att"):
            emit_dump_slab(p, out_d, slab)
            emit_replay(nc, ctx, p, sems)
            return nc
        if upto in ("ydump", "qkdump", "vdump", "prefdump", "vbindump",
                    "voutdump"):
            emit_replay(nc, ctx, p, sems)
            return nc

        A.reset()
        p.barrier()
        for i in range(8):
            bankfree[i] = None

        if upto == "x3":
            emit_dump_slab(p, out_d, slab)
            emit_replay(nc, ctx, p, sems)
            return nc

        emit_phase_d(p, A, nc, locals())

        emit_replay(nc, ctx, p, sems)
    return nc


def emit_dump_slab(p, out_d, slab):
    td = p.dma("gpsimd", out_d.rearrange("(a p) d -> p a d", p=128),
               slab[:, :, :], "qG", after=[p.tok("vS"), p.tok("tS"), p.tok("sS")])
    p.op("gpsimd", "wait_ge", "qG", p.ctr["qG"])


def emit_replay(nc, ctx, p, sems):
    blk = ctx.enter_context(nc.Block())
    for eng in ENGINES:
        getattr(blk, eng)(lambda h, e=eng: p.replay(e, h, sems))


def emit_phase_c(p, A, nc, env, upto="full"):
    g = lambda k: env[k]
    pb, slab, ident, bankfree = g("pb"), g("slab"), g("ident"), g("bankfree")
    epscol, onescol = g("epscol"), g("onescol")
    wqT_d, wkT_d, wvT_d, woT_d = g("wqT_d"), g("wkT_d"), g("wvT_d"), g("woT_d")
    qk_bin, qk_out = g("qk_bin"), g("qk_out")
    v_bin, v_out = g("v_bin"), g("v_out")
    o_bin, o_out = g("o_bin"), g("o_out")
    masks_d = g("masks_d")
    tok_consts = g("tok_consts")
    tok_vconsts = g("tok_vconsts")

    bn = A.alloc(16)
    xlnT = A.alloc(NCH, TPC)
    wqt = A.alloc(NCH, NCH, 128)
    stg = A.alloc(2, GT)

    def emit_ln(src, dst, after=()):
        b1 = p.op("vector", "bn_stats", bn[:, 0:6], src[:, 0:512],
                  after=list(after), inc="vS")
        b2 = p.op("vector", "bn_stats", bn[:, 6:12], src[:, 512:1024], inc="vS")
        ag = p.op("vector", "bn_aggr", bn[:, 12:14], bn[:, 0:12],
                  after=[b1, b2], inc="vS")
        sq = p.op("scalar", "activation", bn[:, 14:15], bn[:, 13:14], AF.Sqrt,
                  bias=epscol, after=[ag], inc="sS")
        rs = p.op("vector", "reciprocal", bn[:, 15:16], bn[:, 14:15],
                  after=[sq], inc="vS")
        return p.op("vector", "tensor_scalar", dst, src, bn[:, 12:13],
                    bn[:, 15:16], op0=ALU.subtract, op1=ALU.mult,
                    after=[rs], inc="vS")

    xtmp = A.alloc(D)
    for ch in range(16):
        ln = emit_ln(slab[:, ch, :], xtmp)
        for dc in range(NCH):
            bk = 6 + dc % 2
            tt_ = p.op("tensor", "transpose", pb[bk][:, 0:128],
                       xtmp[:, dc * 128:(dc + 1) * 128], ident[:, :],
                       after=[ln, bankfree[bk]], inc="tS")
            ev = p.op("vector", "tensor_copy",
                      xlnT[:, dc, ch * 128:(ch + 1) * 128], pb[bk][:, 0:128],
                      after=[tt_], inc="vS")
            bankfree[bk] = ev
    tok_xlnT = p.tok("vS")

    tok_wqt_free = None
    tok_stg_free = [None, None]

    def proj_fm(w_d, qk_slot, scale):
        nonlocal tok_wqt_free
        lw = p.dma("sync", wqt, w_d[:, :, :, :], "qL", after=[tok_wqt_free])
        last_mm = None
        for ic in range(NCH):
            for tg in range(4):
                bk = (ic * 4 + tg) % 2
                mm = None
                for dc in range(NCH):
                    aft = [lw, tok_xlnT, bankfree[bk]] if dc == 0 else []
                    mm = p.op("tensor", "matmul", pb[bk][:, :],
                              wqt[:, ic, dc, :],
                              xlnT[:, dc, tg * GT:(tg + 1) * GT],
                              start=(dc == 0), stop=(dc == NCH - 1), after=aft,
                              inc="tS" if dc == NCH - 1 else None)
                spar = (ic * 4 + tg) % 2
                ev = p.op("scalar", "activation", stg[:, spar, :], pb[bk][:, :],
                          AF.Copy, scale=scale,
                          after=[mm, tok_stg_free[spar]], inc="sS")
                bankfree[bk] = ev
                st = p.dma("gpsimd",
                           qk_bin[ic, qk_slot, :, tg * GT:(tg + 1) * GT],
                           stg[:, spar, :], "qG", after=[ev])
                tok_stg_free[spar] = st
                last_mm = mm
        tok_wqt_free = last_mm

    proj_fm(wqT_d, 0, 0.125)
    proj_fm(wkT_d, 1, 1.0)
    tok_qk_stores = p.tok("qG")

    wvt = wqt
    wvt2 = wvt.rearrange("p a b c -> p a (b c)")
    lwv = p.dma("sync", wvt2, wvT_d.rearrange("(c p) i -> p c i", p=128), "qL",
                after=[tok_wqt_free])
    for ts in range(16):
        for ih in range(2):
            bk = (ts * 2 + ih) % 2
            mm = None
            for dc in range(NCH):
                aft = [lwv, tok_xlnT, bankfree[bk]] if dc == 0 else []
                mm = p.op("tensor", "matmul", pb[bk][:, :],
                          xlnT[:, dc, ts * 128:(ts + 1) * 128],
                          wvt2[:, dc, ih * 512:(ih + 1) * 512],
                          start=(dc == 0), stop=(dc == NCH - 1), after=aft,
                          inc="tS" if dc == NCH - 1 else None)
            spar = (ts * 2 + ih) % 2
            ev = p.op("scalar", "activation", stg[:, spar, :], pb[bk][:, :],
                      AF.Copy, after=[mm, tok_stg_free[spar]], inc="sS")
            bankfree[bk] = ev
            st = None
            for vs in range(4):
                st = p.dma("gpsimd",
                           v_bin[ih * 4 + vs, ts * 128:(ts + 1) * 128, :],
                           stg[:, spar, vs * 128:(vs + 1) * 128], "qG",
                           after=[ev])
            tok_stg_free[spar] = st

    if upto == "vbindump":
        out_d = g("out_d")
        for s_ in range(8):
            p.dma("gpsimd", out_d[:, s_ * 128:(s_ + 1) * 128],
                  v_bin[s_, :, :], "qG", after=[p.tok("qG")])
        p.op("gpsimd", "wait_ge", "qG", p.ctr["qG"])
        return

    a2a_qk = p.op("gpsimd", "collective", "AllToAll", ALU.bypass,
                  [qk_bin], [qk_out], after=[tok_qk_stores], inc="cc")
    a2a_v = p.op("gpsimd", "collective", "AllToAll", ALU.bypass,
                 [v_bin], [v_out], after=[a2a_qk, p.tok("qG")], inc="cc")

    if upto == "qkv":
        p.op("gpsimd", "wait_ge", "cc", p.ctr["cc"])
        return

    if upto == "voutdump":
        out_d = g("out_d")
        for s_ in range(8):
            p.dma("gpsimd", out_d[:, s_ * 128:(s_ + 1) * 128],
                  v_out[s_, :, :], "qG", after=[a2a_v])
        p.op("gpsimd", "wait_ge", "qG", p.ctr["qG"])
        return

    A.reset()
    bn = A.alloc(16)
    masks = A.alloc(4, 512)
    QT2 = A.alloc(S)
    KT2 = A.alloc(S)
    vbuf = A.alloc(64, 130)
    absb = A.alloc(2, 2, 512)
    osb = A.alloc(2, 512)
    otm = A.alloc(2, 4, 64)
    pdif = A.alloc(2, 16)
    pref = A.alloc(2, 16)
    rzc = A.alloc(4)

    tok_masks = p.dma("sync", masks.rearrange("p a b -> p (a b)"),
                      masks_d[:, :], "qL", after=[p.tok("tS")])
    mz = p.op("vector", "memset", osb.rearrange("p a b -> p (a b)"), 0.0, inc="vS")
    tok_osb_zero = mz

    NQB = S // GT
    tok_qt_free = None
    tok_vb_free = None
    tok_otm_free = [None, None]
    abfree = {}
    osb_free = {}
    tslot_free = {}

    for bsw in range(2):
        lq = p.dma("sync", QT2.rearrange("p (s t) -> p s t", s=4),
                   qk_out[4 * bsw:4 * bsw + 4, 0, :, :].rearrange("s p t -> p s t"),
                   "qL", after=[a2a_qk, tok_qt_free])
        lk = p.dma("sync", KT2.rearrange("p (s t) -> p s t", s=4),
                   qk_out[4 * bsw:4 * bsw + 4, 1, :, :].rearrange("s p t -> p s t"),
                   "qL", after=[])
        lvs = []
        for h in range(2):
            for src in range(4):
                lvs.append(p.dma(
                    "scalar", vbuf[:, src * 16:(src + 1) * 16, 65 * h:65 * h + 64],
                    v_out[4 * bsw + src, :, h * 64:(h + 1) * 64]
                    .rearrange("(k p) d -> p k d", p=128), "qA",
                    after=[a2a_v, tok_vb_free]))
        m1 = p.op("vector", "memset", vbuf[:, :, 64:65], 1.0,
                  after=[lvs[-1]], inc="vS")
        m2 = p.op("vector", "memset", vbuf[:, :, 129:130], 1.0, inc="vS")
        tok_vb = m2

        if upto == "qkdump" and bsw == 0:
            out_d = g("out_d")
            p.dma("gpsimd",
                  out_d[0:1024, :].rearrange("(a p) c -> p a c", p=128),
                  QT2.rearrange("p (a c) -> p a c", c=1024), "qG",
                  after=[lq, lk])
            p.dma("gpsimd",
                  out_d[1024:2048, :].rearrange("(a p) c -> p a c", p=128),
                  KT2.rearrange("p (a c) -> p a c", c=1024), "qG",
                  after=[lk])
            p.op("gpsimd", "wait_ge", "qG", p.ctr["qG"])
            return
        if upto == "vdump" and bsw == 0:
            out_d = g("out_d")
            vfl = vbuf.rearrange("p a b -> p (a b)")
            p.dma("gpsimd",
                  out_d[0:1024, :].rearrange("(a p) c -> p a c", p=128),
                  vfl[:, 0:8192].rearrange("p (a c) -> p a c", c=1024), "qG",
                  after=[tok_vb])
            p.dma("gpsimd", out_d[1024:1152, 0:128], vfl[:, 8192:8320],
                  "qG", after=[tok_vb])
            p.op("gpsimd", "wait_ge", "qG", p.ctr["qG"])
            return

        tok_sS_pre = p.tok("sS")
        for h in (() if "prefix" in K_SKIP() else range(2)):
            for qb in range(NQB):
                for j in range(4):
                    kt = qb * 4 + j
                    aft = [tok_vb, tok_vconsts] if j == 0 else []
                    if qb == 0 and j == 0:
                        aft.append(bankfree[5])
                    p.op("tensor", "matmul", pb[5][0:65, qb:qb + 1],
                         vbuf[:, kt, 65 * h:65 * h + 65], onescol[:, :],
                         start=(j == 0), stop=(j == 3), after=aft,
                         inc="tS" if j == 3 else None)
            ev = p.op("vector", "tensor_copy", pdif[0:65, h, :], pb[5][0:65, 0:16],
                      after=[p.tok("tS")], inc="vS")
            bankfree[5] = ev
            pz = p.op("vector", "memset", pref[:, h, :], 0.0,
                      after=[tok_sS_pre], inc="vS")
            prev = pz
            for qb in range(1, NQB):
                prev = p.op("vector", "tensor_tensor", pref[0:65, h, qb:qb + 1],
                            pref[0:65, h, qb - 1:qb], pdif[0:65, h, qb - 1:qb],
                            op=ALU.add, after=[ev, prev], inc="vS")
        tok_pref = p.tok("vS")

        if upto == "prefdump" and bsw == 0:
            out_d = g("out_d")
            p.dma("gpsimd", out_d[0:128, 0:32],
                  pref.rearrange("p a b -> p (a b)"), "qG", after=[tok_pref])
            p.dma("gpsimd", out_d[0:128, 32:64],
                  pdif.rearrange("p a b -> p (a b)"), "qG", after=[tok_pref])
            p.op("gpsimd", "wait_ge", "qG", p.ctr["qG"])
            return

        for qb in range(NQB):
            nkt = qb * 4 + 4
            for kt in range(nkt):
                sA = kt % 2
                sB = 2 + kt % 2
                mmS = []
                for h, bk in ((0, sA), (1, sB)):
                    mm = p.op("tensor", "matmul", pb[bk][:, :],
                              KT2[64 * h:64 * h + 64, kt * 128:(kt + 1) * 128],
                              QT2[64 * h:64 * h + 64, qb * GT:(qb + 1) * GT],
                              start=True, stop=True,
                              tile_position=(64 * h, 0) if h else None,
                              after=[lq, lk, bankfree[bk]], inc="tS")
                    mmS.append(mm)
                diag = kt >= qb * 4
                abtoks = []
                for h, bk in (() if "sq" in K_SKIP() else ((0, sA), (1, sB))):
                    par = kt % 2
                    ev = p.op("scalar", "activation", absb[:, h, par, :],
                              pb[bk][:, :], AF.Square, bias=1.0,
                              after=[mmS[h], abfree.get((h, par))], inc="sS")
                    bankfree[bk] = ev
                    if diag and "diag" not in K_SKIP():
                        ev = p.op("vector", "scalar_tensor_tensor",
                                  absb[:, h, par, :], absb[:, h, par, :], 1.0,
                                  masks[:, kt - qb * 4, :], op0=ALU.add,
                                  op1=ALU.mult, after=[ev, tok_masks], inc="vS")
                    abtoks.append(ev)
                for h, obk in (() if "oacc" in K_SKIP() else ((0, 4), (1, 6))):
                    par = kt % 2
                    aft = [abtoks[h], tok_vb]
                    if kt == 0:
                        aft.append(bankfree[obk])
                    mmO = p.op("tensor", "matmul", pb[obk][0:65, :],
                               vbuf[:, kt, 65 * h:65 * h + 65], absb[:, h, par, :],
                               start=(kt == 0), stop=(kt == nkt - 1), after=aft,
                               inc="tS" if kt == nkt - 1 else None)
                    abfree[(h, par)] = mmO
            tok_av = p.tok("tS")
            for h, obk in (() if ("dance" in K_SKIP() or "oacc" in K_SKIP())
                           else ((0, 4), (1, 6))):
                evO = p.op("vector", "tensor_scalar", osb[0:65, h, :],
                           pb[obk][0:65, :], pref[0:65, h, qb:qb + 1], None,
                           op0=ALU.add,
                           after=[tok_av, tok_pref, tok_osb_zero,
                                  osb_free.get(h)], inc="vS")
                bankfree[obk] = evO
                dv = None
                tt_ = None
                if "tr" in K_SKIP():
                    osb_free[h] = evO
                    continue
                for qs in range(4):
                    sl = (h * 4 + qs) % 4
                    slot = pb[7][:, 128 * sl:128 * sl + 128]
                    tt_ = p.op("tensor", "transpose", slot,
                               osb[:, h, qs * 128:(qs + 1) * 128], ident[:, :],
                               after=[evO, tslot_free.get(sl), p.tok("vS")],
                               inc="tS")
                    if "div" in K_SKIP():
                        tslot_free[sl] = tt_
                        dv = None
                        continue
                    r1 = p.op("vector", "tensor_scalar", rzc[:, qs:qs + 1],
                              slot[:, 64:65], ATT_EPS2, None, op0=ALU.add,
                              after=[tt_], inc="vS")
                    r2 = p.op("vector", "reciprocal", rzc[:, qs:qs + 1],
                              rzc[:, qs:qs + 1], after=[r1], inc="vS")
                    dv = p.op("vector", "tensor_scalar", otm[:, h, qs, :],
                              slot[:, 0:64], rzc[:, qs:qs + 1], None,
                              op0=ALU.mult, after=[r2, tok_otm_free[h]], inc="vS")
                    tslot_free[sl] = dv
                osb_free[h] = tt_
                if "div" in K_SKIP() or "store" in K_SKIP():
                    continue
                dest = 4 * bsw + qb // 4
                r0 = (qb % 4) * 512
                st = p.dma("gpsimd", o_bin[dest, r0:r0 + 512, h * 64:(h + 1) * 64]
                           .rearrange("(a p) d -> p a d", p=128),
                           otm[:, h, :, :], "qG", after=[dv])
                tok_otm_free[h] = st
        tok_qt_free = p.tok("tS")
        tok_vb_free = p.tok("tS")

    a2a_o = p.op("gpsimd", "collective", "AllToAll", ALU.bypass,
                 [o_bin], [o_out], after=[p.tok("qG")], inc="cc")

    if upto == "att":
        p.op("gpsimd", "wait_ge", "cc", p.ctr["cc"])
        return

    if upto == "ydump":
        out_d = g("out_d")
        for s_ in range(8):
            p.dma("gpsimd", out_d.rearrange("t (s d) -> t s d", s=8)[:, s_, :],
                  o_out[s_, :, :], "qG", after=[a2a_o])
        p.op("gpsimd", "wait_ge", "qG", p.ctr["qG"])
        return

    A.reset()
    bn = A.alloc(16)
    ych = A.alloc(2, D)
    ytT = A.alloc(NCH, GT)
    wot = A.alloc(NCH, D)
    lwo = p.dma("sync", wot, woT_d[:, :, :], "qL", after=[p.tok("tS")])
    tok_ych_free = [None, None]
    tok_ytT_free = None
    for tg in range(4):
        evs = None
        for c in range(4):
            ch = tg * 4 + c
            par = c % 2
            ly = p.dma("sync", ych[:, par, :].rearrange("p (s d) -> p s d", s=8),
                       o_out[:, ch * 128:(ch + 1) * 128, :]
                       .rearrange("s p d -> p s d"),
                       "qL", after=[a2a_o, tok_ych_free[par]])
            for jc in range(NCH):
                bk = 6 + jc % 2
                tt_ = p.op("tensor", "transpose", pb[bk][:, 0:128],
                           ych[:, par, jc * 128:(jc + 1) * 128], ident[:, :],
                           after=[ly, bankfree[bk], tok_ytT_free], inc="tS")
                ev = p.op("vector", "tensor_copy",
                          ytT[:, jc, c * 128:(c + 1) * 128], pb[bk][:, 0:128],
                          after=[tt_], inc="vS")
                bankfree[bk] = ev
            tok_ych_free[par] = tt_
            evs = ev
        tok_ytT = evs
        last_mm = None
        for tcl in range(4):
            ch = tg * 4 + tcl
            for dh in range(2):
                bk = (tcl * 2 + dh) % 4
                mm = None
                for jc in range(NCH):
                    aft = [lwo, tok_ytT, bankfree[bk]] if jc == 0 else []
                    mm = p.op("tensor", "matmul", pb[bk][:, :],
                              ytT[:, jc, tcl * 128:(tcl + 1) * 128],
                              wot[:, jc, dh * 512:(dh + 1) * 512],
                              start=(jc == 0), stop=(jc == NCH - 1), after=aft,
                              inc="tS" if jc == NCH - 1 else None)
                t2 = p.op("vector", "tensor_tensor",
                          slab[:, ch, dh * 512:(dh + 1) * 512],
                          slab[:, ch, dh * 512:(dh + 1) * 512],
                          pb[bk][:, :], op=ALU.add, after=[mm], inc="vS")
                bankfree[bk] = t2
                last_mm = mm
        tok_ytT_free = last_mm


def emit_phase_d(p, A, nc, env):
    g = lambda k: env[k]
    pb, slab, ident, bankfree = g("pb"), g("slab"), g("ident"), g("bankfree")
    epscol, onesrow = g("epscol"), g("onesrow")
    ccell = g("ccell")
    t128, srow, scol = g("t128"), g("srow"), g("scol")
    iar_t = g("iar_t")
    out_d = g("out_d")
    wupT_d, wgtT_d, wdnT_d = g("wupT_d"), g("wgtT_d"), g("wdnT_d")
    up_sp, gt_sp = g("up_sp"), g("gt_sp")
    upq_sp, g2_sp, o_sp = g("upq_sp"), g("g2_sp"), g("o_sp")
    ar_bin, ar_out = g("ar_bin"), g("ar_out")

    bn = A.alloc(16)
    xqT = A.alloc(NCH, TPC)
    xtmp = A.alloc(D)
    mx = A.alloc(4)
    sv = A.alloc(8)

    def emit_ln(src, dst, after=()):
        b1 = p.op("vector", "bn_stats", bn[:, 0:6], src[:, 0:512],
                  after=list(after), inc="vS")
        b2 = p.op("vector", "bn_stats", bn[:, 6:12], src[:, 512:1024], inc="vS")
        ag = p.op("vector", "bn_aggr", bn[:, 12:14], bn[:, 0:12],
                  after=[b1, b2], inc="vS")
        sq = p.op("scalar", "activation", bn[:, 14:15], bn[:, 13:14], AF.Sqrt,
                  bias=epscol, after=[ag], inc="sS")
        rs = p.op("vector", "reciprocal", bn[:, 15:16], bn[:, 14:15],
                  after=[sq], inc="vS")
        return p.op("vector", "tensor_scalar", dst, src, bn[:, 12:13],
                    bn[:, 15:16], op0=ALU.subtract, op1=ALU.mult,
                    after=[rs], inc="vS")

    mz = p.op("vector", "memset", mx[:, :], 0.0, inc="vS")
    for ch in range(16):
        ln = emit_ln(slab[:, ch, :], xtmp)
        rd = p.op("vector", "tensor_reduce", mx[:, 1:2], xtmp, axis=AX.X,
                  op=ALU.max, apply_absolute_value=True, after=[ln, mz], inc="vS")
        p.op("vector", "tensor_tensor", mx[:, 0:1], mx[:, 0:1], mx[:, 1:2],
             op=ALU.max, after=[rd], inc="vS")
        for dc in range(NCH):
            bk = 6 + dc % 2
            tt_ = p.op("tensor", "transpose", pb[bk][:, 0:128],
                       xtmp[:, dc * 128:(dc + 1) * 128], ident[:, :],
                       after=[ln, bankfree[bk]], inc="tS")
            ev = p.op("vector", "tensor_copy",
                      xqT[:, dc, ch * 128:(ch + 1) * 128], pb[bk][:, 0:128],
                      after=[tt_], inc="vS")
            bankfree[bk] = ev

    def global_max_to_scalars(cols, bin_i, nvals, after=()):
        ts_ = list(after)
        for i, cap in enumerate(cols):
            ts_.append(p.op("vector", "tensor_copy", t128[:, i:i + 1], cap,
                            inc="vS"))
        tt_ = p.op("tensor", "transpose", pb[5][:, 0:128], t128[:, :], ident[:, :],
                   after=ts_ + [bankfree[5]], inc="tS")
        rd = p.op("vector", "tensor_reduce", scol[0:4, 0:1],
                  pb[5][0:4, 0:128], axis=AX.X, op=ALU.max,
                  after=[tt_], inc="vS")
        bankfree[5] = rd
        std = p.dma("gpsimd", ar_bin[bin_i].rearrange("a b -> b a"),
                    scol[0:4, 0:1], "qG", after=[rd])
        cc_ = p.op("gpsimd", "collective", "AllReduce", ALU.max,
                   [ar_bin[bin_i]], [ar_out[bin_i]], after=[std], inc="cc")
        lda = p.dma("sync", srow[0:1, 8:12], ar_out[bin_i][0:1, 0:4], "qL",
                    after=[cc_])
        outs = []
        for i in range(nvals):
            s1 = p.op("vector", "tensor_scalar", srow[0:1, i:i + 1],
                      srow[0:1, 8 + i:9 + i], ccell[0:1, 4:5], 1e-12,
                      op0=ALU.mult, op1=ALU.max, after=[lda], inc="vS")
            s2 = p.op("vector", "reciprocal", srow[0:1, 4 + i:5 + i],
                      srow[0:1, i:i + 1], after=[s1], inc="vS")
            outs.append(s2)
        tt2 = p.op("tensor", "matmul", pb[5][:, 0:8], onesrow[0:1, :],
                   srow[0:1, 0:8], start=True, stop=True,
                   after=outs + [bankfree[5]], inc="tS")
        evb = p.op("vector", "tensor_copy", sv[:, 0:8], pb[5][:, 0:8],
                   after=[tt2], inc="vS")
        bankfree[5] = evb
        return evb

    tok_sx = global_max_to_scalars([mx[:, 0:1]], 0, 1, after=[p.tok("vS")])

    if g("upto") == "svdump":
        p.dma("gpsimd", out_d[0:128, 0:8], sv[:, 0:8], "qG", after=[tok_sx])
        p.dma("gpsimd", out_d[0:128, 8:12], mx[:, 0:4], "qG", after=[tok_sx])
        p.dma("gpsimd", out_d[0:1, 16:32], srow[0:1, :], "qG", after=[tok_sx])
        p.dma("gpsimd", out_d[0:128, 32:48], scol[:, :], "qG", after=[tok_sx])
        p.op("gpsimd", "wait_ge", "qG", p.ctr["qG"])
        return

    qtoks = []
    for dc in range(NCH):
        v1 = p.op("vector", "tensor_scalar", xqT[:, dc, :], xqT[:, dc, :],
                  sv[:, 4:5], None, op0=ALU.mult, after=[tok_sx], inc="vS")
        cst = p.op("scalar", "activation", iar_t[:, 0:TPC], xqT[:, dc, :],
                   AF.Copy, after=[v1, qtoks[-1] if qtoks else None], inc="sS")
        cst2 = p.op("scalar", "activation", xqT[:, dc, :], iar_t[:, 0:TPC],
                    AF.Copy, after=[cst], inc="sS")
        v2 = p.op("vector", "tensor_scalar", xqT[:, dc, :], xqT[:, dc, :],
                  sv[:, 0:1], None, op0=ALU.mult, after=[cst2], inc="vS")
        qtoks.append(v2)
    tok_xq = qtoks[-1]

    wtl = A.alloc(2, NCH, 128)
    ustg = A.alloc(2, GT)
    mz2 = p.op("vector", "memset", mx[:, 0:2], 0.0, after=[p.tok("vS")], inc="vS")
    tok_wtl_free = [None, None]
    tok_ustg_free = [None, None]
    for mi, (wT_d, sp_d) in enumerate(((wupT_d, up_sp), (wgtT_d, gt_sp))):
        for ic in range(NIC):
            wpar = ic % 2
            lw = p.dma("sync", wtl[:, wpar, :, :], wT_d[ic, :, :, :], "qL",
                       after=[tok_wtl_free[wpar], g("tok_deq")])
            for tg in range(4):
                bk = (ic * 4 + tg) % 4
                mm = None
                for dc in range(NCH):
                    aft = [lw, tok_xq, bankfree[bk]] if dc == 0 else []
                    mm = p.op("tensor", "matmul", pb[bk][:, :],
                              wtl[:, wpar, dc, :],
                              xqT[:, dc, tg * GT:(tg + 1) * GT],
                              start=(dc == 0), stop=(dc == NCH - 1), after=aft,
                              inc="tS" if dc == NCH - 1 else None)
                spar = (ic * 4 + tg) % 2
                ev = p.op("vector", "tensor_copy", ustg[:, spar, :], pb[bk][:, :],
                          after=[mm, tok_ustg_free[spar]], inc="vS")
                bankfree[bk] = ev
                rd = p.op("vector", "tensor_reduce", mx[:, 2:3], ustg[:, spar, :],
                          axis=AX.X, op=ALU.max, apply_absolute_value=True,
                          after=[ev, mz2], inc="vS")
                p.op("vector", "tensor_tensor", mx[:, mi:mi + 1], mx[:, mi:mi + 1],
                     mx[:, 2:3], op=ALU.max, after=[rd], inc="vS")
                st = p.dma("gpsimd", sp_d[ic, :, tg * GT:(tg + 1) * GT],
                           ustg[:, spar, :], "qG", after=[ev])
                tok_ustg_free[spar] = st
            tok_wtl_free[wpar] = mm
    tok_d2 = p.tok("qG")

    tok_sug = global_max_to_scalars([mx[:, 0:1], mx[:, 1:2]], 1, 2,
                                    after=[p.tok("vS")])

    if g("upto") == "svdump2":
        p.dma("gpsimd", out_d[0:128, 0:8], sv[:, 0:8], "qG", after=[tok_sug])
        p.dma("gpsimd", out_d[0:128, 8:12], mx[:, 0:4], "qG", after=[tok_sug])
        p.dma("gpsimd", out_d[0:1, 16:32], srow[0:1, :], "qG", after=[tok_sug])
        p.dma("gpsimd", out_d[0:128, 32:160],
              wupT_d[0, :, 0, :], "qG", after=[tok_sug])
        p.dma("gpsimd", out_d[0:128, 160:288],
              up_sp[0, :, 0:128], "qG", after=[tok_sug])
        p.op("gpsimd", "wait_ge", "qG", p.ctr["qG"])
        return

    A2 = A
    utile = A.alloc(2, GT)
    gtile = A.alloc(2, GT)
    w1 = A.alloc(GT)
    w2 = A.alloc(GT)
    mz3 = p.op("vector", "memset", mx[:, 2:3], 0.0, after=[tok_sug], inc="vS")
    tok_ut_free = [None, None]
    tok_gt_free = [None, None]
    prev_cast = None
    prev_sig = None
    tok_w_free = None
    for ti in range(NIC * 4):
        ic, tg = divmod(ti, 4)
        par = ti % 2
        lu = p.dma("sync", utile[:, par, :], up_sp[ic, :, tg * GT:(tg + 1) * GT],
                   "qL", after=[tok_ut_free[par], tok_d2])
        lg_ = p.dma("scalar", gtile[:, par, :], gt_sp[ic, :, tg * GT:(tg + 1) * GT],
                    "qA", after=[tok_gt_free[par], tok_d2])
        v1 = p.op("vector", "tensor_scalar", utile[:, par, :], utile[:, par, :],
                  sv[:, 4:5], None, op0=ALU.mult, after=[lu, tok_sug], inc="vS")
        c1 = p.op("scalar", "activation", iar_t[:, 0:GT], utile[:, par, :], AF.Copy,
                  after=[v1, prev_cast], inc="sS")
        c1b = p.op("scalar", "activation", utile[:, par, :], iar_t[:, 0:GT],
                   AF.Copy, after=[c1], inc="sS")
        v2 = p.op("vector", "tensor_scalar", utile[:, par, :], utile[:, par, :],
                  sv[:, 0:1], None, op0=ALU.mult, after=[c1b], inc="vS")
        s1 = p.dma("gpsimd", upq_sp[ic, :, tg * GT:(tg + 1) * GT], utile[:, par, :],
                   "qG", after=[v2])
        tok_ut_free[par] = s1
        v3 = p.op("vector", "tensor_scalar", gtile[:, par, :], gtile[:, par, :],
                  sv[:, 5:6], None, op0=ALU.mult, after=[lg_, tok_sug], inc="vS")
        c2 = p.op("scalar", "activation", iar_t[:, 512:512 + GT], gtile[:, par, :],
                  AF.Copy, after=[v3, c1], inc="sS")
        c2b = p.op("scalar", "activation", gtile[:, par, :],
                   iar_t[:, 512:512 + GT], AF.Copy, after=[c2], inc="sS")
        v4 = p.op("vector", "tensor_scalar", gtile[:, par, :],
                  gtile[:, par, :], sv[:, 1:2], None, op0=ALU.mult,
                  after=[c2b], inc="vS")
        sg = p.op("scalar", "activation", w1, gtile[:, par, :], AF.Sigmoid,
                  after=[v4, prev_sig, tok_w_free], inc="sS")
        v5 = p.op("vector", "tensor_scalar", w2, w1, 65536.0, None,
                  op0=ALU.mult, after=[sg, tok_w_free], inc="vS")
        c3 = p.op("scalar", "activation", iar_t[:, 1024:1024 + GT], w2, AF.Copy,
                  after=[v5], inc="sS")
        c3b = p.op("scalar", "activation", w1, iar_t[:, 1024:1024 + GT],
                   AF.Copy, after=[c3], inc="sS")
        v6 = p.op("vector", "tensor_scalar", w1, w1,
                  65535.0, 1.0 / 65536.0, op0=ALU.min, op1=ALU.mult,
                  after=[c3b], inc="vS")
        prev_sig = v6
        v7 = p.op("vector", "tensor_tensor", gtile[:, par, :], gtile[:, par, :],
                  w1, op=ALU.mult, after=[v6], inc="vS")
        tok_w_free = v7
        rd = p.op("vector", "tensor_reduce", mx[:, 3:4], gtile[:, par, :],
                  axis=AX.X, op=ALU.max, apply_absolute_value=True,
                  after=[v7, mz3], inc="vS")
        p.op("vector", "tensor_tensor", mx[:, 2:3], mx[:, 2:3], mx[:, 3:4],
             op=ALU.max, after=[rd], inc="vS")
        s2 = p.dma("gpsimd", g2_sp[ic, :, tg * GT:(tg + 1) * GT], gtile[:, par, :],
                   "qG", after=[v7])
        tok_gt_free[par] = s2
        prev_cast = c3b

    tok_d4 = p.tok("qG")
    tok_sg2 = global_max_to_scalars([mx[:, 2:3]], 2, 1, after=[p.tok("vS")])

    mz4 = p.op("vector", "memset", mx[:, 0:1], 0.0, after=[tok_sg2], inc="vS")
    prev_cast = None
    for ti in range(NIC * 4):
        ic, tg = divmod(ti, 4)
        par = ti % 2
        lgq = p.dma("sync", gtile[:, par, :], g2_sp[ic, :, tg * GT:(tg + 1) * GT],
                    "qL", after=[tok_gt_free[par], tok_d4])
        luq = p.dma("scalar", utile[:, par, :], upq_sp[ic, :, tg * GT:(tg + 1) * GT],
                    "qA", after=[tok_ut_free[par], tok_d4])
        v1 = p.op("vector", "tensor_scalar", gtile[:, par, :], gtile[:, par, :],
                  sv[:, 4:5], None, op0=ALU.mult, after=[lgq, tok_sg2], inc="vS")
        c1 = p.op("scalar", "activation", iar_t[:, 0:GT], gtile[:, par, :], AF.Copy,
                  after=[v1, prev_cast], inc="sS")
        c1b = p.op("scalar", "activation", gtile[:, par, :], iar_t[:, 0:GT],
                   AF.Copy, after=[c1], inc="sS")
        v2 = p.op("vector", "tensor_scalar", gtile[:, par, :], gtile[:, par, :],
                  sv[:, 0:1], None, op0=ALU.mult, after=[c1b], inc="vS")
        prev_cast = c1b
        v3 = p.op("vector", "tensor_tensor", gtile[:, par, :], gtile[:, par, :],
                  utile[:, par, :], op=ALU.mult, after=[v2, luq], inc="vS")
        tok_ut_free[par] = v3
        rd = p.op("vector", "tensor_reduce", mx[:, 1:2], gtile[:, par, :],
                  axis=AX.X, op=ALU.max, apply_absolute_value=True,
                  after=[v3, mz4], inc="vS")
        p.op("vector", "tensor_tensor", mx[:, 0:1], mx[:, 0:1], mx[:, 1:2],
             op=ALU.max, after=[rd], inc="vS")
        st = p.dma("gpsimd", o_sp[ic, :, tg * GT:(tg + 1) * GT], gtile[:, par, :],
                   "qG", after=[v3])
        tok_gt_free[par] = st

    tok_d6 = p.tok("qG")
    tok_so = global_max_to_scalars([mx[:, 0:1]], 3, 1, after=[p.tok("vS")])

    if g("upto") == "odump":
        for ic_ in range(16):
            p.dma("gpsimd", out_d[ic_ * 128:(ic_ + 1) * 128, :]
                  .rearrange("p (a d) -> p a d", d=128),
                  o_sp[ic_, :, 0:1024].rearrange("p (a d) -> p a d", d=128),
                  "qG", after=[tok_so])
        p.op("gpsimd", "wait_ge", "qG", p.ctr["qG"])
        return

    if g("upto") == "svdump3":
        p.dma("gpsimd", out_d[0:128, 0:8], sv[:, 0:8], "qG", after=[tok_so])
        p.dma("gpsimd", out_d[0:128, 8:72], upq_sp[0, :, 0:64], "qG",
              after=[tok_so])
        p.dma("gpsimd", out_d[0:128, 72:136], g2_sp[0, :, 0:64], "qG",
              after=[tok_so])
        p.dma("gpsimd", out_d[0:128, 136:200], o_sp[0, :, 0:64], "qG",
              after=[tok_so])
        p.op("gpsimd", "wait_ge", "qG", p.ctr["qG"])
        return

    wdr = A.alloc(2, D)
    oqt = A.alloc(2, GT)
    tok_wdr_free = [None, None]
    tok_oqt_free = [None, None]
    prev_cast = None
    tok_out = None
    for tg in range(4):
        mmfin = {}
        for ic in range(NIC):
            par = ic % 2
            lo = p.dma("sync", oqt[:, par, :], o_sp[ic, :, tg * GT:(tg + 1) * GT],
                       "qL", after=[tok_oqt_free[par], tok_d6])
            lw = p.dma("scalar", wdr[:, par, :], wdnT_d[ic, :, :], "qA",
                       after=[tok_wdr_free[par], tok_d6])
            v1 = p.op("vector", "tensor_scalar", oqt[:, par, :], oqt[:, par, :],
                      sv[:, 4:5], None, op0=ALU.mult, after=[lo, tok_so], inc="vS")
            c1 = p.op("scalar", "activation", iar_t[:, 0:GT], oqt[:, par, :],
                      AF.Copy, after=[v1, prev_cast], inc="sS")
            c1b = p.op("scalar", "activation", oqt[:, par, :], iar_t[:, 0:GT],
                       AF.Copy, after=[c1], inc="sS")
            v2 = p.op("vector", "tensor_scalar", oqt[:, par, :], oqt[:, par, :],
                      sv[:, 0:1], None, op0=ALU.mult, after=[c1b], inc="vS")
            prev_cast = c1b
            for ts in range(4):
                for dh in range(2):
                    bk = ts * 2 + dh
                    aft = [v2, lw]
                    if ic == 0:
                        aft.append(bankfree[bk])
                    mmfin[bk] = p.op(
                        "tensor", "matmul", pb[bk][:, :],
                        oqt[:, par, ts * 128:(ts + 1) * 128],
                        wdr[:, par, dh * 512:(dh + 1) * 512],
                        start=(ic == 0), stop=(ic == NIC - 1), after=aft,
                        inc="tS" if (ic == NIC - 1 or (ts == 3 and dh == 1))
                        else None)
            tok_oqt_free[par] = mmfin[7]
            tok_wdr_free[par] = mmfin[7]
        for ts in range(4):
            ch = tg * 4 + ts
            for dh in range(2):
                bk = ts * 2 + dh
                t2 = p.op("vector", "tensor_tensor",
                          slab[:, ch, dh * 512:(dh + 1) * 512],
                          slab[:, ch, dh * 512:(dh + 1) * 512],
                          pb[bk][:, :], op=ALU.add, after=[mmfin[bk]], inc="vS")
                bankfree[bk] = t2
            tok_out = p.dma("gpsimd", out_d[ch * 128:(ch + 1) * 128, :],
                            slab[:, ch, :], "qG", after=[p.tok("vS")])
    p.op("gpsimd", "wait_ge", "qG", p.ctr["qG"])


def _mk_masks():
    m = np.zeros((128, 4, 512), np.float32)
    tk = np.arange(128)[:, None]
    tq = np.arange(512)[None, :]
    for j in range(4):
        m[:, j, :] = (128 * j + tk <= tq).astype(np.float32)
    return np.ascontiguousarray(m.reshape(128, 4 * 512))


def _blk_gate(W):
    return np.ascontiguousarray(
        np.asarray(W, np.float32).reshape(8, 128, 8, 128).transpose(1, 2, 0, 3))


def _blk_T(W):
    return np.ascontiguousarray(
        np.asarray(W, np.float32).reshape(8, 128, 8, 128).transpose(3, 0, 2, 1))


def _prep_inputs(inputs):
    f = lambda k: np.asarray(inputs[k], np.float32)
    x = np.ascontiguousarray(f("L0_attn_x").reshape(B * S, D))
    gamma, beta = f("L0_attn_gamma"), f("L0_attn_beta")
    gb = np.ascontiguousarray(
        np.concatenate([np.tile(gamma, (128, 1)), np.tile(beta, (128, 1))], axis=1))
    bcols = np.ascontiguousarray(
        np.stack([f(f"L0_attn_b{k}").reshape(8, 128).T for k in "ifos"],
                 axis=1).reshape(128, 32))
    common = dict(
        Wi_b=_blk_gate(f("L0_attn_Wi")), Wf_b=_blk_gate(f("L0_attn_Wf")),
        Wog_b=_blk_gate(f("L0_attn_Wo")), Ws_b=_blk_gate(f("L0_attn_Ws")),
        bcols=bcols, gb_bc=gb,
        U1=np.ascontiguousarray(f("L0_ffn_U1")),
        V1_b=np.ascontiguousarray(
            f("L0_ffn_V1").reshape(2, 128, 32, 128).transpose(1, 2, 0, 3)),
        b1cols=np.ascontiguousarray(f("L0_ffn_b1").reshape(32, 128).T),
        U2_b=np.ascontiguousarray(
            f("L0_ffn_U2").reshape(32, 128, 2, 128).transpose(1, 0, 2, 3)),
        V2=np.ascontiguousarray(f("L0_ffn_V2")),
        b2_bc=np.ascontiguousarray(np.tile(f("L0_ffn_b2"), (128, 1))),
        WqT_b=_blk_T(f("L1_attn_Wq")), WkT_b=_blk_T(f("L1_attn_Wk")),
        WvT=np.ascontiguousarray(f("L1_attn_Wv").T),
        WoT_b=np.ascontiguousarray(
            f("L1_attn_Wo").T.reshape(8, 128, 1024).transpose(1, 0, 2)),
        masks=_mk_masks(),
        ident_in=np.eye(128, dtype=np.float32),
        wup=np.ascontiguousarray(f("L1_ffn_w_up")),
        wgt=np.ascontiguousarray(f("L1_ffn_w_gate")),
        wdn=np.ascontiguousarray(f("L1_ffn_w_down")),
    )
    in_maps = []
    for c in range(NCORE):
        oh = np.zeros((128, NCORE), np.float32)
        oh[:, c] = 1.0
        m = dict(common)
        m["x0"] = np.ascontiguousarray(x[c * TPC:(c + 1) * TPC])
        m["onehot"] = oh
        in_maps.append(m)
    return in_maps


_UPTO = "full"
_ATT_SKIP = set()


def kernel(**inputs):
    in_maps = _prep_inputs(inputs)
    nc = build(upto=_UPTO)
    res = run_bass_kernel_spmd(nc, in_maps, core_ids=list(range(NCORE)))
    out = np.concatenate([res.results[c]["out"] for c in range(NCORE)], axis=0)
    return out.reshape(B, S, D)
